# version 11
# baseline (speedup 1.0000x reference)
import contextlib
import numpy as np
import ml_dtypes
import concourse.bass as bass
import concourse.mybir as mybir
from concourse.bass import IndirectOffsetOnAxis
from concourse.bass_utils import run_bass_kernel_spmd

F32 = mybir.dt.float32
BF16 = mybir.dt.bfloat16
I32 = mybir.dt.int32
AF = mybir.ActivationFunctionType
ALU = mybir.AluOpType
AX = mybir.AxisListType.X

STAGE = 4

D = 1024
T = 2048
S = 4096
PAIR_COLS = 98

A_XFM = 0
A_XB = 32768
A_K2 = 32768
A_Q2 = 36864
A_V1 = 38912
A_Z2 = 43072
A_EB = 45120
A_WQ = 46144
A_WO = 52288
A_RC = 54336
A_MS = 55360
A_ST = 55360
ARENA_N = 64032

C_DUMP = 0
C_ZA = 1024
C_ZB = 1536
C_SM = 2048


class Sem:
    def __init__(self, h):
        self.h = h
        self.n = 0

    def bump(self, k):
        self.n += k
        return self.n


class KB:
    ENGS = ("sync", "scalar", "vector", "tensor", "gpsimd")

    def __init__(self):
        self.streams = {e: [] for e in self.ENGS}
        self.waited = {e: {} for e in self.ENGS}

    def drain(self, eng):
        self.streams[eng].append(lambda E: E.drain())

    def wait(self, eng, sem, val):
        if val is None:
            return
        key = id(sem)
        if self.waited[eng].get(key, -1) >= val:
            return
        self.waited[eng][key] = val
        self.streams[eng].append(lambda E, s=sem.h, v=val: E.wait_ge(s, v))

    def op(self, eng, fn, inc=None):
        if inc is None:
            self.streams[eng].append(fn)
            return None
        sem, k = inc
        v = sem.bump(k)
        self.streams[eng].append(
            lambda E, f=fn, s=sem.h, kk=k: f(E).then_inc(s, kk))
        return v

    def finalize(self, block):
        def mk(fns):
            def run(E):
                for f in fns:
                    f(E)
            return run
        block.sync(mk(self.streams["sync"]))
        block.scalar(mk(self.streams["scalar"]))
        block.vector(mk(self.streams["vector"]))
        block.tensor(mk(self.streams["tensor"]))
        block.gpsimd(mk(self.streams["gpsimd"]))


class Banks:

    def __init__(self, kb, bank_aps):
        self.kb = kb
        self.banks = bank_aps
        self.free_at = [None] * len(bank_aps)
        self.i = 0

    def acquire(self):
        j = self.i % len(self.banks)
        self.i += 1
        if self.free_at[j] is not None:
            self.kb.wait("tensor", *self.free_at[j])
        return j, self.banks[j]

    def release(self, j, sem, val):
        self.free_at[j] = (sem, val)


def build_program(stage=STAGE):
    nc = bass.Bass("TRN2", target_bir_lowering=False, debug=False)
    dt = nc.dram_tensor

    x_own = dt("x_own", [T, D], F32, kind="ExternalInput").ap()
    x_batch = dt("x_batch", [S, D], F32, kind="ExternalInput").ap()
    pairs_q = dt("pairs_q", [128, PAIR_COLS], I32, kind="ExternalInput").ap()
    pairs_k = dt("pairs_k", [128, PAIR_COLS], I32, kind="ExternalInput").ap()
    mask_thr = dt("mask_thr", [128, 1], F32, kind="ExternalInput").ap()
    ident_c = dt("ident_c", [128, 128], BF16, kind="ExternalInput").ap()
    ones_row_c = dt("ones_row_c", [1, 128], BF16, kind="ExternalInput").ap()
    fconst_c = dt("fconst_c", [128, 8], F32, kind="ExternalInput").ap()
    wqT = dt("wqT", [D, D], F32, kind="ExternalInput").ap()
    wkT = dt("wkT", [D, D], F32, kind="ExternalInput").ap()
    wvT = dt("wvT", [D, D], F32, kind="ExternalInput").ap()
    woT = dt("woT", [D, D], F32, kind="ExternalInput").ap()
    ident32_c = dt("ident32_c", [128, 128], F32, kind="ExternalInput").ap()
    ones32_c = dt("ones32_c", [1, 128], F32, kind="ExternalInput").ap()
    e128_c = dt("e128_c", [128, 128], F32, kind="ExternalInput").ap()
    wr_c = dt("wr_c", [D, 8], F32, kind="ExternalInput").ap()
    w1_c = dt("w1_c", [8, D, 4096], F32, kind="ExternalInput").ap()
    b1_c = dt("b1_c", [8, 128, 32], F32, kind="ExternalInput").ap()
    w2_c = dt("w2_c", [8, 4096, D], F32, kind="ExternalInput").ap()
    b2_c = dt("b2_c", [8, D], F32, kind="ExternalInput").ap()
    l1wq = dt("l1wq", [D, D], F32, kind="ExternalInput").ap()
    l1wk = dt("l1wk", [D, D], F32, kind="ExternalInput").ap()
    l1wv = dt("l1wv", [D, D], F32, kind="ExternalInput").ap()
    l1wo = dt("l1wo", [D, D], F32, kind="ExternalInput").ap()
    l1bq = dt("l1bq", [128, 8], F32, kind="ExternalInput").ap()
    l1bk = dt("l1bk", [128, 8], F32, kind="ExternalInput").ap()
    l1bv_bc = dt("l1bv_bc", [128, D], F32, kind="ExternalInput").ap()
    l1bo_bc = dt("l1bo_bc", [128, D], F32, kind="ExternalInput").ap()
    l1proj = dt("l1proj", [128, 256], F32, kind="ExternalInput").ap()
    wr2_c = dt("wr2_c", [D, 8], F32, kind="ExternalInput").ap()
    aw1T = dt("aw1T", [D, 50], F32, kind="ExternalInput").ap()
    aw2A = dt("aw2A", [128, 6], F32, kind="ExternalInput").ap()
    ab1p = dt("ab1p", [128, 1], F32, kind="ExternalInput").ap()
    jrow_c = dt("jrow_c", [128, 16], F32, kind="ExternalInput").ap()
    gwT = dt("gwT", [8, D, 2688], F32, kind="ExternalInput").ap()
    uwT = dt("uwT", [8, D, 2688], F32, kind="ExternalInput").ap()
    dwT = dt("dwT", [8, 2688, D], F32, kind="ExternalInput").ap()
    out = dt("out", [T, D], F32, kind="ExternalOutput").ap()
    DBG = False
    if DBG:
        dbg_xfm = dt("dbg_xfm", [128, 32768], BF16, kind="ExternalOutput").ap()
        dbg_q2 = dt("dbg_q2", [128, 2048], BF16, kind="ExternalOutput").ap()
        dbg_k2 = dt("dbg_k2", [128, 4096], BF16, kind="ExternalOutput").ap()
        dbg_v1 = dt("dbg_v1", [128, 4160], BF16, kind="ExternalOutput").ap()
        dbg_z2 = dt("dbg_z2", [128, 2048], BF16, kind="ExternalOutput").ap()
        dbg_xtm = dt("dbg_xtm", [128, 16384], F32, kind="ExternalOutput").ap()
        dbg_scr = dt("dbg_scr", [128, 1024], F32, kind="ExternalOutput").ap()
        dbg_xtm2 = dt("dbg_xtm2", [128, 16384], F32, kind="ExternalOutput").ap()
        dbg_kv = dt("dbg_kv", [128, 2080], BF16, kind="ExternalOutput").ap()
        dbg_kp = dt("dbg_kp", [128, 8192], BF16, kind="ExternalOutput").ap()
        dbg_kvp = dt("dbg_kvp", [4096, 65], F32, kind="ExternalOutput").ap()
        dbg_v5 = dt("dbg_v5", [128, 16640], BF16, kind="ExternalOutput").ap()
        dbg_scr6 = dt("dbg_scr6", [128, 1024], F32, kind="ExternalOutput").ap()
        dbg_z1 = dt("dbg_z1", [128, 16384], BF16, kind="ExternalOutput").ap()
        dbg_xtm3 = dt("dbg_xtm3", [128, 16384], F32, kind="ExternalOutput").ap()

    kv_ccin = dt("kv_ccin", [4096, 65], F32, kind="Internal").ap()
    kv_ccout = dt("kv_ccout", [4096, 65], F32, kind="Internal").ap()
    sc_ccin = dt("sc_ccin", [8, 2048], F32, kind="Internal").ap()
    sc_ccout = dt("sc_ccout", [64, 2048], F32, kind="Internal").ap()
    mci = dt("mask_cc_in", [2048 * 2048, 1], BF16, kind="Internal").ap()
    mco = dt("mask_cc_out", [2048 * 2048, 1], BF16, kind="Internal").ap()

    with contextlib.ExitStack() as es:
        ent = es.enter_context
        x_tm = ent(nc.sbuf_tensor("x_tm", [128, 16 * D], F32))
        scr = ent(nc.sbuf_tensor("scr", [128, 3328], F32))
        arena = ent(nc.sbuf_tensor("arena", [128, ARENA_N], BF16))
        ident = ent(nc.sbuf_tensor("ident", [128, 128], BF16))
        ones_row = ent(nc.sbuf_tensor("ones_row", [1, 128], BF16))
        pq_sb = ent(nc.sbuf_tensor("pq_sb", [128, PAIR_COLS], I32))
        pk_sb = ent(nc.sbuf_tensor("pk_sb", [128, PAIR_COLS], I32))
        pidx = ent(nc.sbuf_tensor("pidx", [128, PAIR_COLS], I32))
        ones_sc = ent(nc.sbuf_tensor("ones_sc", [128, 1], BF16))
        mthr_sb = ent(nc.sbuf_tensor("mthr_sb", [128, 1], F32))
        fconst = ent(nc.sbuf_tensor("fconst", [128, 8], F32))
        ident32 = ent(nc.sbuf_tensor("ident32", [128, 128], F32))
        ones32 = ent(nc.sbuf_tensor("ones32", [1, 128], F32))
        e128 = ent(nc.sbuf_tensor("e128", [128, 128], F32))


        wk0 = ent(nc.psum_tensor("wk0", [128, 512], F32))
        wk1 = ent(nc.psum_tensor("wk1", [128, 512], F32))
        wk23 = ent(nc.psum_tensor("wk23", [128, 1024], F32))
        zz = ent(nc.psum_tensor("zz", [128, 1024], F32))
        tpA = ent(nc.psum_tensor("tpA", [128, 1024], BF16))
        tpB = ent(nc.psum_tensor("tpB", [128, 1024], BF16))

        s_ld = Sem(ent(nc.semaphore("s_ld")))
        g_ld = Sem(ent(nc.semaphore("g_ld")))
        t_mm = Sem(ent(nc.semaphore("t_mm")))
        v_ev = Sem(ent(nc.semaphore("v_ev")))
        a_ev = Sem(ent(nc.semaphore("a_ev")))
        g_ms = Sem(ent(nc.semaphore("g_ms")))
        g_xb = Sem(ent(nc.semaphore("g_xb")))
        st = Sem(ent(nc.semaphore("st")))

        block = ent(nc.Block())
        kb = KB()
        WB = Banks(kb, [wk0[:], wk1[:], wk23[:, 0:512], wk23[:, 512:1024]])

        xfm = arena[:, A_XFM:A_XFM + 32768].rearrange("p (dc f) -> p dc f", dc=8)
        v1v = arena[:, A_V1:A_V1 + 4160].rearrange("p (tc s) -> p tc s", tc=32)

        x_own_ap = x_own.rearrange("(c p) d -> p c d", p=128)
        x_tm_ap = x_tm[:].rearrange("p (c d) -> p c d", c=16)
        kb.op("sync", lambda E: E.dma_start(out=x_tm_ap, in_=x_own_ap), inc=(s_ld, 16))
        kb.op("sync", lambda E: E.dma_start(out=ident[:], in_=ident_c), inc=(s_ld, 16))
        kb.op("sync", lambda E: E.dma_start(out=ones_row[:], in_=ones_row_c), inc=(s_ld, 16))
        kb.op("sync", lambda E: E.dma_start(out=pq_sb[:], in_=pairs_q), inc=(s_ld, 16))
        kb.op("sync", lambda E: E.dma_start(out=pk_sb[:], in_=pairs_k), inc=(s_ld, 16))
        kb.op("sync", lambda E: E.dma_start(out=mthr_sb[:], in_=mask_thr), inc=(s_ld, 16))
        kb.op("sync", lambda E: E.dma_start(out=fconst[:], in_=fconst_c), inc=(s_ld, 16))
        ld_base = s_ld.n

        xb_ap = x_batch.rearrange("(c p) d -> p c d", p=128)

        wslot = lambda w, sl: A_WQ + (w * 2 + sl) * 1024
        woslot = lambda sl: A_WO + sl * 1024
        wT = [wqT, wkT, wvT]
        w_ld_done = []
        w_free_at = [None, None]

        def emit_wload(hg):
            sl = hg % 2
            if w_free_at[sl] is not None:
                kb.wait("gpsimd", *w_free_at[sl])
            if w_ld_done:
                kb.wait("gpsimd", g_ld, w_ld_done[-1])
            for w in range(3):
                src3 = wT[w][:, 128 * hg:128 * (hg + 1)].rearrange(
                    "(dc p) c -> p dc c", p=128)
                dst3 = arena[:, wslot(w, sl):wslot(w, sl) + 1024].rearrange(
                    "p (dc c) -> p dc c", dc=8)
                kb.op("gpsimd", lambda E, a=dst3, b=src3: E.dma_start(out=a, in_=b),
                      inc=(g_ld, 16))
            dst = arena[:, woslot(sl):woslot(sl) + 1024]
            src = woT[128 * hg:128 * (hg + 1), :]
            kb.op("gpsimd", lambda E, a=dst, b=src: E.dma_start(out=a, in_=b),
                  inc=(g_ld, 16))
            w_ld_done.append(g_ld.n)

        emit_wload(0)
        emit_wload(1)

        kb.op("vector", lambda E: E.memset(arena[:, A_MS:A_MS + 2048], 0.0))
        kb.op("vector", lambda E: E.memset(ones_sc[:], 1.0))
        kb.wait("vector", s_ld, ld_base)
        v_idx = kb.op("vector", lambda E: E.scalar_tensor_tensor(
            out=pidx[:], in0=pk_sb[:], scalar=2048.0, in1=pq_sb[:],
            op0=ALU.mult, op1=ALU.add), inc=(v_ev, 1))
        mci3 = mci.rearrange("(r p f) one -> p r (f one)", p=128, f=2048)
        kb.wait("sync", v_ev, v_idx)
        zs_done = None
        for r in range(16):
            zs_done = kb.op(
                "sync", lambda E, r=r: E.dma_start(
                    out=mci3[:, r, :], in_=arena[:, A_MS:A_MS + 2048]),
                inc=(s_ld, 16))
        kb.wait("gpsimd", v_ev, v_idx)
        kb.wait("gpsimd", s_ld, zs_done)
        sc_done = None
        for j in range(PAIR_COLS):
            sc_done = kb.op(
                "gpsimd", lambda E, j=j: E.indirect_dma_start(
                    out=mci,
                    out_offset=IndirectOffsetOnAxis(ap=pidx[:, j:j + 1], axis=0),
                    in_=ones_sc[:], in_offset=None),
                inc=(g_ms, 16))
        kb.wait("gpsimd", g_ms, sc_done)
        ar_done = kb.op("gpsimd", lambda E: E.collective_compute(
            "AllReduce", ALU.add, ins=[mci], outs=[mco],
            replica_groups=[list(range(8))]), inc=(g_ms, 1))

        kb.wait("vector", s_ld, ld_base)
        v_x2 = None
        for tc in range(16):
            v_x2 = kb.op("vector", lambda E, tc=tc: E.tensor_scalar_mul(
                out=x_tm[:, D * tc:D * (tc + 1)],
                in0=x_tm[:, D * tc:D * (tc + 1)], scalar1=2.0),
                inc=(v_ev, 1) if tc == 15 else None)

        kb.wait("tensor", s_ld, ld_base)
        tp_ev = []
        xbch_done = []
        xbld = []

        def emit_xbload(j):
            slot = A_ST + 2048 * (j % 2)
            if j >= 2:
                kb.wait("gpsimd", t_mm, xbch_done[j - 2])
            kb.wait("gpsimd", g_xb, g_xb.n)
            v = kb.op("gpsimd", lambda E, j=j, slot=slot: E.dma_start(
                out=arena[:, slot:slot + 2048].rearrange(
                    "p (c d) -> p c d", c=2),
                in_=xb_ap[:, 2 * j:2 * (j + 1), :]), inc=(g_xb, 16))
            xbld.append(v)

        emit_xbload(0)
        emit_xbload(1)
        for j in range(16):
            slot = A_ST + 2048 * (j % 2)
            kb.wait("tensor", g_xb, xbld[j])
            for tt in range(2):
                tc = 2 * j + tt
                pt = [tpA, tpB][tc % 2]
                if tc >= 2:
                    kb.wait("tensor", v_ev, tp_ev[tc - 2])
                tmm = None
                for dc in range(8):
                    sap = arena[:, slot + 1024 * tt + 128 * dc:
                                slot + 1024 * tt + 128 * (dc + 1)]
                    tmm = kb.op("tensor",
                                lambda E, pt=pt, dc=dc, sap=sap: E.transpose(
                                    pt[:, 128 * dc:128 * (dc + 1)], sap, ident[:]),
                                inc=(t_mm, 1) if dc == 7 else None)
                kb.wait("vector", t_mm, tmm)
                ptv = pt[:].rearrange("p (dc f) -> p dc f", dc=8)
                ev = kb.op("vector", lambda E, tc=tc, ptv=ptv: E.tensor_copy(
                    out=xfm[:, :, 128 * tc:128 * (tc + 1)], in_=ptv),
                    inc=(v_ev, 1))
                tp_ev.append(ev)
            xbch_done.append(tmm)
            if j + 2 < 16:
                emit_xbload(j + 2)
        v_xfm = tp_ev[-1]

        v_ones = kb.op("vector", lambda E: E.memset(
            arena[:, A_V1:A_V1 + 4160], 1.0), inc=(v_ev, 1))

        mco2d = mco.rearrange("(r c) one -> r (c one)", c=2048)


        ms_state = {"prev_end": None, "prev_cons": None}

        def load_mask_batch(tqc):
            kb.wait("sync", g_ms, ar_done)
            kb.wait("sync", v_ev, v_xfm)
            if ms_state["prev_end"] is not None:
                kb.wait("sync", s_ld, ms_state["prev_end"])
            if ms_state["prev_cons"] is not None:
                kb.wait("sync", v_ev, ms_state["prev_cons"])
            v = None
            for kt in range(16):
                src = mco2d[128 * kt:128 * (kt + 1), 512 * tqc:512 * (tqc + 1)]
                dst = arena[:, A_MS + 512 * kt:A_MS + 512 * (kt + 1)]
                v = kb.op("sync", lambda E, a=dst, b=src: E.dma_start(
                    out=a, in_=b), inc=(s_ld, 16))
            ms_state["prev_end"] = v
            return v

        KTS = list(range(16, 32)) + list(range(16))

        z_guard = [None]

        def attn_hg(hg):
            sl = hg % 2
            kb.wait("tensor", g_ld, w_ld_done[hg])
            kb.wait("tensor", v_ev, v_xfm)
            kb.wait("tensor", v_ev, v_ones)

            def proj_fm(w, dst_base, col0, nch):
                last = None
                for ch in range(nch):
                    j, wb = WB.acquire()
                    for dc in range(8):
                        lhs = arena[:, wslot(w, sl) + 128 * dc:
                                    wslot(w, sl) + 128 * (dc + 1)]
                        rhs = xfm[:, dc, col0 + 512 * ch:col0 + 512 * (ch + 1)]
                        kb.op("tensor",
                              lambda E, wb=wb, lhs=lhs, rhs=rhs, dc=dc: E.matmul(
                                  wb[:, 0:512], lhs, rhs,
                                  start=(dc == 0), stop=(dc == 7)),
                              inc=(t_mm, 1) if dc == 7 else None)
                    tv = t_mm.n
                    kb.wait("vector", t_mm, tv)
                    dst = arena[:, dst_base + 512 * ch:dst_base + 512 * (ch + 1)]
                    last = kb.op("vector", lambda E, wb=wb, dst=dst: E.tensor_copy(
                        out=dst, in_=wb[:, 0:512]), inc=(v_ev, 1))
                    WB.release(j, v_ev, last)
                return last

            proj_fm(0, A_Q2, 0, 4)
            proj_fm(1, A_K2, 0, 8)
            v_done = None
            for ch in range(32):
                j, wb = WB.acquire()
                for dc in range(8):
                    lhs = xfm[:, dc, 128 * ch:128 * (ch + 1)]
                    rhs = arena[:, wslot(2, sl) + 128 * dc:
                                wslot(2, sl) + 128 * (dc + 1)]
                    kb.op("tensor",
                          lambda E, wb=wb, lhs=lhs, rhs=rhs, dc=dc: E.matmul(
                              wb[:, 0:128], lhs, rhs,
                              start=(dc == 0), stop=(dc == 7)),
                          inc=(t_mm, 1) if dc == 7 else None)
                tv = t_mm.n
                kb.wait("vector", t_mm, tv)
                kb.op("vector", lambda E, wb=wb, ch=ch: E.tensor_copy(
                    out=v1v[:, ch, 0:64],
                    in_=wb[:, 0:64]))
                v_done = kb.op("vector", lambda E, wb=wb, ch=ch: E.tensor_copy(
                    out=v1v[:, ch, 65:129],
                    in_=wb[:, 64:128]), inc=(v_ev, 1))
                WB.release(j, v_ev, v_done)
            kb.wait("tensor", v_ev, v_done)

            for tqc in range(4):
                ms_batch = load_mask_batch(tqc)
                if z_guard[0] is not None:
                    kb.wait("tensor", v_ev, z_guard[0])
                first = True
                for kt in KTS:
                    jA, wbA = WB.acquire()
                    jB, wbB = WB.acquire()
                    lhsA = arena[0:64, A_K2 + 128 * kt:A_K2 + 128 * (kt + 1)]
                    lhsB = arena[64:128, A_K2 + 128 * kt:A_K2 + 128 * (kt + 1)]
                    rhsA = arena[0:64, A_Q2 + 512 * tqc:A_Q2 + 512 * (tqc + 1)]
                    rhsB = arena[64:128, A_Q2 + 512 * tqc:A_Q2 + 512 * (tqc + 1)]
                    kb.op("tensor", lambda E, wbA=wbA, lhsA=lhsA, rhsA=rhsA:
                          E.matmul(wbA[:, 0:512], lhsA, rhsA, start=True, stop=True))
                    smm = kb.op("tensor", lambda E, wbB=wbB, lhsB=lhsB, rhsB=rhsB:
                                E.matmul(wbB[:, 0:512], lhsB, rhsB,
                                         start=True, stop=True), inc=(t_mm, 1))
                    kb.wait("scalar", t_mm, smm)
                    eA = arena[:, A_EB:A_EB + 512]
                    eB = arena[:, A_EB + 512:A_EB + 1024]
                    kb.op("scalar", lambda E, wbA=wbA, eA=eA: E.activation(
                        out=eA, in_=wbA[:, 0:512], func=AF.Exp, scale=0.125))
                    ex = kb.op("scalar", lambda E, wbB=wbB, eB=eB: E.activation(
                        out=eB, in_=wbB[:, 0:512], func=AF.Exp, scale=0.125),
                        inc=(a_ev, 1))
                    WB.release(jA, a_ev, ex)
                    WB.release(jB, a_ev, ex)
                    if kt < 16:
                        kb.wait("vector", a_ev, ex)
                        kb.wait("vector", s_ld, ms_batch)
                        mt = arena[:, A_MS + 512 * kt:A_MS + 512 * (kt + 1)]
                        kb.op("vector", lambda E, mt=mt, eA=eA: E.scalar_tensor_tensor(
                            out=eA, in0=mt, scalar=mthr_sb[:], in1=eA,
                            op0=ALU.is_le, op1=ALU.mult))
                        emask = kb.op("vector",
                                      lambda E, mt=mt, eB=eB: E.scalar_tensor_tensor(
                                          out=eB, in0=mt, scalar=mthr_sb[:], in1=eB,
                                          op0=ALU.is_le, op1=ALU.mult),
                                      inc=(v_ev, 1))
                        ms_state["prev_cons"] = emask
                        kb.wait("tensor", v_ev, emask)
                    else:
                        kb.wait("tensor", a_ev, ex)
                    lhsvA = v1v[:, kt, 0:65]
                    lhsvB = v1v[:, kt, 65:130]
                    kb.op("tensor", lambda E, lhsvA=lhsvA, eA=eA, first=first,
                          last=(kt == KTS[-1]):
                          E.matmul(zz[0:65, 0:512], lhsvA, eA,
                                   start=first, stop=last))
                    zmm = kb.op("tensor", lambda E, lhsvB=lhsvB, eB=eB, first=first,
                                last=(kt == KTS[-1]):
                                E.matmul(zz[0:65, 512:1024], lhsvB, eB,
                                         start=first, stop=last),
                                inc=(t_mm, 1) if kt == KTS[-1] else None)
                    first = False
                zdone = t_mm.n
                kb.wait("vector", t_mm, zdone)
                czA = 1024 - 1024 * (tqc % 2)
                czB = czA + 512
                kb.op("vector", lambda E, czA=czA: E.tensor_copy(
                    out=scr[0:65, czA:czA + 512], in_=zz[0:65, 0:512]))
                zc = kb.op("vector", lambda E, czB=czB: E.tensor_copy(
                    out=scr[0:65, czB:czB + 512], in_=zz[0:65, 512:1024]),
                    inc=(v_ev, 1))
                z_guard[0] = zc

                def emit_zdiv(tqd):
                    cA = 1024 - 1024 * (tqd % 2)
                    cB = cA + 512
                    kb.op("vector", lambda E, cA=cA: E.reciprocal(
                        out=scr[64:65, cA:cA + 512], in_=scr[64:65, cA:cA + 512]))
                    kb.op("vector", lambda E, cB=cB: E.reciprocal(
                        out=scr[64:65, cB:cB + 512], in_=scr[64:65, cB:cB + 512]))
                    kb.op("vector", lambda E, cA=cA: E.tensor_copy(
                        out=arena[0:1, A_RC:A_RC + 512],
                        in_=scr[64:65, cA:cA + 512]))
                    rc = kb.op("vector", lambda E, cB=cB: E.tensor_copy(
                        out=arena[0:1, A_RC + 512:A_RC + 1024],
                        in_=scr[64:65, cB:cB + 512]), inc=(v_ev, 1))
                    kb.wait("tensor", v_ev, rc)
                    jbA, bcA = WB.acquire()
                    jbB, bcB = WB.acquire()
                    kb.op("tensor", lambda E, bcA=bcA: E.matmul(
                        bcA[0:64, 0:512], ones_row[0:1, 0:64],
                        arena[0:1, A_RC:A_RC + 512], start=True, stop=True))
                    bc = kb.op("tensor", lambda E, bcB=bcB: E.matmul(
                        bcB[0:64, 0:512], ones_row[0:1, 0:64],
                        arena[0:1, A_RC + 512:A_RC + 1024],
                        start=True, stop=True), inc=(t_mm, 1))
                    kb.wait("vector", t_mm, bc)
                    zA2 = arena[0:64, A_Z2 + 512 * tqd:A_Z2 + 512 * (tqd + 1)]
                    zB2 = arena[64:128, A_Z2 + 512 * tqd:A_Z2 + 512 * (tqd + 1)]
                    kb.op("vector", lambda E, zA2=zA2, bcA=bcA, cA=cA:
                          E.tensor_mul(out=zA2, in0=scr[0:64, cA:cA + 512],
                                       in1=bcA[0:64, 0:512]))
                    z2w = kb.op("vector", lambda E, zB2=zB2, bcB=bcB, cB=cB:
                                E.tensor_mul(out=zB2, in0=scr[0:64, cB:cB + 512],
                                             in1=bcB[0:64, 0:512]), inc=(v_ev, 1))
                    WB.release(jbA, v_ev, z2w)
                    WB.release(jbB, v_ev, z2w)
                    return z2w

                if tqc > 0:
                    emit_zdiv(tqc - 1)
                if tqc == 3:
                    last_z2w = emit_zdiv(3)

            kb.wait("tensor", v_ev, last_z2w)
            kb.wait("tensor", v_ev, v_x2)
            for tc in range(16):
                j0, wb0 = WB.acquire()
                j1, wb1 = WB.acquire()
                lhs = arena[:, A_Z2 + 128 * tc:A_Z2 + 128 * (tc + 1)]
                rhs0 = arena[:, woslot(sl):woslot(sl) + 512]
                rhs1 = arena[:, woslot(sl) + 512:woslot(sl) + 1024]
                kb.op("tensor", lambda E, wb0=wb0, lhs=lhs, rhs0=rhs0: E.matmul(
                    wb0[:, 0:512], lhs, rhs0, start=True, stop=True))
                omm = kb.op("tensor", lambda E, wb1=wb1, lhs=lhs, rhs1=rhs1: E.matmul(
                    wb1[:, 0:512], lhs, rhs1, start=True, stop=True), inc=(t_mm, 1))
                kb.wait("vector", t_mm, omm)
                d0 = x_tm[:, D * tc:D * tc + 512]
                d1 = x_tm[:, D * tc + 512:D * (tc + 1)]
                kb.op("vector", lambda E, wb0=wb0, d0=d0: E.tensor_add(
                    out=d0, in0=wb0[:, 0:512], in1=d0))
                oev = kb.op("vector", lambda E, wb1=wb1, d1=d1: E.tensor_add(
                    out=d1, in0=wb1[:, 0:512], in1=d1), inc=(v_ev, 1))
                WB.release(j0, v_ev, oev)
                WB.release(j1, v_ev, oev)
            w_free_at[sl] = (v_ev, v_ev.n)
            if hg + 2 < 8:
                emit_wload(hg + 2)

        for hg in range(8):
            attn_hg(hg)

        if DBG:
            kb.wait("sync", v_ev, v_ev.n)
            vpre = kb.op("sync", lambda E: E.dma_start(out=dbg_xtm, in_=x_tm[:]),
                         inc=(st, 16))
            kb.wait("vector", st, vpre)

        def layernorm_inplace():
            for tc in range(16):
                xs = x_tm[:, D * tc:D * (tc + 1)]
                negm = scr[:, C_SM:C_SM + 1]
                ssq = scr[:, C_SM + 1:C_SM + 2]
                rstd = scr[:, C_SM + 2:C_SM + 3]
                nsum = scr[:, C_SM + 3:C_SM + 4]
                vred = kb.op("vector", lambda E, xs=xs, nsum=nsum: E.reduce_sum(
                    out=nsum, in_=xs, axis=AX, negate=True), inc=(v_ev, 1))
                kb.wait("scalar", v_ev, vred)
                asc = kb.op("scalar", lambda E, nsum=nsum, negm=negm: E.activation(
                    out=negm, in_=nsum, func=AF.Copy, scale=1.0 / D),
                    inc=(a_ev, 1))
                kb.wait("vector", a_ev, asc)
                vsub = kb.op("vector", lambda E, xs=xs, negm=negm: E.tensor_scalar_add(
                    out=xs, in0=xs, scalar1=negm), inc=(v_ev, 1))
                kb.wait("scalar", v_ev, vsub)
                kb.op("scalar", lambda E, xs=xs, ssq=ssq: E.activation(
                    out=scr[:, C_DUMP:C_DUMP + 1024], in_=xs, func=AF.Square,
                    accum_out=ssq))
                kb.op("scalar", lambda E, ssq=ssq, rstd=rstd: E.activation(
                    out=rstd, in_=ssq, func=AF.Ln, scale=1.0 / D,
                    bias=fconst[:, 0:1]))
                arstd = kb.op("scalar", lambda E, rstd=rstd: E.activation(
                    out=rstd, in_=rstd, func=AF.Exp, scale=-0.5), inc=(a_ev, 1))
                kb.wait("vector", a_ev, arstd)
                kb.op("vector", lambda E, xs=xs, rstd=rstd: E.tensor_scalar_mul(
                    out=xs, in0=xs, scalar1=rstd), inc=(v_ev, 1))

        layernorm_inplace()
        v_ln1 = v_ev.n

        if stage >= 2:
            v_ln1 = v_ev.n
            a_ln1 = a_ev.n
            A4_XFM2 = 0
            A4_H1 = 16384
            A4_W1 = 32768
            A4_W2 = 34816
            A4_WR = 51200
            A4_B2 = 51264
            A4_XC = 52288
            A4_CMP = 54336
            C_SBG = 0
            C_WBG = 128
            C_THR = 256
            C_BI = 264
            C_WSC = 272
            C_B1 = 400
            C_SF = 1024

            xfm2 = arena[:, A4_XFM2:A4_XFM2 + 16384].rearrange(
                "p (dc f) -> p dc f", dc=8)

            kb.op("sync", lambda E: E.dma_start(out=ident32[:], in_=ident32_c),
                  inc=(s_ld, 16))
            kb.op("sync", lambda E: E.dma_start(out=ones32[:], in_=ones32_c),
                  inc=(s_ld, 16))
            kb.op("sync", lambda E: E.dma_start(out=e128[:], in_=e128_c),
                  inc=(s_ld, 16))
            A4_B2 = 51264
            wr3 = wr_c.rearrange("(dc p) e -> p dc e", p=128)
            kb.op("sync", lambda E: E.dma_start(
                out=scr[:, 2944:3008].rearrange("p (dc e) -> p dc e", dc=8),
                in_=wr3), inc=(s_ld, 16))
            kb.wait("gpsimd", g_ld, g_ld.n)
            p4cg = kb.op("gpsimd", lambda E: E.dma_start(
                out=arena[0:8, A4_B2:A4_B2 + 1024], in_=b2_c), inc=(g_ld, 16))
            p4c = s_ld.n

            kb.wait("vector", v_ev, v_ln1)
            kb.wait("vector", a_ev, a_ln1)
            xc_ev = []
            for tc in range(16):
                slot = A4_XC + 1024 * (tc % 2)
                if tc >= 2:
                    kb.wait("vector", t_mm, xc_ev[tc - 2][0])
                vv = kb.op("vector", lambda E, tc=tc, slot=slot: E.tensor_copy(
                    out=arena[:, slot:slot + 1024],
                    in_=x_tm[:, D * tc:D * (tc + 1)]), inc=(v_ev, 1))
                kb.wait("tensor", v_ev, vv)
                pt = [tpA, tpB][tc % 2]
                if tc >= 2:
                    kb.wait("tensor", v_ev, xc_ev[tc - 2][1])
                tmm = None
                for dc in range(8):
                    sap = arena[:, slot + 128 * dc:slot + 128 * (dc + 1)]
                    tmm = kb.op("tensor", lambda E, pt=pt, dc=dc, sap=sap:
                                E.transpose(pt[:, 128 * dc:128 * (dc + 1)], sap,
                                            ident[:]),
                                inc=(t_mm, 1) if dc == 7 else None)
                kb.wait("vector", t_mm, tmm)
                ptv = pt[:].rearrange("p (dc f) -> p dc f", dc=8)
                ev = kb.op("vector", lambda E, tc=tc, ptv=ptv: E.tensor_copy(
                    out=xfm2[:, :, 128 * tc:128 * (tc + 1)], in_=ptv),
                    inc=(v_ev, 1))
                xc_ev.append((tmm, ev))
            v_xfm2 = xc_ev[-1][1]

            kb.wait("tensor", s_ld, p4c)
            kb.wait("tensor", g_ld, p4cg)
            kb.wait("tensor", v_ev, v_ln1)
            kb.wait("tensor", a_ev, a_ln1)
            sden = scr[:, 2100:2101]
            rden = scr[:, 2101:2102]
            xstg = scr[:, C_SF:C_SF + 1024]
            for tc in range(16):
                jt, wbt = WB.acquire()
                jt2, wbt2 = WB.acquire()
                for dc in range(8):
                    wbx = wbt if dc < 4 else wbt2
                    kb.op("tensor", lambda E, wbx=wbx, dc=dc, tc=tc: E.transpose(
                        wbx[:, 128 * (dc % 4):128 * (dc % 4 + 1)],
                        x_tm[:, D * tc + 128 * dc:D * tc + 128 * (dc + 1)],
                        ident32[:]), inc=(t_mm, 1) if dc == 7 else None)
                tvx = t_mm.n
                kb.wait("vector", t_mm, tvx)
                kb.op("vector", lambda E, wbt=wbt: E.tensor_copy(
                    out=xstg[:, 0:512], in_=wbt[:, 0:512]))
                vx = kb.op("vector", lambda E, wbt2=wbt2: E.tensor_copy(
                    out=xstg[:, 512:1024], in_=wbt2[:, 0:512]), inc=(v_ev, 1))
                WB.release(jt, v_ev, vx)
                WB.release(jt2, v_ev, vx)
                j, wb = WB.acquire()
                kb.wait("tensor", v_ev, vx)
                for dc in range(8):
                    kb.op("tensor", lambda E, wb=wb, dc=dc: E.matmul(
                        wb[:, 0:8],
                        xstg[:, 128 * dc:128 * (dc + 1)],
                        scr[:, 2944 + 8 * dc:2944 + 8 * (dc + 1)],
                        start=(dc == 0), stop=(dc == 7)),
                        inc=(t_mm, 1) if dc == 7 else None)
                tv = t_mm.n
                kb.wait("vector", t_mm, tv)
                negmax = scr[:, 2102:2103]
                vm = kb.op("vector", lambda E, wb=wb, negmax=negmax: E.reduce_max(
                    out=negmax, in_=wb[:, 0:8], axis=AX, negate=True),
                    inc=(v_ev, 1))
                kb.wait("scalar", v_ev, vm)
                sb = scr[:, C_SBG + 8 * tc:C_SBG + 8 * (tc + 1)]
                ae = kb.op("scalar", lambda E, wb=wb, sb=sb, negmax=negmax:
                           E.activation(out=sb, in_=wb[:, 0:8], func=AF.Exp,
                                        bias=negmax, accum_out=sden),
                           inc=(a_ev, 1))
                WB.release(j, a_ev, ae)
                kb.wait("vector", a_ev, ae)
                vr = kb.op("vector", lambda E: E.reciprocal(out=rden, in_=sden),
                           inc=(v_ev, 1))
                kb.wait("scalar", v_ev, vr)
                kb.op("scalar", lambda E, sb=sb: E.activation(
                    out=sb, in_=sb, func=AF.Copy, scale=rden), inc=(a_ev, 1))
            a_probs = a_ev.n

            kb.wait("tensor", a_ev, a_probs)
            for tc in range(16):
                j, wb = WB.acquire()
                tv = kb.op("tensor", lambda E, wb=wb, tc=tc: E.transpose(
                    wb[0:8, 0:128], scr[:, C_SBG + 8 * tc:C_SBG + 8 * (tc + 1)],
                    ident32[:]), inc=(t_mm, 1))
                kb.wait("vector", t_mm, tv)
                colo = (C_SF + 128 * tc) if tc < 8 else (2304 + 128 * (tc - 8))
                vv = kb.op("vector", lambda E, wb=wb, colo=colo:
                           E.tensor_copy(out=scr[0:8, colo:colo + 128],
                                         in_=wb[0:8, 0:128]), inc=(v_ev, 1))
                WB.release(j, v_ev, vv)
            v_sem8 = v_ev.n
            kb.wait("sync", v_ev, v_sem8)
            kb.op("sync", lambda E: E.dma_start(
                out=sc_ccin[:, 0:1024], in_=scr[0:8, C_SF:C_SF + 1024]),
                inc=(st, 16))
            stv = kb.op("sync", lambda E: E.dma_start(
                out=sc_ccin[:, 1024:2048], in_=scr[0:8, 2304:3328]),
                inc=(st, 16))
            kb.wait("gpsimd", st, stv)
            ag_done = kb.op("gpsimd", lambda E: E.collective_compute(
                "AllGather", ALU.bypass, ins=[sc_ccin], outs=[sc_ccout],
                replica_groups=[list(range(8))]), inc=(g_ms, 1))
            kb.wait("sync", g_ms, ag_done)
            sfold = scr[:, C_SF:C_SF + 1024]
            sf_ld = None
            for c2 in range(8):
                for h in range(2):
                    dst = scr[h + 2 * c2::16, C_SF:C_SF + 1024]
                    ssrc = sc_ccout[8 * c2:8 * (c2 + 1), 1024 * h:1024 * (h + 1)]
                    sf_ld = kb.op("sync", lambda E, dst=dst, ssrc=ssrc:
                                  E.dma_start(out=dst, in_=ssrc), inc=(s_ld, 16))

            lo = scr[:, C_BI + 0:C_BI + 1]
            hi = scr[:, C_BI + 1:C_BI + 2]
            mid = scr[:, C_BI + 2:C_BI + 3]
            sum_lh = scr[:, C_BI + 3:C_BI + 4]
            part = scr[:, C_BI + 4:C_BI + 5]
            cond = scr[:, C_BI + 5:C_BI + 6]
            cond_c = scr[:, C_BI + 6:C_BI + 7]
            dtmp = scr[:, C_BI + 7:C_BI + 8]
            cmp_bf = arena[:, A4_CMP:A4_CMP + 1024]
            kb.op("vector", lambda E: E.memset(lo, 0.0))
            kb.wait("vector", s_ld, sf_ld)
            kb.op("vector", lambda E: E.memset(hi, 1.0), inc=(v_ev, 1))
            for rnd in range(40):
                kb.drain("vector")
                va = kb.op("vector", lambda E: E.tensor_add(
                    out=sum_lh, in0=lo, in1=hi), inc=(v_ev, 1))
                kb.wait("scalar", v_ev, va)
                am = kb.op("scalar", lambda E: E.activation(
                    out=mid, in_=sum_lh, func=AF.Copy, scale=0.5), inc=(a_ev, 1))
                kb.wait("vector", a_ev, am)
                kb.op("vector", lambda E: E.tensor_scalar(
                    out=cmp_bf, in0=sfold, scalar1=mid, scalar2=None,
                    op0=ALU.is_gt))
                kb.drain("vector")
                vp = kb.op("vector", lambda E: E.reduce_sum(
                    out=part, in_=cmp_bf, axis=AX), inc=(v_ev, 1))
                kb.wait("tensor", v_ev, vp)
                tb = kb.op("tensor", lambda E: E.matmul(
                    wk0[:, 0:1], e128[:], part, start=True, stop=True),
                    inc=(t_mm, 1))
                kb.wait("vector", t_mm, tb)
                vc = kb.op("vector", lambda E: E.tensor_scalar(
                    out=cond, in0=wk0[:, 0:1], scalar1=4096.0, scalar2=None,
                    op0=ALU.is_ge), inc=(v_ev, 1))
                kb.wait("scalar", v_ev, vc)
                ac = kb.op("scalar", lambda E: E.activation(
                    out=cond_c, in_=cond, func=AF.Copy), inc=(a_ev, 1))
                kb.wait("vector", a_ev, ac)
                kb.drain("vector")
                kb.op("vector", lambda E: E.tensor_sub(out=dtmp, in0=mid, in1=lo))
                kb.drain("vector")
                kb.op("vector", lambda E: E.scalar_tensor_tensor(
                    out=lo, in0=dtmp, scalar=cond_c, in1=lo,
                    op0=ALU.mult, op1=ALU.add))
                kb.drain("vector")
                kb.op("vector", lambda E: E.tensor_sub(out=dtmp, in0=hi, in1=mid))
                kb.drain("vector")
                vs = kb.op("vector", lambda E: E.scalar_tensor_tensor(
                    out=hi, in0=dtmp, scalar=cond_c, in1=mid,
                    op0=ALU.mult, op1=ALU.add), inc=(v_ev, 1))
                kb.wait("tensor", v_ev, vs)
            kb.drain("vector")
            vb = kb.op("vector", lambda E: E.tensor_scalar_mul(
                out=part, in0=lo, scalar1=1.0 / 16.0), inc=(v_ev, 1))
            kb.wait("tensor", v_ev, vb)
            t1 = kb.op("tensor", lambda E: E.matmul(
                wk0[0:1, 0:8], part, e128[:, 0::16], start=True, stop=True),
                inc=(t_mm, 1))
            kb.wait("vector", t_mm, t1)
            vthr = kb.op("vector", lambda E: E.tensor_copy(
                out=scr[0:1, C_THR + 8:C_THR + 16], in_=wk0[0:1, 0:8]),
                inc=(v_ev, 1))
            kb.wait("tensor", v_ev, vthr)
            t2 = kb.op("tensor", lambda E: E.matmul(
                wk1[:, 0:8], ones32[0:1, :],
                scr[0:1, C_THR + 8:C_THR + 16], start=True, stop=True),
                inc=(t_mm, 1))
            kb.wait("vector", t_mm, t2)
            gthr = kb.op("vector", lambda E: E.tensor_copy(
                out=scr[:, C_THR:C_THR + 8], in_=wk1[:, 0:8]), inc=(v_ev, 1))
            kb.drain("vector")

            vmk = None
            for tc in range(16):
                sb = scr[:, C_SBG + 8 * tc:C_SBG + 8 * (tc + 1)]
                wbg = scr[:, C_WBG + 8 * tc:C_WBG + 8 * (tc + 1)]
                kb.op("vector", lambda E, sb=sb, wbg=wbg: E.tensor_tensor(
                    out=wbg, in0=sb, in1=scr[:, C_THR:C_THR + 8], op=ALU.is_gt))
                kb.drain("vector")
                vmk = kb.op("vector", lambda E, sb=sb, wbg=wbg: E.tensor_tensor(
                    out=wbg, in0=sb, in1=wbg, op=ALU.mult),
                    inc=(v_ev, 1) if tc == 15 else None)
            kb.wait("scalar", v_ev, vmk)
            aws = kb.op("scalar", lambda E: E.activation(
                out=scr[:, C_WSC:C_WSC + 128], in_=scr[:, C_WBG:C_WBG + 128],
                func=AF.Copy), inc=(a_ev, 1))

            kb.wait("tensor", a_ev, aws)
            kb.wait("vector", st, stv)
            wem_ev = None
            for tc in range(16):
                j, wb = WB.acquire()
                tv = kb.op("tensor", lambda E, wb=wb, tc=tc: E.transpose(
                    wb[0:8, 0:128], scr[:, C_WSC + 8 * tc:C_WSC + 8 * (tc + 1)],
                    ident32[:]), inc=(t_mm, 1))
                kb.wait("vector", t_mm, tv)
                wem_ev = kb.op("vector", lambda E, wb=wb, tc=tc: E.tensor_copy(
                    out=arena[0:8, A4_XC + 128 * tc:A4_XC + 128 * (tc + 1)],
                    in_=wb[0:8, 0:128]), inc=(v_ev, 1))
                WB.release(j, v_ev, wem_ev)
            kb.wait("tensor", v_ev, wem_ev)
            for tc in range(16):
                for half in range(2):
                    j, wb = WB.acquire()
                    tv = kb.op("tensor", lambda E, wb=wb, tc=tc, half=half:
                               E.matmul(wb[:, 0:512],
                                        arena[0:8, A4_XC + 128 * tc:
                                              A4_XC + 128 * (tc + 1)],
                                        arena[0:8, A4_B2 + 512 * half:
                                              A4_B2 + 512 * (half + 1)],
                                        start=True, stop=True), inc=(t_mm, 1))
                    kb.wait("vector", t_mm, tv)
                    dstx = x_tm[:, D * tc + 512 * half:D * tc + 512 * (half + 1)]
                    vv = kb.op("vector", lambda E, wb=wb, dstx=dstx: E.tensor_add(
                        out=dstx, in0=wb[:, 0:512], in1=dstx), inc=(v_ev, 1))
                    WB.release(j, v_ev, vv)

            if DBG:
                kb.wait("sync", v_ev, v_ev.n)
                kb.wait("sync", a_ev, a_ev.n)
                kb.op("sync", lambda E: E.dma_start(
                    out=dbg_scr, in_=scr[:, 0:1024]), inc=(st, 16))

            w1slot = lambda s: A4_W1 + 1024 * s
            w2slot = lambda s: A4_W2 + 8192 * s
            w1_i = [0]
            w2_state = {"free": [None, None]}
            p4_g_last = [g_ld.n]

            def load_w1(e, g, hc, free_at):
                sl = w1_i[0] % 2
                w1_i[0] += 1
                kb.wait("gpsimd", g_ld, p4_g_last[0])
                if free_at[sl] is not None:
                    kb.wait("gpsimd", *free_at[sl])
                col = 128 * (8 * g + hc)
                src3 = w1_c[e, :, col:col + 128].rearrange(
                    "(dc p) c -> p dc c", p=128)
                dst3 = arena[:, w1slot(sl):w1slot(sl) + 1024].rearrange(
                    "p (dc c) -> p dc c", dc=8)
                v = kb.op("gpsimd", lambda E, a=dst3, b=src3: E.dma_start(
                    out=a, in_=b), inc=(g_ld, 16))
                p4_g_last[0] = v
                return sl, v

            def load_w2(e, g):
                sl = (e * 4 + g) % 2
                kb.wait("gpsimd", g_ld, p4_g_last[0])
                if w2_state["free"][sl] is not None:
                    kb.wait("gpsimd", *w2_state["free"][sl])
                v = None
                for hc in range(8):
                    row = 128 * (8 * g + hc)
                    ssrc = w2_c[e, row:row + 128, :]
                    dst = arena[:, w2slot(sl) + 1024 * hc:
                                w2slot(sl) + 1024 * (hc + 1)]
                    v = kb.op("gpsimd", lambda E, a=dst, b=ssrc: E.dma_start(
                        out=a, in_=b), inc=(g_ld, 16))
                p4_g_last[0] = v
                return sl, v

            def load_b1(e):
                kb.wait("gpsimd", g_ld, p4_g_last[0])
                v = kb.op("gpsimd", lambda E, e=e: E.dma_start(
                    out=scr[:, C_B1:C_B1 + 32], in_=b1_c[e]), inc=(g_ld, 16))
                p4_g_last[0] = v
                return v

            w1_free = [None, None]
            xps = [zz, wk23]
            xfree = [None, None]
            kb.wait("tensor", v_ev, v_xfm2)
            for e in range(8):
                b1v = load_b1(e)
                for g in range(4):
                    w2sl, w2v = load_w2(e, g)
                    h1_ev = None
                    for hc in range(8):
                        w1sl, w1v = load_w1(e, g, hc, w1_free)
                        kb.wait("tensor", g_ld, w1v)
                        kb.wait("tensor", g_ld, b1v)
                        hv = None
                        for tq in range(4):
                            j, wb = WB.acquire()
                            for dc in range(8):
                                kb.op("tensor", lambda E, wb=wb, dc=dc, tq=tq,
                                      w1sl=w1sl: E.matmul(
                                          wb[:, 0:512],
                                          arena[:, w1slot(w1sl) + 128 * dc:
                                                w1slot(w1sl) + 128 * (dc + 1)],
                                          xfm2[:, dc, 512 * tq:512 * (tq + 1)],
                                          start=(dc == 0), stop=(dc == 7)),
                                      inc=(t_mm, 1) if dc == 7 else None)
                            tv = t_mm.n
                            kb.wait("scalar", t_mm, tv)
                            dsth = arena[:, A4_H1 + 2048 * hc + 512 * tq:
                                         A4_H1 + 2048 * hc + 512 * (tq + 1)]
                            hv = kb.op("scalar", lambda E, wb=wb, dsth=dsth:
                                       E.activation(out=dsth, in_=wb[:, 0:512],
                                                    func=AF.Gelu,
                                                    bias=scr[:, C_B1 + 8 * g +
                                                             hc:C_B1 + 8 * g +
                                                             hc + 1]),
                                       inc=(a_ev, 1))
                            WB.release(j, a_ev, hv)
                        w1_free[w1sl] = (a_ev, hv)
                        h1_ev = hv
                    kb.wait("tensor", a_ev, h1_ev)
                    kb.wait("tensor", g_ld, w2v)
                    xb_ev = None
                    for tc in range(16):
                        ps = xps[tc % 2]
                        if xfree[tc % 2] is not None:
                            kb.wait("tensor", v_ev, xfree[tc % 2])
                        for hc in range(8):
                            lhs = arena[:, A4_H1 + 2048 * hc + 128 * tc:
                                        A4_H1 + 2048 * hc + 128 * (tc + 1)]
                            kb.op("tensor", lambda E, ps=ps, lhs=lhs, hc=hc,
                                  w2sl=w2sl: E.matmul(
                                      ps[:, 0:512], lhs,
                                      arena[:, w2slot(w2sl) + 1024 * hc:
                                            w2slot(w2sl) + 1024 * hc + 512],
                                      start=(hc == 0), stop=False))
                            kb.op("tensor", lambda E, ps=ps, lhs=lhs, hc=hc,
                                  w2sl=w2sl: E.matmul(
                                      ps[:, 512:1024], lhs,
                                      arena[:, w2slot(w2sl) + 1024 * hc + 512:
                                            w2slot(w2sl) + 1024 * (hc + 1)],
                                      start=(hc == 0), stop=(hc == 7)),
                                  inc=(t_mm, 1) if hc == 7 else None)
                        tv = t_mm.n
                        kb.wait("vector", t_mm, tv)
                        wcol = scr[:, C_WSC + 8 * tc + e:C_WSC + 8 * tc + e + 1]
                        d0 = x_tm[:, D * tc:D * tc + 512]
                        d1 = x_tm[:, D * tc + 512:D * (tc + 1)]
                        kb.op("vector", lambda E, ps=ps, wcol=wcol, d0=d0:
                              E.scalar_tensor_tensor(
                                  out=d0, in0=ps[:, 0:512], scalar=wcol,
                                  in1=d0, op0=ALU.mult, op1=ALU.add))
                        xb_ev = kb.op("vector", lambda E, ps=ps, wcol=wcol, d1=d1:
                                      E.scalar_tensor_tensor(
                                          out=d1, in0=ps[:, 512:1024],
                                          scalar=wcol, in1=d1,
                                          op0=ALU.mult, op1=ALU.add),
                                      inc=(v_ev, 1))
                        xfree[tc % 2] = xb_ev
                    w2_state["free"][w2sl] = (v_ev, xb_ev)
                    WB.free_at[2] = (v_ev, xb_ev)
                    WB.free_at[3] = (v_ev, xb_ev)

            if DBG:
                kb.wait("sync", v_ev, v_ev.n)
                vpre2 = kb.op("sync", lambda E: E.dma_start(
                    out=dbg_xtm2, in_=x_tm[:]), inc=(st, 16))
                kb.wait("vector", st, vpre2)

            layernorm_inplace()

        if stage >= 3:
            NRM = 0.3535533905932738
            RATIO_EPS = 1e-6 / 16.0
            A5_XFM3 = 0
            A5_KF = 16384
            A5_V1 = 32768
            A5_KP = 49408
            A5_KV = 57600
            A5_PR = 59680
            A5_WS = 61984
            A5_BO = 59936
            A5_BV = 60960
            C5_BIAS = 2104
            xfm3 = arena[:, A5_XFM3:A5_XFM3 + 16384].rearrange(
                "p (dc f) -> p dc f", dc=8)
            v5 = arena[:, A5_V1:A5_V1 + 16640].rearrange(
                "p (tc s) -> p tc s", tc=16)
            v_ln2 = v_ev.n
            a_ln2 = a_ev.n

            kb.op("sync", lambda E: E.dma_start(
                out=scr[:, C5_BIAS:C5_BIAS + 8], in_=l1bq), inc=(s_ld, 16))
            kb.op("sync", lambda E: E.dma_start(
                out=scr[:, C5_BIAS + 8:C5_BIAS + 16], in_=l1bk), inc=(s_ld, 16))
            p5c = kb.op("sync", lambda E: E.dma_start(
                out=scr[:, 2304:2560], in_=l1proj), inc=(s_ld, 16))
            kb.wait("gpsimd", g_ld, g_ld.n)
            kb.op("gpsimd", lambda E: E.dma_start(
                out=arena[:, A5_BO:A5_BO + 1024], in_=l1bo_bc), inc=(g_ld, 16))
            p5cg = kb.op("gpsimd", lambda E: E.dma_start(
                out=arena[:, A5_BV:A5_BV + 1024], in_=l1bv_bc), inc=(g_ld, 16))
            kb.wait("scalar", s_ld, p5c)
            kb.wait("scalar", v_ev, v_ln2)
            aprj = kb.op("scalar", lambda E: E.activation(
                out=arena[:, A5_PR:A5_PR + 256], in_=scr[:, 2304:2560],
                func=AF.Copy, scale=NRM), inc=(a_ev, 1))

            kb.wait("vector", v_ev, v_ln2)
            kb.wait("vector", a_ev, a_ln2)
            xc_ev5 = []
            for tc in range(16):
                slot = A4_XC + 1024 * (tc % 2)
                if tc >= 2:
                    kb.wait("vector", t_mm, xc_ev5[tc - 2][0])
                vv = kb.op("vector", lambda E, tc=tc, slot=slot: E.tensor_copy(
                    out=arena[:, slot:slot + 1024],
                    in_=x_tm[:, D * tc:D * (tc + 1)]), inc=(v_ev, 1))
                kb.wait("tensor", v_ev, vv)
                pt = [tpA, tpB][tc % 2]
                if tc >= 2:
                    kb.wait("tensor", v_ev, xc_ev5[tc - 2][1])
                tmm = None
                for dc in range(8):
                    sap = arena[:, slot + 128 * dc:slot + 128 * (dc + 1)]
                    tmm = kb.op("tensor",
                                lambda E, pt=pt, dc=dc, sap=sap: E.transpose(
                                    pt[:, 128 * dc:128 * (dc + 1)], sap,
                                    ident[:]),
                                inc=(t_mm, 1) if dc == 7 else None)
                kb.wait("vector", t_mm, tmm)
                ptv = pt[:].rearrange("p (dc f) -> p dc f", dc=8)
                ev = kb.op("vector", lambda E, tc=tc, ptv=ptv: E.tensor_copy(
                    out=xfm3[:, :, 128 * tc:128 * (tc + 1)], in_=ptv),
                    inc=(v_ev, 1))
                xc_ev5.append((tmm, ev))
            v_xfm3 = xc_ev5[-1][1]

            p5_g_last = [g_ld.n]
            ws_free = [None, None]
            ws_i = [0]

            def load_wslice(wt, fc):
                sl = ws_i[0] % 2
                ws_i[0] += 1
                kb.wait("gpsimd", g_ld, p5_g_last[0])
                if ws_free[sl] is not None:
                    kb.wait("gpsimd", *ws_free[sl])
                src3 = wt[:, 128 * fc:128 * (fc + 1)].rearrange(
                    "(dc p) c -> p dc c", p=128)
                dst3 = arena[:, A5_WS + 1024 * sl:A5_WS + 1024 * (sl + 1)].rearrange(
                    "p (dc c) -> p dc c", dc=8)
                v = kb.op("gpsimd", lambda E, a=dst3, b=src3: E.dma_start(
                    out=a, in_=b), inc=(g_ld, 16))
                p5_g_last[0] = v
                return sl, v

            kb.wait("tensor", g_ld, p5cg)
            for fc in range(8):
                sl, wv = load_wslice(l1wk, fc)
                kb.wait("tensor", g_ld, wv)
                kb.wait("tensor", v_ev, v_xfm3)
                for ch in range(4):
                    j, wb = WB.acquire()
                    for dc in range(8):
                        kb.op("tensor", lambda E, wb=wb, dc=dc, ch=ch, sl=sl:
                              E.matmul(wb[:, 0:512],
                                       arena[:, A5_WS + 1024 * sl + 128 * dc:
                                             A5_WS + 1024 * sl + 128 * (dc + 1)],
                                       xfm3[:, dc, 512 * ch:512 * (ch + 1)],
                                       start=(dc == 0), stop=(dc == 7)),
                              inc=(t_mm, 1) if dc == 7 else None)
                    tv = t_mm.n
                    kb.wait("vector", t_mm, tv)
                    dst = arena[:, A5_KF + 2048 * fc + 512 * ch:
                                A5_KF + 2048 * fc + 512 * (ch + 1)]
                    bcol = scr[:, C5_BIAS + 8 + fc:C5_BIAS + 9 + fc]
                    vv = kb.op("vector", lambda E, wb=wb, dst=dst, bcol=bcol:
                               E.tensor_scalar(out=dst, in0=wb[:, 0:512],
                                               scalar1=bcol, scalar2=None,
                                               op0=ALU.add), inc=(v_ev, 1))
                    WB.release(j, v_ev, vv)
                ws_free[sl] = (v_ev, v_ev.n)

            vones5 = kb.op("vector", lambda E: E.memset(
                arena[:, A5_V1:A5_V1 + 16640], 1.0), inc=(v_ev, 1))
            for fc in range(8):
                sl, wv = load_wslice(l1wv, fc)
                kb.wait("tensor", g_ld, wv)
                kb.wait("tensor", v_ev, vones5)
                for tc in range(16):
                    j, wb = WB.acquire()
                    for dc in range(8):
                        kb.op("tensor", lambda E, wb=wb, dc=dc, tc=tc, sl=sl:
                              E.matmul(wb[:, 0:128],
                                       xfm3[:, dc, 128 * tc:128 * (tc + 1)],
                                       arena[:, A5_WS + 1024 * sl + 128 * dc:
                                             A5_WS + 1024 * sl + 128 * (dc + 1)],
                                       start=(dc == 0), stop=(dc == 7)),
                              inc=(t_mm, 1) if dc == 7 else None)
                    tv = t_mm.n
                    kb.wait("vector", t_mm, tv)
                    hA, hB = 2 * fc, 2 * fc + 1
                    bvA = arena[:, A5_BV + 128 * fc:A5_BV + 128 * fc + 64]
                    bvB = arena[:, A5_BV + 128 * fc + 64:A5_BV + 128 * (fc + 1)]
                    kb.op("vector", lambda E, wb=wb, tc=tc, hA=hA, bvA=bvA:
                          E.tensor_add(out=v5[:, tc, 65 * hA:65 * hA + 64],
                                       in0=wb[:, 0:64], in1=bvA))
                    vv = kb.op("vector", lambda E, wb=wb, tc=tc, hB=hB, bvB=bvB:
                               E.tensor_add(out=v5[:, tc, 65 * hB:65 * hB + 64],
                                            in0=wb[:, 64:128], in1=bvB),
                               inc=(v_ev, 1))
                    WB.release(j, v_ev, vv)
                ws_free[sl] = (v_ev, v_ev.n)

            kb.wait("scalar", a_ev, aprj)
            kq_i = [0]
            kp_free = [None, None]
            tp5_guard = [None, None]

            def kprime(h, src_base):
                sl = kq_i[0] % 2
                kq_i[0] += 1
                if kp_free[sl] is not None:
                    kb.wait("tensor", *kp_free[sl])
                fc, half = h // 2, h % 2
                kbase = src_base + 2048 * fc
                pr = 64 * half
                last = None
                for tc in range(16):
                    j, wb = WB.acquire()
                    lhs = arena[pr:pr + 64, kbase + 128 * tc:kbase + 128 * (tc + 1)]
                    kb.op("tensor", lambda E, wb=wb, lhs=lhs, pr=pr: E.matmul(
                        wb[:, 0:256], lhs, arena[pr:pr + 64, A5_PR:A5_PR + 256],
                        start=True, stop=True))
                    pt = [tpA, tpB][tc % 2]
                    if tp5_guard[tc % 2] is not None:
                        kb.wait("tensor", v_ev, tp5_guard[tc % 2])
                    idsl = ident[pr:pr + 64, pr:pr + 64]
                    tv = kb.op("tensor", lambda E, pt=pt, lhs=lhs, idsl=idsl:
                               E.transpose(pt[:, 0:64], lhs, idsl),
                               inc=(t_mm, 1))
                    kb.wait("scalar", t_mm, tv)
                    sq = scr[:, 2140:2204]
                    dg = scr[:, 2204:2205]
                    asq = kb.op("scalar", lambda E, pt=pt, sq=sq: E.activation(
                        out=sq, in_=pt[:, 0:64], func=AF.Square), inc=(a_ev, 1))
                    kb.wait("vector", a_ev, asq)
                    vd = kb.op("vector", lambda E, sq=sq, dg=dg: E.reduce_sum(
                        out=dg, in_=sq, axis=AX), inc=(v_ev, 1))
                    tp5_guard[tc % 2] = vd
                    kb.wait("scalar", v_ev, vd)
                    et = scr[:, 2205:2206]
                    kb.op("scalar", lambda E, dg=dg, et=et: E.activation(
                        out=et, in_=dg, func=AF.Exp, scale=-(NRM * NRM) / 2.0,
                        bias=fconst[:, 1:2]))
                    dst = arena[:, A5_KP + 4096 * sl + 256 * tc:
                                A5_KP + 4096 * sl + 256 * (tc + 1)]
                    ae = kb.op("scalar", lambda E, wb=wb, dst=dst: E.activation(
                        out=dst, in_=wb[:, 0:256], func=AF.Exp), inc=(a_ev, 1))
                    kb.wait("vector", a_ev, ae)
                    WB.release(j, a_ev, ae)
                    last = kb.op("vector", lambda E, dst=dst, et=et:
                                 E.tensor_scalar(out=dst, in0=dst, scalar1=et,
                                                 scalar2=RATIO_EPS, op0=ALU.mult,
                                                 op1=ALU.add), inc=(v_ev, 1))
                return sl, last

            kv_st = []
            for h in range(16):
                sl, kl = kprime(h, A5_KF)
                kb.wait("tensor", v_ev, kl)
                for fc2 in range(2):
                    j, wb = WB.acquire()
                    for tc in range(16):
                        lhs = arena[:, A5_KP + 4096 * sl + 256 * tc + 128 * fc2:
                                    A5_KP + 4096 * sl + 256 * tc + 128 * (fc2 + 1)]
                        rhs = v5[:, tc, 65 * h:65 * (h + 1)]
                        kb.op("tensor", lambda E, wb=wb, lhs=lhs, rhs=rhs, tc=tc:
                              E.matmul(wb[:, 0:65], lhs, rhs,
                                       start=(tc == 0), stop=(tc == 15)),
                              inc=(t_mm, 1) if tc == 15 else None)
                    tv = t_mm.n
                    kb.wait("vector", t_mm, tv)
                    ksl = (2 * h + fc2) % 2
                    kvs = scr[:, 2208 + 65 * ksl:2273 + 65 * ksl]
                    if len(kv_st) >= 2:
                        kb.wait("vector", st, kv_st[-2])
                    vv = kb.op("vector", lambda E, wb=wb, kvs=kvs: E.tensor_copy(
                        out=kvs, in_=wb[:, 0:65]), inc=(v_ev, 1))
                    WB.release(j, v_ev, vv)
                    kb.wait("sync", v_ev, vv)
                    row0 = 256 * h + 128 * fc2
                    sv = kb.op("sync", lambda E, row0=row0, kvs=kvs: E.dma_start(
                        out=kv_ccin[row0:row0 + 128, :], in_=kvs),
                        inc=(st, 16))
                    if DBG:
                        sv = kb.op("sync", lambda E, row0=row0, kvs=kvs:
                                   E.dma_start(out=dbg_kvp[row0:row0 + 128, :],
                                               in_=kvs), inc=(st, 16))
                    kv_st.append(sv)
                kp_free[sl] = (t_mm, t_mm.n)
            if DBG:
                kb.wait("sync", st, kv_st[-1])
                kb.op("sync", lambda E: E.dma_start(
                    out=dbg_kp, in_=arena[:, A5_KP:A5_KP + 8192]), inc=(st, 16))
                kb.op("sync", lambda E: E.dma_start(
                    out=dbg_v5, in_=arena[:, A5_V1:A5_V1 + 16640]), inc=(st, 16))
            kb.wait("gpsimd", st, kv_st[-1])
            kvar = kb.op("gpsimd", lambda E: E.collective_compute(
                "AllReduce", ALU.add, ins=[kv_ccin], outs=[kv_ccout],
                replica_groups=[[0, 1], [2, 3], [4, 5], [6, 7]]), inc=(g_ms, 1))
            kvo4 = kv_ccout.rearrange("(h fc p) c -> p h fc c", h=16, fc=2)
            kva4 = arena[:, A5_KV:A5_KV + 2080].rearrange(
                "p (h fc c) -> p h fc c", h=16, fc=2)
            kb.wait("gpsimd", g_ms, kvar)
            kvld = kb.op("gpsimd", lambda E: E.dma_start(out=kva4, in_=kvo4),
                         inc=(g_ld, 16))
            p5_g_last[0] = kvld

            kb.wait("tensor", t_mm, t_mm.n)
            for fc in range(8):
                sl, wv = load_wslice(l1wq, fc)
                kb.wait("tensor", g_ld, wv)
                for ch in range(4):
                    j, wb = WB.acquire()
                    for dc in range(8):
                        kb.op("tensor", lambda E, wb=wb, dc=dc, ch=ch, sl=sl:
                              E.matmul(wb[:, 0:512],
                                       arena[:, A5_WS + 1024 * sl + 128 * dc:
                                             A5_WS + 1024 * sl + 128 * (dc + 1)],
                                       xfm3[:, dc, 512 * ch:512 * (ch + 1)],
                                       start=(dc == 0), stop=(dc == 7)),
                              inc=(t_mm, 1) if dc == 7 else None)
                    tv = t_mm.n
                    kb.wait("vector", t_mm, tv)
                    dst = arena[:, A5_KF + 2048 * fc + 512 * ch:
                                A5_KF + 2048 * fc + 512 * (ch + 1)]
                    bcol = scr[:, C5_BIAS + fc:C5_BIAS + 1 + fc]
                    vv = kb.op("vector", lambda E, wb=wb, dst=dst, bcol=bcol:
                               E.tensor_scalar(out=dst, in0=wb[:, 0:512],
                                               scalar1=bcol, scalar2=None,
                                               op0=ALU.add), inc=(v_ev, 1))
                    WB.release(j, v_ev, vv)
                ws_free[sl] = (v_ev, v_ev.n)


            z1 = arena[:, A5_V1:A5_V1 + 16384]
            kb.wait("tensor", g_ld, kvld)
            qp_ev = None
            for h in range(16):
                sl, ql = kprime(h, A5_KF)
                kb.wait("tensor", v_ev, ql)
                fsl = (sl + 1) % 2
                if kp_free[fsl] is not None:
                    kb.wait("tensor", *kp_free[fsl])
                tp_ev5 = []
                for tc in range(16):
                    pt = [tpA, tpB][tc % 2]
                    if tc >= 2:
                        kb.wait("tensor", v_ev, tp_ev5[tc - 2])
                    elif tp5_guard[tc % 2] is not None:
                        kb.wait("tensor", v_ev, tp5_guard[tc % 2])
                    tv = None
                    for f2 in range(2):
                        sap = arena[:, A5_KP + 4096 * sl + 256 * tc + 128 * f2:
                                    A5_KP + 4096 * sl + 256 * tc + 128 * (f2 + 1)]
                        tv = kb.op("tensor", lambda E, pt=pt, f2=f2, sap=sap:
                                   E.transpose(pt[:, 128 * f2:128 * (f2 + 1)],
                                               sap, ident[:]),
                                   inc=(t_mm, 1) if f2 == 1 else None)
                    kb.wait("vector", t_mm, tv)
                    vv = None
                    for f2 in range(2):
                        dst = arena[:, A5_KP + 4096 * fsl + 2048 * f2 + 128 * tc:
                                    A5_KP + 4096 * fsl + 2048 * f2 + 128 * (tc + 1)]
                        vv = kb.op("vector", lambda E, pt=pt, f2=f2, dst=dst:
                                   E.tensor_copy(out=dst,
                                                 in_=pt[:, 128 * f2:128 * (f2 + 1)]),
                                   inc=(v_ev, 1) if f2 == 1 else None)
                    tp_ev5.append(vv)
                    tp5_guard[tc % 2] = vv
                kb.wait("tensor", v_ev, tp_ev5[-1])
                for tc in range(16):
                    j, wb = WB.acquire()
                    for f2 in range(2):
                        lhs = arena[:, A5_KP + 4096 * fsl + 2048 * f2 + 128 * tc:
                                    A5_KP + 4096 * fsl + 2048 * f2 + 128 * (tc + 1)]
                        rhs = kva4[:, h, f2, :]
                        kb.op("tensor", lambda E, wb=wb, lhs=lhs, rhs=rhs, f2=f2:
                              E.matmul(wb[:, 0:65], lhs, rhs,
                                       start=(f2 == 0), stop=(f2 == 1)),
                              inc=(t_mm, 1) if f2 == 1 else None)
                    tv = t_mm.n
                    kb.wait("vector", t_mm, tv)
                    nrm = scr[:, 2340:2341]
                    rec = scr[:, 2341:2342]
                    rec_c = scr[:, 2342:2343]
                    kb.op("vector", lambda E, wb=wb, nrm=nrm: E.tensor_scalar(
                        out=nrm, in0=wb[:, 64:65], scalar1=1e-6, scalar2=None,
                        op0=ALU.add))
                    kb.drain("vector")
                    vr = kb.op("vector", lambda E, nrm=nrm, rec=rec: E.reciprocal(
                        out=rec, in_=nrm), inc=(v_ev, 1))
                    kb.wait("scalar", v_ev, vr)
                    arc = kb.op("scalar", lambda E, rec=rec, rec_c=rec_c:
                                E.activation(out=rec_c, in_=rec, func=AF.Copy),
                                inc=(a_ev, 1))
                    kb.wait("vector", a_ev, arc)
                    dst = z1[:, 1024 * tc + 64 * h:1024 * tc + 64 * (h + 1)]
                    qp_ev = kb.op("vector", lambda E, wb=wb, dst=dst, rec_c=rec_c:
                                  E.tensor_scalar(out=dst, in0=wb[:, 0:64],
                                                  scalar1=rec_c, scalar2=None,
                                                  op0=ALU.mult), inc=(v_ev, 1))
                    WB.release(j, v_ev, qp_ev)
                kp_free[sl] = (t_mm, t_mm.n)

            kb.wait("gpsimd", t_mm, t_mm.n)
            wo_ld = []
            for ec in range(8):
                kb.wait("gpsimd", g_ld, p5_g_last[0])
                dst = arena[:, A5_KP + 1024 * ec:A5_KP + 1024 * (ec + 1)]
                ssrc = l1wo[128 * ec:128 * (ec + 1), :]
                v = kb.op("gpsimd", lambda E, a=dst, b=ssrc: E.dma_start(
                    out=a, in_=b), inc=(g_ld, 16))
                p5_g_last[0] = v
                wo_ld.append(v)

            kb.wait("tensor", v_ev, qp_ev)
            z1fm = arena[:, A5_XFM3:A5_XFM3 + 16384].rearrange(
                "p (dc f) -> p dc f", dc=8)
            tp_ev6 = []
            for tc in range(16):
                pt = [tpA, tpB][tc % 2]
                if tc >= 2:
                    kb.wait("tensor", v_ev, tp_ev6[tc - 2])
                elif tp5_guard[tc % 2] is not None:
                    kb.wait("tensor", v_ev, tp5_guard[tc % 2])
                tmm = None
                for dc in range(8):
                    sap = z1[:, 1024 * tc + 128 * dc:1024 * tc + 128 * (dc + 1)]
                    tmm = kb.op("tensor", lambda E, pt=pt, dc=dc, sap=sap:
                                E.transpose(pt[:, 128 * dc:128 * (dc + 1)], sap,
                                            ident[:]),
                                inc=(t_mm, 1) if dc == 7 else None)
                kb.wait("vector", t_mm, tmm)
                ptv = pt[:].rearrange("p (dc f) -> p dc f", dc=8)
                ev = kb.op("vector", lambda E, tc=tc, ptv=ptv: E.tensor_copy(
                    out=z1fm[:, :, 128 * tc:128 * (tc + 1)], in_=ptv),
                    inc=(v_ev, 1))
                tp_ev6.append(ev)
            v_z1fm = tp_ev6[-1]

            kb.wait("vector", g_ld, p5cg)
            for tc in range(16):
                kb.op("vector", lambda E, tc=tc: E.tensor_add(
                    out=x_tm[:, D * tc:D * (tc + 1)],
                    in0=arena[:, A5_BO:A5_BO + 1024],
                    in1=x_tm[:, D * tc:D * (tc + 1)]),
                    inc=(v_ev, 1) if tc == 15 else None)
            kb.wait("tensor", v_ev, v_z1fm)
            kb.wait("tensor", g_ld, wo_ld[-1])
            xps5 = [zz, wk23]
            xfree5 = [None, None]
            o_ev = None
            for tc in range(16):
                ps = xps5[tc % 2]
                if xfree5[tc % 2] is not None:
                    kb.wait("tensor", v_ev, xfree5[tc % 2])
                for half in range(2):
                    for ec in range(8):
                        lhs = z1fm[:, ec, 128 * tc:128 * (tc + 1)]
                        rhs = arena[:, A5_KP + 1024 * ec + 512 * half:
                                    A5_KP + 1024 * ec + 512 * (half + 1)]
                        kb.op("tensor", lambda E, ps=ps, lhs=lhs, rhs=rhs,
                              ec=ec, half=half: E.matmul(
                                  ps[:, 512 * half:512 * (half + 1)], lhs, rhs,
                                  start=(ec == 0), stop=(ec == 7)),
                              inc=(t_mm, 1) if (half == 1 and ec == 7) else None)
                tv = t_mm.n
                kb.wait("vector", t_mm, tv)
                d0 = x_tm[:, D * tc:D * tc + 512]
                d1 = x_tm[:, D * tc + 512:D * (tc + 1)]
                kb.op("vector", lambda E, ps=ps, d0=d0: E.tensor_add(
                    out=d0, in0=ps[:, 0:512], in1=d0))
                o_ev = kb.op("vector", lambda E, ps=ps, d1=d1: E.tensor_add(
                    out=d1, in0=ps[:, 512:1024], in1=d1), inc=(v_ev, 1))
                xfree5[tc % 2] = o_ev

            if DBG:
                kb.wait("sync", v_ev, v_ev.n)
                kb.op("sync", lambda E: E.dma_start(
                    out=dbg_kv, in_=arena[:, A5_KV:A5_KV + 2080]), inc=(st, 16))
                kb.op("sync", lambda E: E.dma_start(
                    out=dbg_z1, in_=z1), inc=(st, 16))
                vpre3 = kb.op("sync", lambda E: E.dma_start(
                    out=dbg_xtm3, in_=x_tm[:]), inc=(st, 16))
                kb.wait("vector", st, vpre3)

            layernorm_inplace()

        if stage >= 4:
            A6_XFM4 = 0
            A6_A = 16384
            A6_GU = 38912
            A6_DW = 43008
            A6_WR = 57984
            A6_W1T = 54336
            A6_W2A = 54848
            A6_ST = 54856
            A6_AH = 55880
            C6_K0 = 2580
            C6_K0C = 2600
            C6_LG = 2620
            xfm4 = arena[:, A6_XFM4:A6_XFM4 + 16384].rearrange(
                "p (dc f) -> p dc f", dc=8)
            v_ln3 = v_ev.n
            a_ln3 = a_ev.n

            kb.op("sync", lambda E: E.dma_start(
                out=scr[:, 2560:2576], in_=jrow_c), inc=(s_ld, 16))
            p6c = kb.op("sync", lambda E: E.dma_start(
                out=scr[:, 2576:2577], in_=ab1p), inc=(s_ld, 16))
            kb.wait("gpsimd", g_ld, g_ld.n)
            wr23 = wr2_c.rearrange("(dc p) e -> p dc e", p=128)
            kb.op("gpsimd", lambda E: E.dma_start(
                out=arena[:, A6_WR:A6_WR + 64].rearrange(
                    "p (dc e) -> p dc e", dc=8), in_=wr23), inc=(g_ld, 16))
            aw13 = aw1T.rearrange("(dc p) a -> p dc a", p=128)
            kb.op("gpsimd", lambda E: E.dma_start(
                out=arena[:, A6_W1T:A6_W1T + 400].rearrange(
                    "p (dc a) -> p dc a", dc=8), in_=aw13), inc=(g_ld, 16))
            p6cg = kb.op("gpsimd", lambda E: E.dma_start(
                out=arena[:, A6_W2A:A6_W2A + 6], in_=aw2A), inc=(g_ld, 16))

            kb.wait("vector", v_ev, v_ln3)
            kb.wait("vector", a_ev, a_ln3)
            xc_ev6 = []
            for tc in range(16):
                slot = A4_XC + 1024 * (tc % 2)
                if tc >= 2:
                    kb.wait("vector", t_mm, xc_ev6[tc - 2][0])
                vv = kb.op("vector", lambda E, tc=tc, slot=slot: E.tensor_copy(
                    out=arena[:, slot:slot + 1024],
                    in_=x_tm[:, D * tc:D * (tc + 1)]), inc=(v_ev, 1))
                kb.wait("tensor", v_ev, vv)
                pt = [tpA, tpB][tc % 2]
                if tc >= 2:
                    kb.wait("tensor", v_ev, xc_ev6[tc - 2][1])
                elif tp5_guard[tc % 2] is not None:
                    kb.wait("tensor", v_ev, tp5_guard[tc % 2])
                tmm = None
                for dc in range(8):
                    sap = arena[:, slot + 128 * dc:slot + 128 * (dc + 1)]
                    tmm = kb.op("tensor",
                                lambda E, pt=pt, dc=dc, sap=sap: E.transpose(
                                    pt[:, 128 * dc:128 * (dc + 1)], sap,
                                    ident[:]),
                                inc=(t_mm, 1) if dc == 7 else None)
                kb.wait("vector", t_mm, tmm)
                ptv = pt[:].rearrange("p (dc f) -> p dc f", dc=8)
                ev = kb.op("vector", lambda E, tc=tc, ptv=ptv: E.tensor_copy(
                    out=xfm4[:, :, 128 * tc:128 * (tc + 1)], in_=ptv),
                    inc=(v_ev, 1))
                xc_ev6.append((tmm, ev))
            v_xfm4 = xc_ev6[-1][1]

            kb.wait("tensor", g_ld, p6cg)
            kb.wait("tensor", s_ld, p6c)
            kb.wait("tensor", v_ev, v_xfm4)
            sden6 = scr[:, 2100:2101]
            rden6 = scr[:, 2101:2102]
            for tc in range(16):
                j, wb = WB.acquire()
                for dc in range(8):
                    kb.op("tensor", lambda E, wb=wb, dc=dc, tc=tc: E.matmul(
                        wb[:, 0:8],
                        xfm4[:, dc, 128 * tc:128 * (tc + 1)],
                        arena[:, A6_WR + 8 * dc:A6_WR + 8 * (dc + 1)],
                        start=(dc == 0), stop=(dc == 7)),
                        inc=(t_mm, 1) if dc == 7 else None)
                tv = t_mm.n
                kb.wait("vector", t_mm, tv)
                negmax = scr[:, 2102:2103]
                vm = kb.op("vector", lambda E, wb=wb, negmax=negmax: E.reduce_max(
                    out=negmax, in_=wb[:, 0:8], axis=AX, negate=True),
                    inc=(v_ev, 1))
                kb.wait("scalar", v_ev, vm)
                sb = scr[:, C_SBG + 8 * tc:C_SBG + 8 * (tc + 1)]
                ae = kb.op("scalar", lambda E, wb=wb, sb=sb, negmax=negmax:
                           E.activation(out=sb, in_=wb[:, 0:8], func=AF.Exp,
                                        bias=negmax, accum_out=sden6),
                           inc=(a_ev, 1))
                WB.release(j, a_ev, ae)
                kb.wait("vector", a_ev, ae)
                vr = kb.op("vector", lambda E: E.reciprocal(
                    out=rden6, in_=sden6), inc=(v_ev, 1))
                kb.wait("scalar", v_ev, vr)
                kb.op("scalar", lambda E, sb=sb: E.activation(
                    out=sb, in_=sb, func=AF.Copy, scale=rden6), inc=(a_ev, 1))
            a_probs6 = a_ev.n
            if DBG:
                kb.wait("vector", a_ev, a_probs6)
                kb.op("vector", lambda E: E.tensor_copy(
                    out=scr[:, 2816:2944], in_=scr[:, C_SBG:C_SBG + 128]),
                    inc=(v_ev, 1))

            vah = kb.op("vector", lambda E: E.memset(
                arena[32:64, A6_AH:A6_AH + 2048], 1.0), inc=(v_ev, 1))
            kb.wait("scalar", v_ev, vah)
            for ch in range(4):
                j, wb = WB.acquire()
                for dc in range(8):
                    kb.op("tensor", lambda E, wb=wb, dc=dc, ch=ch: E.matmul(
                        wb[0:50, 0:512],
                        arena[:, A6_W1T + 50 * dc:A6_W1T + 50 * (dc + 1)],
                        xfm4[:, dc, 512 * ch:512 * (ch + 1)],
                        start=(dc == 0), stop=(dc == 7)),
                        inc=(t_mm, 1) if dc == 7 else None)
                tv = t_mm.n
                kb.wait("scalar", t_mm, tv)
                dst = arena[0:50, A6_AH + 512 * ch:A6_AH + 512 * (ch + 1)]
                ah = kb.op("scalar", lambda E, wb=wb, dst=dst: E.activation(
                    out=dst, in_=wb[0:50, 0:512], func=AF.Gelu_apprx_tanh,
                    bias=scr[0:50, 2576:2577]), inc=(a_ev, 1))
                WB.release(j, a_ev, ah)
            a_ah = a_ev.n

            kb.wait("tensor", a_ev, a_ah)
            lg = scr[:, C6_LG:C6_LG + 6]
            m6 = scr[:, C6_LG + 6:C6_LG + 7]
            m6c = scr[:, C6_LG + 7:C6_LG + 8]
            tmp6 = scr[:, C6_LG + 8:C6_LG + 14]
            for tc in range(16):
                j, wb = WB.acquire()
                tv = kb.op("tensor", lambda E, wb=wb, tc=tc: E.matmul(
                    wb[:, 0:6],
                    arena[0:51, A6_AH + 128 * tc:A6_AH + 128 * (tc + 1)],
                    arena[0:51, A6_W2A:A6_W2A + 6],
                    start=True, stop=True), inc=(t_mm, 1))
                kb.wait("vector", t_mm, tv)
                vc = kb.op("vector", lambda E, wb=wb: E.tensor_scalar(
                    out=lg, in0=wb[:, 0:6], scalar1=30.0, scalar2=-30.0,
                    op0=ALU.min, op1=ALU.max), inc=(v_ev, 1))
                WB.release(j, v_ev, vc)
                kb.drain("vector")
                kb.op("vector", lambda E: E.reduce_max(out=m6, in_=lg, axis=AX),
                      inc=(v_ev, 1))
                kb.wait("scalar", v_ev, v_ev.n)
                kb.op("scalar", lambda E: E.activation(
                    out=m6c, in_=m6, func=AF.Copy), inc=(a_ev, 1))
                kb.wait("vector", a_ev, a_ev.n)
                kb.op("vector", lambda E: E.tensor_scalar(
                    out=tmp6, in0=lg, scalar1=m6c, scalar2=None, op0=ALU.is_ge))
                kb.drain("vector")
                kb.op("vector", lambda E: E.scalar_tensor_tensor(
                    out=tmp6, in0=tmp6, scalar=-1000.0, in1=scr[:, 2560:2566],
                    op0=ALU.mult, op1=ALU.add))
                kb.drain("vector")
                kb.op("vector", lambda E, tc=tc: E.tensor_reduce(
                    out=scr[:, C6_K0 + tc:C6_K0 + tc + 1], in_=tmp6,
                    axis=AX, op=ALU.min), inc=(v_ev, 1))
            kb.wait("scalar", v_ev, v_ev.n)
            ak0 = kb.op("scalar", lambda E: E.activation(
                out=scr[:, C6_K0C:C6_K0C + 16], in_=scr[:, C6_K0:C6_K0 + 16],
                func=AF.Copy), inc=(a_ev, 1))

            kb.wait("vector", a_ev, a_probs6)
            V6 = scr[:, C_WBG:C_WBG + 128]
            kb.drain("vector")
            kb.op("vector", lambda E: E.tensor_copy(
                out=V6, in_=scr[:, C_SBG:C_SBG + 128]))
            CE = [(0, 1), (2, 3), (4, 5), (6, 7), (0, 2), (1, 3), (4, 6),
                  (5, 7), (1, 2), (5, 6), (0, 4), (3, 7), (1, 5), (2, 6),
                  (3, 5), (2, 4), (1, 2), (3, 4), (5, 6)]
            t8 = scr[:, 2644:2660]
            for (i, jj) in CE:
                Vi = scr[:, C_WBG + i:C_WBG + i + 121:8]
                Vj = scr[:, C_WBG + jj:C_WBG + jj + 121:8]
                kb.drain("vector")
                kb.op("vector", lambda E, Vi=Vi, Vj=Vj: E.tensor_tensor(
                    out=t8, in0=Vi, in1=Vj, op=ALU.min))
                kb.drain("vector")
                kb.op("vector", lambda E, Vi=Vi, Vj=Vj: E.tensor_tensor(
                    out=Vi, in0=Vi, in1=Vj, op=ALU.max))
                kb.drain("vector")
                kb.op("vector", lambda E, Vj=Vj: E.tensor_copy(
                    out=Vj, in_=t8))
            vso = kb.op("vector", lambda E: E.engine_nop(), inc=(v_ev, 1))

            kb.wait("vector", a_ev, ak0)
            tau = scr[:, 2660:2661]
            tauc = scr[:, 2661:2662]
            tmp8b = scr[:, 2662:2670]
            mk8 = scr[:, 2670:2678]
            for tc in range(16):
                kb.drain("vector")
                kb.op("vector", lambda E, tc=tc: E.tensor_scalar(
                    out=tmp8b, in0=scr[:, 2568:2576],
                    scalar1=scr[:, C6_K0C + tc:C6_K0C + tc + 1], scalar2=None,
                    op0=ALU.is_equal))
                kb.drain("vector")
                kb.op("vector", lambda E, tc=tc: E.tensor_mul(
                    out=tmp8b, in0=tmp8b,
                    in1=scr[:, C_WBG + 8 * tc:C_WBG + 8 * (tc + 1)]))
                kb.drain("vector")
                vt = kb.op("vector", lambda E: E.reduce_sum(
                    out=tau, in_=tmp8b, axis=AX), inc=(v_ev, 1))
                kb.wait("scalar", v_ev, vt)
                at = kb.op("scalar", lambda E: E.activation(
                    out=tauc, in_=tau, func=AF.Copy), inc=(a_ev, 1))
                kb.wait("vector", a_ev, at)
                sb = scr[:, C_SBG + 8 * tc:C_SBG + 8 * (tc + 1)]
                kb.op("vector", lambda E, sb=sb: E.tensor_scalar(
                    out=mk8, in0=sb, scalar1=tauc, scalar2=None, op0=ALU.is_ge))
                kb.drain("vector")
                kb.op("vector", lambda E, sb=sb, tc=tc: E.tensor_mul(
                    out=scr[:, C_WSC + 8 * tc:C_WSC + 8 * (tc + 1)],
                    in0=sb, in1=mk8), inc=(v_ev, 1) if tc == 15 else None)
            vsc = v_ev.n
            kb.wait("scalar", v_ev, vsc)
            asc6 = kb.op("scalar", lambda E: E.activation(
                out=scr[:, C_SBG:C_SBG + 128], in_=scr[:, C_WSC:C_WSC + 128],
                func=AF.Copy), inc=(a_ev, 1))

            ICG = [list(range(0, 7)), list(range(7, 14)), list(range(14, 21))]
            gu_free = [None] * 4
            gu_i = [0]
            p6_g_last = [g_ld.n]

            def load_gu(wt, e, icg):
                sl = gu_i[0] % 4
                gu_i[0] += 1
                kb.wait("gpsimd", g_ld, p6_g_last[0])
                if gu_free[sl] is not None:
                    kb.wait("gpsimd", *gu_free[sl])
                src3 = wt[e, :, 128 * icg:128 * (icg + 1)].rearrange(
                    "(dc p) c -> p dc c", p=128)
                dst3 = arena[:, A6_GU + 1024 * sl:A6_GU + 1024 * (sl + 1)].rearrange(
                    "p (dc c) -> p dc c", dc=8)
                v = kb.op("gpsimd", lambda E, a=dst3, b=src3: E.dma_start(
                    out=a, in_=b), inc=(g_ld, 16))
                p6_g_last[0] = v
                return sl, v

            DWS = [43008, 30720]

            def load_down(e, g, gate_after, slot):
                base = DWS[slot]
                kb.wait("gpsimd", g_ld, p6_g_last[0])
                if gate_after is not None:
                    kb.wait("gpsimd", *gate_after)
                v = None
                for i_loc, icg in enumerate(ICG[g]):
                    ssrc = dwT[e, 128 * icg:128 * (icg + 1), :]
                    dst = arena[:, base + 1024 * i_loc:base + 1024 * (i_loc + 1)]
                    v = kb.op("gpsimd", lambda E, a=dst, b=ssrc: E.dma_start(
                        out=a, in_=b), inc=(g_ld, 16))
                p6_g_last[0] = v
                return v

            kb.wait("tensor", a_ev, asc6)
            xps6 = [zz, wk23]
            xfree6 = [None, None]
            dwn_guard = [None, None]
            gidx = 0
            for e in range(8):
                for g in range(3):
                    slot = 0
                    dv = load_down(e, g, dwn_guard[0], 0)
                    a_last = None
                    st_hist = []
                    for i_loc, icg in enumerate(ICG[g]):
                        slg, gv = load_gu(gwT, e, icg)
                        slu, uv = load_gu(uwT, e, icg)
                        kb.wait("tensor", g_ld, uv)
                        tvg = tvu = None
                        for tq in range(4):
                            jg, wbg_ = WB.acquire()
                            for dc in range(8):
                                kb.op("tensor", lambda E, wbg_=wbg_, dc=dc,
                                      tq=tq, slg=slg: E.matmul(
                                          wbg_[:, 0:512],
                                          arena[:, A6_GU + 1024 * slg + 128 * dc:
                                                A6_GU + 1024 * slg + 128 * (dc + 1)],
                                          xfm4[:, dc, 512 * tq:512 * (tq + 1)],
                                          start=(dc == 0), stop=(dc == 7)),
                                      inc=(t_mm, 1) if dc == 7 else None)
                            tvg = t_mm.n
                            ju, wbu = WB.acquire()
                            for dc in range(8):
                                kb.op("tensor", lambda E, wbu=wbu, dc=dc,
                                      tq=tq, slu=slu: E.matmul(
                                          wbu[:, 0:512],
                                          arena[:, A6_GU + 1024 * slu + 128 * dc:
                                                A6_GU + 1024 * slu + 128 * (dc + 1)],
                                          xfm4[:, dc, 512 * tq:512 * (tq + 1)],
                                          start=(dc == 0), stop=(dc == 7)),
                                      inc=(t_mm, 1) if dc == 7 else None)
                            tvu = t_mm.n
                            kb.wait("scalar", t_mm, tvg)
                            ssl = A6_ST + 512 * (tq % 2)
                            if len(st_hist) >= 2:
                                kb.wait("scalar", v_ev, st_hist[-2])
                            ag = kb.op("scalar", lambda E, wbg_=wbg_, ssl=ssl:
                                       E.activation(out=arena[:, ssl:ssl + 512],
                                                    in_=wbg_[:, 0:512],
                                                    func=AF.Silu), inc=(a_ev, 1))
                            WB.release(jg, a_ev, ag)
                            kb.wait("vector", a_ev, ag)
                            kb.wait("vector", t_mm, tvu)
                            dsta = arena[:, A6_A + 2048 * i_loc + 512 * tq:
                                         A6_A + 2048 * i_loc + 512 * (tq + 1)]
                            a_last = kb.op("vector", lambda E, wbu=wbu, ssl=ssl,
                                           dsta=dsta: E.tensor_mul(
                                               out=dsta, in0=arena[:, ssl:ssl + 512],
                                               in1=wbu[:, 0:512]), inc=(v_ev, 1))
                            WB.release(ju, v_ev, a_last)
                            st_hist.append(a_last)
                        gu_free[slg] = (t_mm, tvg)
                        gu_free[slu] = (t_mm, tvu)
                    kb.wait("tensor", v_ev, a_last)
                    kb.wait("tensor", g_ld, dv)
                    for tc in range(16):
                        ps = xps6[tc % 2]
                        if xfree6[tc % 2] is not None:
                            kb.wait("tensor", v_ev, xfree6[tc % 2])
                        for i_loc in range(7):
                            lhs = arena[:, A6_A + 2048 * i_loc + 128 * tc:
                                        A6_A + 2048 * i_loc + 128 * (tc + 1)]
                            for half in range(2):
                                kb.op("tensor", lambda E, ps=ps, lhs=lhs,
                                      i_loc=i_loc, half=half, slot=slot: E.matmul(
                                          ps[:, 512 * half:512 * (half + 1)], lhs,
                                          arena[:, DWS[slot] + 1024 * i_loc + 512 * half:
                                                DWS[slot] + 1024 * i_loc + 512 * (half + 1)],
                                          start=(i_loc == 0), stop=(i_loc == 6)),
                                      inc=(t_mm, 1) if (i_loc == 6 and half == 1)
                                      else None)
                        tv = t_mm.n
                        kb.wait("vector", t_mm, tv)
                        wcol = scr[:, C_SBG + 8 * tc + e:C_SBG + 8 * tc + e + 1]
                        d0 = x_tm[:, D * tc:D * tc + 512]
                        d1 = x_tm[:, D * tc + 512:D * (tc + 1)]
                        kb.op("vector", lambda E, ps=ps, wcol=wcol, d0=d0:
                              E.scalar_tensor_tensor(
                                  out=d0, in0=ps[:, 0:512], scalar=wcol,
                                  in1=d0, op0=ALU.mult, op1=ALU.add))
                        xv = kb.op("vector", lambda E, ps=ps, wcol=wcol, d1=d1:
                                   E.scalar_tensor_tensor(
                                       out=d1, in0=ps[:, 512:1024], scalar=wcol,
                                       in1=d1, op0=ALU.mult, op1=ALU.add),
                                   inc=(v_ev, 1))
                        xfree6[tc % 2] = xv
                    WB.free_at[2] = (v_ev, xv)
                    WB.free_at[3] = (v_ev, xv)
                    dwn_guard[0] = (t_mm, t_mm.n)
                    gidx += 1

            if DBG:
                kb.wait("sync", v_ev, v_ev.n)
                kb.wait("sync", a_ev, a_ev.n)
                kb.op("sync", lambda E: E.dma_start(
                    out=dbg_scr6[:, 0:512], in_=scr[:, 0:512]), inc=(st, 16))
                kb.op("sync", lambda E: E.dma_start(
                    out=dbg_scr6[:, 512:1024], in_=scr[:, 2304:2816]), inc=(st, 16))
                kb.op("sync", lambda E: E.dma_start(
                    out=dbg_xtm[:, 0:128], in_=scr[:, 2816:2944]), inc=(st, 16))
                kb.op("sync", lambda E: E.dma_start(
                    out=dbg_kp[:, 0:64], in_=arena[:, A6_WR:A6_WR + 64]),
                    inc=(st, 16))

            layernorm_inplace()

        if DBG:
            kb.wait("sync", v_ev, v_ev.n)
            kb.op("sync", lambda E: E.dma_start(out=dbg_xfm, in_=arena[:, A_XFM:A_XFM + 32768]), inc=(st, 16))
            kb.op("sync", lambda E: E.dma_start(out=dbg_q2, in_=arena[:, A_Q2:A_Q2 + 2048]), inc=(st, 16))
            kb.op("sync", lambda E: E.dma_start(out=dbg_k2, in_=arena[:, A_K2:A_K2 + 4096]), inc=(st, 16))
            kb.op("sync", lambda E: E.dma_start(out=dbg_v1, in_=arena[:, A_V1:A_V1 + 4160]), inc=(st, 16))
            kb.op("sync", lambda E: E.dma_start(out=dbg_z2, in_=arena[:, A_Z2:A_Z2 + 2048]), inc=(st, 16))

        kb.wait("sync", v_ev, v_ev.n)
        out_ap = out.rearrange("(c p) d -> p c d", p=128)
        kb.op("sync", lambda E: E.dma_start(out=out_ap, in_=x_tm_ap), inc=(st, 16))
        kb.wait("sync", st, st.n)

        kb.finalize(block)
    return nc



_BF = ml_dtypes.bfloat16


def kernel(**inputs):
    x = np.asarray(inputs["L0_attn_x"])
    mask_inds = np.asarray(inputs["L0_attn_mask_inds"]).astype(np.int32)
    wq = np.asarray(inputs["L0_attn_Wq"])
    wk = np.asarray(inputs["L0_attn_Wk"])
    wv = np.asarray(inputs["L0_attn_Wv"])
    wo = np.asarray(inputs["L0_attn_Wo"])

    wqT = np.ascontiguousarray(wq.T)
    wkT = np.ascontiguousarray(wk.T)
    wvT = np.ascontiguousarray(wv.T)
    woT = np.ascontiguousarray(wo.T)

    ident = np.eye(128, dtype=np.float32).astype(_BF)
    ident32 = np.eye(128, dtype=np.float32)
    e128 = (np.arange(128)[:, None] // 16 == np.arange(128)[None, :] // 16
            ).astype(np.float32)
    ones32 = np.ones((1, 128), np.float32)
    ones_row = np.ones((1, 128), np.float32).astype(_BF)
    fconst = np.zeros((128, 8), np.float32)
    fconst[:, 0] = 1e-5
    fconst[:, 1] = float(np.log(1.0 / 16.0))

    per = 100000 // 8
    padded = 128 * PAIR_COLS

    in_maps = []
    for c in range(8):
        b, h = c // 2, c % 2
        own = np.ascontiguousarray(x[b, 2048 * h:2048 * (h + 1)])
        other = np.ascontiguousarray(x[b, 2048 * (1 - h):2048 * (2 - h)])
        xb = np.concatenate([own, other], 0)
        pq = mask_inds[0, per * c:per * (c + 1)]
        pk = mask_inds[1, per * c:per * (c + 1)]
        pq = np.concatenate([pq, np.full(padded - per, pq[0], np.int32)])
        pk = np.concatenate([pk, np.full(padded - per, pk[0], np.int32)])
        mthr = np.full((128, 1), 0.0 if h == 0 else 1e9, np.float32)
        in_maps.append({
            "x_own": own,
            "x_batch": np.ascontiguousarray(xb),
            "pairs_q": np.ascontiguousarray(pq.reshape(128, PAIR_COLS)),
            "pairs_k": np.ascontiguousarray(pk.reshape(128, PAIR_COLS)),
            "mask_thr": mthr,
            "ident_c": ident,
            "ones_row_c": ones_row,
            "fconst_c": fconst,
            "wqT": wqT, "wkT": wkT, "wvT": wvT, "woT": woT,
            "ident32_c": ident32, "e128_c": e128, "ones32_c": ones32,
            "wr_c": np.ascontiguousarray(inputs["L0_ffn_Wr"]),
            "w1_c": np.ascontiguousarray(inputs["L0_ffn_W1"]),
            "b1_c": np.ascontiguousarray(
                np.asarray(inputs["L0_ffn_b1"]).reshape(8, 32, 128)
                .transpose(0, 2, 1)),
            "w2_c": np.ascontiguousarray(inputs["L0_ffn_W2"]),
            "b2_c": np.ascontiguousarray(inputs["L0_ffn_b2"]),
            "l1wq": np.ascontiguousarray(inputs["L1_attn_Wq"]),
            "l1wk": np.ascontiguousarray(inputs["L1_attn_Wk"]),
            "l1wv": np.ascontiguousarray(inputs["L1_attn_Wv"]),
            "l1wo": np.ascontiguousarray(inputs["L1_attn_Wo"]),
            "l1bq": np.ascontiguousarray(
                np.asarray(inputs["L1_attn_bq"]).reshape(8, 128).T),
            "l1bk": np.ascontiguousarray(
                np.asarray(inputs["L1_attn_bk"]).reshape(8, 128).T),
            "l1bv_bc": np.ascontiguousarray(np.broadcast_to(
                np.asarray(inputs["L1_attn_bv"]), (128, 1024))),
            "l1bo_bc": np.ascontiguousarray(np.broadcast_to(
                np.asarray(inputs["L1_attn_bo"]), (128, 1024))),
            "wr2_c": np.ascontiguousarray(np.asarray(inputs["L1_ffn_router_w"]).T),
            "aw1T": np.ascontiguousarray(np.asarray(inputs["L1_ffn_actor_w1"]).T),
            "aw2A": np.ascontiguousarray(np.vstack(
                [np.asarray(inputs["L1_ffn_actor_w2"]).T,
                 np.asarray(inputs["L1_ffn_actor_b2"]).reshape(1, 6),
                 np.zeros((77, 6), np.float32)])),
            "ab1p": np.ascontiguousarray(np.pad(
                np.asarray(inputs["L1_ffn_actor_b1"]),
                (0, 78)).reshape(128, 1)),
            "jrow_c": np.ascontiguousarray(np.broadcast_to(np.concatenate(
                [np.arange(6) + 1000.0, np.zeros(2),
                 np.arange(8), np.zeros(0)]).astype(np.float32), (128, 16))),
            "gwT": np.ascontiguousarray(
                np.asarray(inputs["L1_ffn_gate_w"]).transpose(0, 2, 1)),
            "uwT": np.ascontiguousarray(
                np.asarray(inputs["L1_ffn_up_w"]).transpose(0, 2, 1)),
            "dwT": np.ascontiguousarray(
                np.asarray(inputs["L1_ffn_down_w"]).transpose(0, 2, 1)),
            "l1proj": np.ascontiguousarray(np.vstack(
                [np.asarray(inputs["L1_attn_proj"]),
                 np.asarray(inputs["L1_attn_proj"])])),
        })

    nc = build_program()
    res = run_bass_kernel_spmd(nc, in_maps, core_ids=list(range(8)))
    outs = [r["out"] for r in res.results]
    y = np.zeros((4, 4096, 1024), np.float32)
    for c in range(8):
        b, h = c // 2, c % 2
        y[b, 2048 * h:2048 * (h + 1)] = outs[c]
    return y


# version 12
# speedup vs baseline: 1.0405x; 1.0405x over previous
import contextlib
import numpy as np
import ml_dtypes
import concourse.bass as bass
import concourse.mybir as mybir
from concourse.bass import IndirectOffsetOnAxis
from concourse.bass_utils import run_bass_kernel_spmd

F32 = mybir.dt.float32
BF16 = mybir.dt.bfloat16
I32 = mybir.dt.int32
AF = mybir.ActivationFunctionType
ALU = mybir.AluOpType
AX = mybir.AxisListType.X

STAGE = 4

D = 1024
T = 2048
S = 4096
PAIR_COLS = 98

A_XFM = 0
A_XB = 32768
A_K2 = 32768
A_Q2 = 36864
A_V1 = 38912
A_Z2 = 43072
A_EB = 45120
A_WQ = 46144
A_WO = 52288
A_RC = 54336
A_MS = 55360
A_ST = 55360
ARENA_N = 64032

C_DUMP = 0
C_ZA = 1024
C_ZB = 1536
C_SM = 2048


class Sem:
    def __init__(self, h):
        self.h = h
        self.n = 0

    def bump(self, k):
        self.n += k
        return self.n


class KB:
    ENGS = ("sync", "scalar", "vector", "tensor", "gpsimd")

    def __init__(self):
        self.streams = {e: [] for e in self.ENGS}
        self.waited = {e: {} for e in self.ENGS}

    def drain(self, eng):
        self.streams[eng].append(lambda E: E.drain())

    def wait(self, eng, sem, val):
        if val is None:
            return
        key = id(sem)
        if self.waited[eng].get(key, -1) >= val:
            return
        self.waited[eng][key] = val
        self.streams[eng].append(lambda E, s=sem.h, v=val: E.wait_ge(s, v))

    def op(self, eng, fn, inc=None):
        if inc is None:
            self.streams[eng].append(fn)
            return None
        sem, k = inc
        v = sem.bump(k)
        self.streams[eng].append(
            lambda E, f=fn, s=sem.h, kk=k: f(E).then_inc(s, kk))
        return v

    def finalize(self, block):
        def mk(fns):
            def run(E):
                for f in fns:
                    f(E)
            return run
        block.sync(mk(self.streams["sync"]))
        block.scalar(mk(self.streams["scalar"]))
        block.vector(mk(self.streams["vector"]))
        block.tensor(mk(self.streams["tensor"]))
        block.gpsimd(mk(self.streams["gpsimd"]))


class Banks:

    def __init__(self, kb, bank_aps):
        self.kb = kb
        self.banks = bank_aps
        self.free_at = [None] * len(bank_aps)
        self.i = 0

    def acquire(self):
        j = self.i % len(self.banks)
        self.i += 1
        if self.free_at[j] is not None:
            self.kb.wait("tensor", *self.free_at[j])
        return j, self.banks[j]

    def release(self, j, sem, val):
        self.free_at[j] = (sem, val)


def build_program(stage=STAGE):
    nc = bass.Bass("TRN2", target_bir_lowering=False, debug=False)
    dt = nc.dram_tensor

    x_own = dt("x_own", [T, D], F32, kind="ExternalInput").ap()
    x_batch = dt("x_batch", [S, D], F32, kind="ExternalInput").ap()
    pairs_q = dt("pairs_q", [128, PAIR_COLS], I32, kind="ExternalInput").ap()
    pairs_k = dt("pairs_k", [128, PAIR_COLS], I32, kind="ExternalInput").ap()
    mask_thr = dt("mask_thr", [128, 1], F32, kind="ExternalInput").ap()
    ident_c = dt("ident_c", [128, 128], BF16, kind="ExternalInput").ap()
    ones_row_c = dt("ones_row_c", [1, 128], BF16, kind="ExternalInput").ap()
    fconst_c = dt("fconst_c", [128, 8], F32, kind="ExternalInput").ap()
    wqT = dt("wqT", [D, D], F32, kind="ExternalInput").ap()
    wkT = dt("wkT", [D, D], F32, kind="ExternalInput").ap()
    wvT = dt("wvT", [D, D], F32, kind="ExternalInput").ap()
    woT = dt("woT", [D, D], F32, kind="ExternalInput").ap()
    ident32_c = dt("ident32_c", [128, 128], F32, kind="ExternalInput").ap()
    ones32_c = dt("ones32_c", [1, 128], F32, kind="ExternalInput").ap()
    e128_c = dt("e128_c", [128, 128], F32, kind="ExternalInput").ap()
    wr_c = dt("wr_c", [D, 8], F32, kind="ExternalInput").ap()
    w1_c = dt("w1_c", [8, D, 4096], F32, kind="ExternalInput").ap()
    b1_c = dt("b1_c", [8, 128, 32], F32, kind="ExternalInput").ap()
    w2_c = dt("w2_c", [8, 4096, D], F32, kind="ExternalInput").ap()
    b2_c = dt("b2_c", [8, D], F32, kind="ExternalInput").ap()
    l1wq = dt("l1wq", [D, D], F32, kind="ExternalInput").ap()
    l1wk = dt("l1wk", [D, D], F32, kind="ExternalInput").ap()
    l1wv = dt("l1wv", [D, D], F32, kind="ExternalInput").ap()
    l1wo = dt("l1wo", [D, D], F32, kind="ExternalInput").ap()
    l1bq = dt("l1bq", [128, 8], F32, kind="ExternalInput").ap()
    l1bk = dt("l1bk", [128, 8], F32, kind="ExternalInput").ap()
    l1bv_bc = dt("l1bv_bc", [128, D], F32, kind="ExternalInput").ap()
    l1bo_bc = dt("l1bo_bc", [128, D], F32, kind="ExternalInput").ap()
    l1proj = dt("l1proj", [128, 256], F32, kind="ExternalInput").ap()
    wr2_c = dt("wr2_c", [D, 8], F32, kind="ExternalInput").ap()
    aw1T = dt("aw1T", [D, 50], F32, kind="ExternalInput").ap()
    aw2A = dt("aw2A", [128, 6], F32, kind="ExternalInput").ap()
    ab1p = dt("ab1p", [128, 1], F32, kind="ExternalInput").ap()
    jrow_c = dt("jrow_c", [128, 16], F32, kind="ExternalInput").ap()
    gwT = dt("gwT", [8, D, 2688], F32, kind="ExternalInput").ap()
    uwT = dt("uwT", [8, D, 2688], F32, kind="ExternalInput").ap()
    dwT = dt("dwT", [8, 2688, D], F32, kind="ExternalInput").ap()
    out = dt("out", [T, D], F32, kind="ExternalOutput").ap()
    DBG = False
    if DBG:
        dbg_xfm = dt("dbg_xfm", [128, 32768], BF16, kind="ExternalOutput").ap()
        dbg_q2 = dt("dbg_q2", [128, 2048], BF16, kind="ExternalOutput").ap()
        dbg_k2 = dt("dbg_k2", [128, 4096], BF16, kind="ExternalOutput").ap()
        dbg_v1 = dt("dbg_v1", [128, 4160], BF16, kind="ExternalOutput").ap()
        dbg_z2 = dt("dbg_z2", [128, 2048], BF16, kind="ExternalOutput").ap()
        dbg_xtm = dt("dbg_xtm", [128, 16384], F32, kind="ExternalOutput").ap()
        dbg_scr = dt("dbg_scr", [128, 1024], F32, kind="ExternalOutput").ap()
        dbg_xtm2 = dt("dbg_xtm2", [128, 16384], F32, kind="ExternalOutput").ap()
        dbg_kv = dt("dbg_kv", [128, 2080], BF16, kind="ExternalOutput").ap()
        dbg_kp = dt("dbg_kp", [128, 8192], BF16, kind="ExternalOutput").ap()
        dbg_kvp = dt("dbg_kvp", [4096, 65], F32, kind="ExternalOutput").ap()
        dbg_v5 = dt("dbg_v5", [128, 16640], BF16, kind="ExternalOutput").ap()
        dbg_scr6 = dt("dbg_scr6", [128, 1024], F32, kind="ExternalOutput").ap()
        dbg_z1 = dt("dbg_z1", [128, 16384], BF16, kind="ExternalOutput").ap()
        dbg_xtm3 = dt("dbg_xtm3", [128, 16384], F32, kind="ExternalOutput").ap()

    kv_ccin = dt("kv_ccin", [4096, 65], F32, kind="Internal").ap()
    kv_ccout = dt("kv_ccout", [4096, 65], F32, kind="Internal").ap()
    sc_ccin = dt("sc_ccin", [8, 2048], F32, kind="Internal").ap()
    sc_ccout = dt("sc_ccout", [64, 2048], F32, kind="Internal").ap()
    mci = dt("mask_cc_in", [2048 * 2048, 1], BF16, kind="Internal").ap()
    mco = dt("mask_cc_out", [2048 * 2048, 1], BF16, kind="Internal").ap()

    with contextlib.ExitStack() as es:
        ent = es.enter_context
        x_tm = ent(nc.sbuf_tensor("x_tm", [128, 16 * D], F32))
        scr = ent(nc.sbuf_tensor("scr", [128, 3328], F32))
        arena = ent(nc.sbuf_tensor("arena", [128, ARENA_N], BF16))
        ident = ent(nc.sbuf_tensor("ident", [128, 128], BF16))
        ones_row = ent(nc.sbuf_tensor("ones_row", [1, 128], BF16))
        pq_sb = ent(nc.sbuf_tensor("pq_sb", [128, PAIR_COLS], I32))
        pk_sb = ent(nc.sbuf_tensor("pk_sb", [128, PAIR_COLS], I32))
        pidx = ent(nc.sbuf_tensor("pidx", [128, PAIR_COLS], I32))
        ones_sc = ent(nc.sbuf_tensor("ones_sc", [128, 1], BF16))
        mthr_sb = ent(nc.sbuf_tensor("mthr_sb", [128, 1], F32))
        fconst = ent(nc.sbuf_tensor("fconst", [128, 8], F32))
        ident32 = ent(nc.sbuf_tensor("ident32", [128, 128], F32))
        ones32 = ent(nc.sbuf_tensor("ones32", [1, 128], F32))
        e128 = ent(nc.sbuf_tensor("e128", [128, 128], F32))


        wk0 = ent(nc.psum_tensor("wk0", [128, 512], F32))
        wk1 = ent(nc.psum_tensor("wk1", [128, 512], F32))
        wk23 = ent(nc.psum_tensor("wk23", [128, 1024], F32))
        zz = ent(nc.psum_tensor("zz", [128, 1024], F32))
        tpA = ent(nc.psum_tensor("tpA", [128, 1024], BF16))
        tpB = ent(nc.psum_tensor("tpB", [128, 1024], BF16))

        s_ld = Sem(ent(nc.semaphore("s_ld")))
        g_ld = Sem(ent(nc.semaphore("g_ld")))
        t_mm = Sem(ent(nc.semaphore("t_mm")))
        v_ev = Sem(ent(nc.semaphore("v_ev")))
        a_ev = Sem(ent(nc.semaphore("a_ev")))
        g_ms = Sem(ent(nc.semaphore("g_ms")))
        g_xb = Sem(ent(nc.semaphore("g_xb")))
        st = Sem(ent(nc.semaphore("st")))

        block = ent(nc.Block())
        kb = KB()
        WB = Banks(kb, [wk0[:], wk1[:], wk23[:, 0:512], wk23[:, 512:1024]])

        xfm = arena[:, A_XFM:A_XFM + 32768].rearrange("p (dc f) -> p dc f", dc=8)
        v1v = arena[:, A_V1:A_V1 + 4160].rearrange("p (tc s) -> p tc s", tc=32)

        x_own_ap = x_own.rearrange("(c p) d -> p c d", p=128)
        x_tm_ap = x_tm[:].rearrange("p (c d) -> p c d", c=16)
        kb.op("sync", lambda E: E.dma_start(out=x_tm_ap, in_=x_own_ap), inc=(s_ld, 16))
        kb.op("sync", lambda E: E.dma_start(out=ident[:], in_=ident_c), inc=(s_ld, 16))
        kb.op("sync", lambda E: E.dma_start(out=ones_row[:], in_=ones_row_c), inc=(s_ld, 16))
        kb.op("sync", lambda E: E.dma_start(out=pq_sb[:], in_=pairs_q), inc=(s_ld, 16))
        kb.op("sync", lambda E: E.dma_start(out=pk_sb[:], in_=pairs_k), inc=(s_ld, 16))
        kb.op("sync", lambda E: E.dma_start(out=mthr_sb[:], in_=mask_thr), inc=(s_ld, 16))
        kb.op("sync", lambda E: E.dma_start(out=fconst[:], in_=fconst_c), inc=(s_ld, 16))
        ld_base = s_ld.n

        xb_ap = x_batch.rearrange("(c p) d -> p c d", p=128)

        wslot = lambda w, sl: A_WQ + (w * 2 + sl) * 1024
        woslot = lambda sl: A_WO + sl * 1024
        wT = [wqT, wkT, wvT]
        w_ld_done = []
        w_free_at = [None, None]

        def emit_wload(hg):
            sl = hg % 2
            if w_free_at[sl] is not None:
                kb.wait("gpsimd", *w_free_at[sl])
            if w_ld_done:
                kb.wait("gpsimd", g_ld, w_ld_done[-1])
            for w in range(3):
                src3 = wT[w][:, 128 * hg:128 * (hg + 1)].rearrange(
                    "(dc p) c -> p dc c", p=128)
                dst3 = arena[:, wslot(w, sl):wslot(w, sl) + 1024].rearrange(
                    "p (dc c) -> p dc c", dc=8)
                kb.op("gpsimd", lambda E, a=dst3, b=src3: E.dma_start(out=a, in_=b),
                      inc=(g_ld, 16))
            dst = arena[:, woslot(sl):woslot(sl) + 1024]
            src = woT[128 * hg:128 * (hg + 1), :]
            kb.op("gpsimd", lambda E, a=dst, b=src: E.dma_start(out=a, in_=b),
                  inc=(g_ld, 16))
            w_ld_done.append(g_ld.n)

        emit_wload(0)
        emit_wload(1)

        kb.op("vector", lambda E: E.memset(arena[:, A_MS:A_MS + 2048], 0.0))
        kb.op("vector", lambda E: E.memset(ones_sc[:], 1.0))
        kb.wait("vector", s_ld, ld_base)
        v_idx = kb.op("vector", lambda E: E.scalar_tensor_tensor(
            out=pidx[:], in0=pk_sb[:], scalar=2048.0, in1=pq_sb[:],
            op0=ALU.mult, op1=ALU.add), inc=(v_ev, 1))
        mci3 = mci.rearrange("(r p f) one -> p r (f one)", p=128, f=2048)
        kb.wait("sync", v_ev, v_idx)
        zs_done = None
        for r in range(16):
            zs_done = kb.op(
                "sync", lambda E, r=r: E.dma_start(
                    out=mci3[:, r, :], in_=arena[:, A_MS:A_MS + 2048]),
                inc=(s_ld, 16))
        kb.wait("gpsimd", v_ev, v_idx)
        kb.wait("gpsimd", s_ld, zs_done)
        sc_done = None
        for j in range(PAIR_COLS):
            sc_done = kb.op(
                "gpsimd", lambda E, j=j: E.indirect_dma_start(
                    out=mci,
                    out_offset=IndirectOffsetOnAxis(ap=pidx[:, j:j + 1], axis=0),
                    in_=ones_sc[:], in_offset=None),
                inc=(g_ms, 16))
        kb.wait("gpsimd", g_ms, sc_done)
        ar_done = kb.op("gpsimd", lambda E: E.collective_compute(
            "AllReduce", ALU.add, ins=[mci], outs=[mco],
            replica_groups=[list(range(8))]), inc=(g_ms, 1))

        kb.wait("vector", s_ld, ld_base)
        v_x2 = None
        for tc in range(16):
            v_x2 = kb.op("vector", lambda E, tc=tc: E.tensor_scalar_mul(
                out=x_tm[:, D * tc:D * (tc + 1)],
                in0=x_tm[:, D * tc:D * (tc + 1)], scalar1=2.0),
                inc=(v_ev, 1) if tc == 15 else None)

        kb.wait("tensor", s_ld, ld_base)
        tp_ev = []
        xbch_done = []
        xbld = []

        def emit_xbload(j):
            slot = A_ST + 2048 * (j % 2)
            if j >= 2:
                kb.wait("gpsimd", t_mm, xbch_done[j - 2])
            kb.wait("gpsimd", g_xb, g_xb.n)
            v = kb.op("gpsimd", lambda E, j=j, slot=slot: E.dma_start(
                out=arena[:, slot:slot + 2048].rearrange(
                    "p (c d) -> p c d", c=2),
                in_=xb_ap[:, 2 * j:2 * (j + 1), :]), inc=(g_xb, 16))
            xbld.append(v)

        emit_xbload(0)
        emit_xbload(1)
        for j in range(16):
            slot = A_ST + 2048 * (j % 2)
            kb.wait("tensor", g_xb, xbld[j])
            for tt in range(2):
                tc = 2 * j + tt
                pt = [tpA, tpB][tc % 2]
                if tc >= 2:
                    kb.wait("tensor", v_ev, tp_ev[tc - 2])
                tmm = None
                for dc in range(8):
                    sap = arena[:, slot + 1024 * tt + 128 * dc:
                                slot + 1024 * tt + 128 * (dc + 1)]
                    tmm = kb.op("tensor",
                                lambda E, pt=pt, dc=dc, sap=sap: E.transpose(
                                    pt[:, 128 * dc:128 * (dc + 1)], sap, ident[:]),
                                inc=(t_mm, 1) if dc == 7 else None)
                kb.wait("vector", t_mm, tmm)
                ptv = pt[:].rearrange("p (dc f) -> p dc f", dc=8)
                ev = kb.op("vector", lambda E, tc=tc, ptv=ptv: E.tensor_copy(
                    out=xfm[:, :, 128 * tc:128 * (tc + 1)], in_=ptv),
                    inc=(v_ev, 1))
                tp_ev.append(ev)
            xbch_done.append(tmm)
            if j + 2 < 16:
                emit_xbload(j + 2)
        v_xfm = tp_ev[-1]

        v_ones = kb.op("vector", lambda E: E.memset(
            arena[:, A_V1:A_V1 + 4160], 1.0), inc=(v_ev, 1))

        mco2d = mco.rearrange("(r c) one -> r (c one)", c=2048)


        ms_state = {"prev_end": None, "prev_cons": None}

        def load_mask_batch(tqc):
            kb.wait("sync", g_ms, ar_done)
            kb.wait("sync", v_ev, v_xfm)
            if ms_state["prev_end"] is not None:
                kb.wait("sync", s_ld, ms_state["prev_end"])
            if ms_state["prev_cons"] is not None:
                kb.wait("sync", v_ev, ms_state["prev_cons"])
            v = None
            for kt in range(16):
                src = mco2d[128 * kt:128 * (kt + 1), 512 * tqc:512 * (tqc + 1)]
                dst = arena[:, A_MS + 512 * kt:A_MS + 512 * (kt + 1)]
                v = kb.op("sync", lambda E, a=dst, b=src: E.dma_start(
                    out=a, in_=b), inc=(s_ld, 16))
            ms_state["prev_end"] = v
            return v

        KTS = list(range(16, 32)) + list(range(16))

        z_guard = [None]

        def attn_hg(hg):
            sl = hg % 2
            kb.wait("tensor", g_ld, w_ld_done[hg])
            kb.wait("tensor", v_ev, v_xfm)
            kb.wait("tensor", v_ev, v_ones)

            def proj_fm(w, dst_base, col0, nch):
                last = None
                for ch in range(nch):
                    j, wb = WB.acquire()
                    for dc in range(8):
                        lhs = arena[:, wslot(w, sl) + 128 * dc:
                                    wslot(w, sl) + 128 * (dc + 1)]
                        rhs = xfm[:, dc, col0 + 512 * ch:col0 + 512 * (ch + 1)]
                        kb.op("tensor",
                              lambda E, wb=wb, lhs=lhs, rhs=rhs, dc=dc: E.matmul(
                                  wb[:, 0:512], lhs, rhs,
                                  start=(dc == 0), stop=(dc == 7)),
                              inc=(t_mm, 1) if dc == 7 else None)
                    tv = t_mm.n
                    kb.wait("vector", t_mm, tv)
                    dst = arena[:, dst_base + 512 * ch:dst_base + 512 * (ch + 1)]
                    last = kb.op("vector", lambda E, wb=wb, dst=dst: E.tensor_copy(
                        out=dst, in_=wb[:, 0:512]), inc=(v_ev, 1))
                    WB.release(j, v_ev, last)
                return last

            proj_fm(0, A_Q2, 0, 4)
            proj_fm(1, A_K2, 0, 8)
            v_done = None
            for ch in range(32):
                j, wb = WB.acquire()
                for dc in range(8):
                    lhs = xfm[:, dc, 128 * ch:128 * (ch + 1)]
                    rhs = arena[:, wslot(2, sl) + 128 * dc:
                                wslot(2, sl) + 128 * (dc + 1)]
                    kb.op("tensor",
                          lambda E, wb=wb, lhs=lhs, rhs=rhs, dc=dc: E.matmul(
                              wb[:, 0:128], lhs, rhs,
                              start=(dc == 0), stop=(dc == 7)),
                          inc=(t_mm, 1) if dc == 7 else None)
                tv = t_mm.n
                kb.wait("vector", t_mm, tv)
                kb.op("vector", lambda E, wb=wb, ch=ch: E.tensor_copy(
                    out=v1v[:, ch, 0:64],
                    in_=wb[:, 0:64]))
                v_done = kb.op("vector", lambda E, wb=wb, ch=ch: E.tensor_copy(
                    out=v1v[:, ch, 65:129],
                    in_=wb[:, 64:128]), inc=(v_ev, 1))
                WB.release(j, v_ev, v_done)
            kb.wait("tensor", v_ev, v_done)

            for tqc in range(4):
                ms_batch = load_mask_batch(tqc)
                if z_guard[0] is not None:
                    kb.wait("tensor", v_ev, z_guard[0])
                first = True
                for kt in KTS:
                    jA, wbA = WB.acquire()
                    jB, wbB = WB.acquire()
                    lhsA = arena[0:64, A_K2 + 128 * kt:A_K2 + 128 * (kt + 1)]
                    lhsB = arena[64:128, A_K2 + 128 * kt:A_K2 + 128 * (kt + 1)]
                    rhsA = arena[0:64, A_Q2 + 512 * tqc:A_Q2 + 512 * (tqc + 1)]
                    rhsB = arena[64:128, A_Q2 + 512 * tqc:A_Q2 + 512 * (tqc + 1)]
                    kb.op("tensor", lambda E, wbA=wbA, lhsA=lhsA, rhsA=rhsA:
                          E.matmul(wbA[:, 0:512], lhsA, rhsA, start=True, stop=True))
                    smm = kb.op("tensor", lambda E, wbB=wbB, lhsB=lhsB, rhsB=rhsB:
                                E.matmul(wbB[:, 0:512], lhsB, rhsB,
                                         start=True, stop=True), inc=(t_mm, 1))
                    kb.wait("scalar", t_mm, smm)
                    eA = arena[:, A_EB:A_EB + 512]
                    eB = arena[:, A_EB + 512:A_EB + 1024]
                    kb.op("scalar", lambda E, wbA=wbA, eA=eA: E.activation(
                        out=eA, in_=wbA[:, 0:512], func=AF.Exp, scale=0.125))
                    ex = kb.op("scalar", lambda E, wbB=wbB, eB=eB: E.activation(
                        out=eB, in_=wbB[:, 0:512], func=AF.Exp, scale=0.125),
                        inc=(a_ev, 1))
                    WB.release(jA, a_ev, ex)
                    WB.release(jB, a_ev, ex)
                    if kt < 16:
                        kb.wait("vector", a_ev, ex)
                        kb.wait("vector", s_ld, ms_batch)
                        mt = arena[:, A_MS + 512 * kt:A_MS + 512 * (kt + 1)]
                        kb.op("vector", lambda E, mt=mt, eA=eA: E.scalar_tensor_tensor(
                            out=eA, in0=mt, scalar=mthr_sb[:], in1=eA,
                            op0=ALU.is_le, op1=ALU.mult))
                        emask = kb.op("vector",
                                      lambda E, mt=mt, eB=eB: E.scalar_tensor_tensor(
                                          out=eB, in0=mt, scalar=mthr_sb[:], in1=eB,
                                          op0=ALU.is_le, op1=ALU.mult),
                                      inc=(v_ev, 1))
                        ms_state["prev_cons"] = emask
                        kb.wait("tensor", v_ev, emask)
                    else:
                        kb.wait("tensor", a_ev, ex)
                    lhsvA = v1v[:, kt, 0:65]
                    lhsvB = v1v[:, kt, 65:130]
                    kb.op("tensor", lambda E, lhsvA=lhsvA, eA=eA, first=first,
                          last=(kt == KTS[-1]):
                          E.matmul(zz[0:65, 0:512], lhsvA, eA,
                                   start=first, stop=last))
                    zmm = kb.op("tensor", lambda E, lhsvB=lhsvB, eB=eB, first=first,
                                last=(kt == KTS[-1]):
                                E.matmul(zz[0:65, 512:1024], lhsvB, eB,
                                         start=first, stop=last),
                                inc=(t_mm, 1) if kt == KTS[-1] else None)
                    first = False
                zdone = t_mm.n
                kb.wait("vector", t_mm, zdone)
                czA = 1024 - 1024 * (tqc % 2)
                czB = czA + 512
                kb.op("vector", lambda E, czA=czA: E.tensor_copy(
                    out=scr[0:65, czA:czA + 512], in_=zz[0:65, 0:512]))
                zc = kb.op("vector", lambda E, czB=czB: E.tensor_copy(
                    out=scr[0:65, czB:czB + 512], in_=zz[0:65, 512:1024]),
                    inc=(v_ev, 1))
                z_guard[0] = zc

                def emit_zdiv(tqd):
                    cA = 1024 - 1024 * (tqd % 2)
                    cB = cA + 512
                    kb.op("vector", lambda E, cA=cA: E.reciprocal(
                        out=scr[64:65, cA:cA + 512], in_=scr[64:65, cA:cA + 512]))
                    kb.op("vector", lambda E, cB=cB: E.reciprocal(
                        out=scr[64:65, cB:cB + 512], in_=scr[64:65, cB:cB + 512]))
                    kb.op("vector", lambda E, cA=cA: E.tensor_copy(
                        out=arena[0:1, A_RC:A_RC + 512],
                        in_=scr[64:65, cA:cA + 512]))
                    rc = kb.op("vector", lambda E, cB=cB: E.tensor_copy(
                        out=arena[0:1, A_RC + 512:A_RC + 1024],
                        in_=scr[64:65, cB:cB + 512]), inc=(v_ev, 1))
                    kb.wait("tensor", v_ev, rc)
                    jbA, bcA = WB.acquire()
                    jbB, bcB = WB.acquire()
                    kb.op("tensor", lambda E, bcA=bcA: E.matmul(
                        bcA[0:64, 0:512], ones_row[0:1, 0:64],
                        arena[0:1, A_RC:A_RC + 512], start=True, stop=True))
                    bc = kb.op("tensor", lambda E, bcB=bcB: E.matmul(
                        bcB[0:64, 0:512], ones_row[0:1, 0:64],
                        arena[0:1, A_RC + 512:A_RC + 1024],
                        start=True, stop=True), inc=(t_mm, 1))
                    kb.wait("vector", t_mm, bc)
                    zA2 = arena[0:64, A_Z2 + 512 * tqd:A_Z2 + 512 * (tqd + 1)]
                    zB2 = arena[64:128, A_Z2 + 512 * tqd:A_Z2 + 512 * (tqd + 1)]
                    kb.op("vector", lambda E, zA2=zA2, bcA=bcA, cA=cA:
                          E.tensor_mul(out=zA2, in0=scr[0:64, cA:cA + 512],
                                       in1=bcA[0:64, 0:512]))
                    z2w = kb.op("vector", lambda E, zB2=zB2, bcB=bcB, cB=cB:
                                E.tensor_mul(out=zB2, in0=scr[0:64, cB:cB + 512],
                                             in1=bcB[0:64, 0:512]), inc=(v_ev, 1))
                    WB.release(jbA, v_ev, z2w)
                    WB.release(jbB, v_ev, z2w)
                    return z2w

                if tqc > 0:
                    emit_zdiv(tqc - 1)
                if tqc == 3:
                    last_z2w = emit_zdiv(3)

            kb.wait("tensor", v_ev, last_z2w)
            kb.wait("tensor", v_ev, v_x2)
            for tc in range(16):
                j0, wb0 = WB.acquire()
                j1, wb1 = WB.acquire()
                lhs = arena[:, A_Z2 + 128 * tc:A_Z2 + 128 * (tc + 1)]
                rhs0 = arena[:, woslot(sl):woslot(sl) + 512]
                rhs1 = arena[:, woslot(sl) + 512:woslot(sl) + 1024]
                kb.op("tensor", lambda E, wb0=wb0, lhs=lhs, rhs0=rhs0: E.matmul(
                    wb0[:, 0:512], lhs, rhs0, start=True, stop=True))
                omm = kb.op("tensor", lambda E, wb1=wb1, lhs=lhs, rhs1=rhs1: E.matmul(
                    wb1[:, 0:512], lhs, rhs1, start=True, stop=True), inc=(t_mm, 1))
                kb.wait("vector", t_mm, omm)
                d0 = x_tm[:, D * tc:D * tc + 512]
                d1 = x_tm[:, D * tc + 512:D * (tc + 1)]
                kb.op("vector", lambda E, wb0=wb0, d0=d0: E.tensor_add(
                    out=d0, in0=wb0[:, 0:512], in1=d0))
                oev = kb.op("vector", lambda E, wb1=wb1, d1=d1: E.tensor_add(
                    out=d1, in0=wb1[:, 0:512], in1=d1), inc=(v_ev, 1))
                WB.release(j0, v_ev, oev)
                WB.release(j1, v_ev, oev)
            w_free_at[sl] = (v_ev, v_ev.n)
            if hg + 2 < 8:
                emit_wload(hg + 2)

        for hg in range(8):
            attn_hg(hg)

        if DBG:
            kb.wait("sync", v_ev, v_ev.n)
            vpre = kb.op("sync", lambda E: E.dma_start(out=dbg_xtm, in_=x_tm[:]),
                         inc=(st, 16))
            kb.wait("vector", st, vpre)

        def layernorm_inplace():
            for tc in range(16):
                xs = x_tm[:, D * tc:D * (tc + 1)]
                negm = scr[:, C_SM:C_SM + 1]
                ssq = scr[:, C_SM + 1:C_SM + 2]
                rstd = scr[:, C_SM + 2:C_SM + 3]
                nsum = scr[:, C_SM + 3:C_SM + 4]
                vred = kb.op("vector", lambda E, xs=xs, nsum=nsum: E.reduce_sum(
                    out=nsum, in_=xs, axis=AX, negate=True), inc=(v_ev, 1))
                kb.wait("scalar", v_ev, vred)
                asc = kb.op("scalar", lambda E, nsum=nsum, negm=negm: E.activation(
                    out=negm, in_=nsum, func=AF.Copy, scale=1.0 / D),
                    inc=(a_ev, 1))
                kb.wait("vector", a_ev, asc)
                vsub = kb.op("vector", lambda E, xs=xs, negm=negm: E.tensor_scalar_add(
                    out=xs, in0=xs, scalar1=negm), inc=(v_ev, 1))
                kb.wait("scalar", v_ev, vsub)
                kb.op("scalar", lambda E, xs=xs, ssq=ssq: E.activation(
                    out=scr[:, C_DUMP:C_DUMP + 1024], in_=xs, func=AF.Square,
                    accum_out=ssq))
                kb.op("scalar", lambda E, ssq=ssq, rstd=rstd: E.activation(
                    out=rstd, in_=ssq, func=AF.Ln, scale=1.0 / D,
                    bias=fconst[:, 0:1]))
                arstd = kb.op("scalar", lambda E, rstd=rstd: E.activation(
                    out=rstd, in_=rstd, func=AF.Exp, scale=-0.5), inc=(a_ev, 1))
                kb.wait("vector", a_ev, arstd)
                kb.op("vector", lambda E, xs=xs, rstd=rstd: E.tensor_scalar_mul(
                    out=xs, in0=xs, scalar1=rstd), inc=(v_ev, 1))

        layernorm_inplace()
        v_ln1 = v_ev.n

        if stage >= 2:
            v_ln1 = v_ev.n
            a_ln1 = a_ev.n
            A4_XFM2 = 0
            A4_H1 = 16384
            A4_W1 = 32768
            A4_W2 = 34816
            A4_WR = 51200
            A4_B2 = 51264
            A4_XC = 52288
            A4_CMP = 54336
            C_SBG = 0
            C_WBG = 128
            C_THR = 256
            C_BI = 264
            C_WSC = 272
            C_B1 = 400
            C_SF = 1024

            xfm2 = arena[:, A4_XFM2:A4_XFM2 + 16384].rearrange(
                "p (dc f) -> p dc f", dc=8)

            kb.op("sync", lambda E: E.dma_start(out=ident32[:], in_=ident32_c),
                  inc=(s_ld, 16))
            kb.op("sync", lambda E: E.dma_start(out=ones32[:], in_=ones32_c),
                  inc=(s_ld, 16))
            kb.op("sync", lambda E: E.dma_start(out=e128[:], in_=e128_c),
                  inc=(s_ld, 16))
            A4_B2 = 51264
            wr3 = wr_c.rearrange("(dc p) e -> p dc e", p=128)
            kb.op("sync", lambda E: E.dma_start(
                out=scr[:, 2944:3008].rearrange("p (dc e) -> p dc e", dc=8),
                in_=wr3), inc=(s_ld, 16))
            kb.wait("gpsimd", g_ld, g_ld.n)
            p4cg = kb.op("gpsimd", lambda E: E.dma_start(
                out=arena[0:8, A4_B2:A4_B2 + 1024], in_=b2_c), inc=(g_ld, 16))
            p4c = s_ld.n

            kb.wait("vector", v_ev, v_ln1)
            kb.wait("vector", a_ev, a_ln1)
            xc_ev = []
            for tc in range(16):
                slot = A4_XC + 1024 * (tc % 2)
                if tc >= 2:
                    kb.wait("vector", t_mm, xc_ev[tc - 2][0])
                vv = kb.op("vector", lambda E, tc=tc, slot=slot: E.tensor_copy(
                    out=arena[:, slot:slot + 1024],
                    in_=x_tm[:, D * tc:D * (tc + 1)]), inc=(v_ev, 1))
                kb.wait("tensor", v_ev, vv)
                pt = [tpA, tpB][tc % 2]
                if tc >= 2:
                    kb.wait("tensor", v_ev, xc_ev[tc - 2][1])
                tmm = None
                for dc in range(8):
                    sap = arena[:, slot + 128 * dc:slot + 128 * (dc + 1)]
                    tmm = kb.op("tensor", lambda E, pt=pt, dc=dc, sap=sap:
                                E.transpose(pt[:, 128 * dc:128 * (dc + 1)], sap,
                                            ident[:]),
                                inc=(t_mm, 1) if dc == 7 else None)
                kb.wait("vector", t_mm, tmm)
                ptv = pt[:].rearrange("p (dc f) -> p dc f", dc=8)
                ev = kb.op("vector", lambda E, tc=tc, ptv=ptv: E.tensor_copy(
                    out=xfm2[:, :, 128 * tc:128 * (tc + 1)], in_=ptv),
                    inc=(v_ev, 1))
                xc_ev.append((tmm, ev))
            v_xfm2 = xc_ev[-1][1]

            kb.wait("tensor", s_ld, p4c)
            kb.wait("tensor", g_ld, p4cg)
            kb.wait("tensor", v_ev, v_ln1)
            kb.wait("tensor", a_ev, a_ln1)
            sden = scr[:, 2100:2101]
            rden = scr[:, 2101:2102]
            xstg = scr[:, C_SF:C_SF + 1024]
            for tc in range(16):
                jt, wbt = WB.acquire()
                jt2, wbt2 = WB.acquire()
                for dc in range(8):
                    wbx = wbt if dc < 4 else wbt2
                    kb.op("tensor", lambda E, wbx=wbx, dc=dc, tc=tc: E.transpose(
                        wbx[:, 128 * (dc % 4):128 * (dc % 4 + 1)],
                        x_tm[:, D * tc + 128 * dc:D * tc + 128 * (dc + 1)],
                        ident32[:]), inc=(t_mm, 1) if dc == 7 else None)
                tvx = t_mm.n
                kb.wait("vector", t_mm, tvx)
                kb.op("vector", lambda E, wbt=wbt: E.tensor_copy(
                    out=xstg[:, 0:512], in_=wbt[:, 0:512]))
                vx = kb.op("vector", lambda E, wbt2=wbt2: E.tensor_copy(
                    out=xstg[:, 512:1024], in_=wbt2[:, 0:512]), inc=(v_ev, 1))
                WB.release(jt, v_ev, vx)
                WB.release(jt2, v_ev, vx)
                j, wb = WB.acquire()
                kb.wait("tensor", v_ev, vx)
                for dc in range(8):
                    kb.op("tensor", lambda E, wb=wb, dc=dc: E.matmul(
                        wb[:, 0:8],
                        xstg[:, 128 * dc:128 * (dc + 1)],
                        scr[:, 2944 + 8 * dc:2944 + 8 * (dc + 1)],
                        start=(dc == 0), stop=(dc == 7)),
                        inc=(t_mm, 1) if dc == 7 else None)
                tv = t_mm.n
                kb.wait("vector", t_mm, tv)
                negmax = scr[:, 2102:2103]
                vm = kb.op("vector", lambda E, wb=wb, negmax=negmax: E.reduce_max(
                    out=negmax, in_=wb[:, 0:8], axis=AX, negate=True),
                    inc=(v_ev, 1))
                kb.wait("scalar", v_ev, vm)
                sb = scr[:, C_SBG + 8 * tc:C_SBG + 8 * (tc + 1)]
                ae = kb.op("scalar", lambda E, wb=wb, sb=sb, negmax=negmax:
                           E.activation(out=sb, in_=wb[:, 0:8], func=AF.Exp,
                                        bias=negmax, accum_out=sden),
                           inc=(a_ev, 1))
                WB.release(j, a_ev, ae)
                kb.wait("vector", a_ev, ae)
                vr = kb.op("vector", lambda E: E.reciprocal(out=rden, in_=sden),
                           inc=(v_ev, 1))
                kb.wait("scalar", v_ev, vr)
                kb.op("scalar", lambda E, sb=sb: E.activation(
                    out=sb, in_=sb, func=AF.Copy, scale=rden), inc=(a_ev, 1))
            a_probs = a_ev.n

            kb.wait("tensor", a_ev, a_probs)
            for tc in range(16):
                j, wb = WB.acquire()
                tv = kb.op("tensor", lambda E, wb=wb, tc=tc: E.transpose(
                    wb[0:8, 0:128], scr[:, C_SBG + 8 * tc:C_SBG + 8 * (tc + 1)],
                    ident32[:]), inc=(t_mm, 1))
                kb.wait("vector", t_mm, tv)
                colo = (C_SF + 128 * tc) if tc < 8 else (2304 + 128 * (tc - 8))
                vv = kb.op("vector", lambda E, wb=wb, colo=colo:
                           E.tensor_copy(out=scr[0:8, colo:colo + 128],
                                         in_=wb[0:8, 0:128]), inc=(v_ev, 1))
                WB.release(j, v_ev, vv)
            v_sem8 = v_ev.n
            kb.wait("sync", v_ev, v_sem8)
            kb.op("sync", lambda E: E.dma_start(
                out=sc_ccin[:, 0:1024], in_=scr[0:8, C_SF:C_SF + 1024]),
                inc=(st, 16))
            stv = kb.op("sync", lambda E: E.dma_start(
                out=sc_ccin[:, 1024:2048], in_=scr[0:8, 2304:3328]),
                inc=(st, 16))
            kb.wait("gpsimd", st, stv)
            ag_done = kb.op("gpsimd", lambda E: E.collective_compute(
                "AllGather", ALU.bypass, ins=[sc_ccin], outs=[sc_ccout],
                replica_groups=[list(range(8))]), inc=(g_ms, 1))
            kb.wait("sync", g_ms, ag_done)
            sfold = scr[:, C_SF:C_SF + 1024]
            sf_ld = None
            for c2 in range(8):
                for h in range(2):
                    dst = scr[h + 2 * c2::16, C_SF:C_SF + 1024]
                    ssrc = sc_ccout[8 * c2:8 * (c2 + 1), 1024 * h:1024 * (h + 1)]
                    sf_ld = kb.op("sync", lambda E, dst=dst, ssrc=ssrc:
                                  E.dma_start(out=dst, in_=ssrc), inc=(s_ld, 16))

            lo = scr[:, C_BI + 0:C_BI + 1]
            hi = scr[:, C_BI + 1:C_BI + 2]
            mid = scr[:, C_BI + 2:C_BI + 3]
            sum_lh = scr[:, C_BI + 3:C_BI + 4]
            part = scr[:, C_BI + 4:C_BI + 5]
            cond = scr[:, C_BI + 5:C_BI + 6]
            cond_c = scr[:, C_BI + 6:C_BI + 7]
            dtmp = scr[:, C_BI + 7:C_BI + 8]
            cmp_bf = arena[:, A4_CMP:A4_CMP + 1024]
            kb.op("vector", lambda E: E.memset(lo, 0.0))
            kb.wait("vector", s_ld, sf_ld)
            kb.op("vector", lambda E: E.memset(hi, 1.0), inc=(v_ev, 1))
            for rnd in range(40):
                kb.drain("vector")
                va = kb.op("vector", lambda E: E.tensor_add(
                    out=sum_lh, in0=lo, in1=hi), inc=(v_ev, 1))
                kb.wait("scalar", v_ev, va)
                am = kb.op("scalar", lambda E: E.activation(
                    out=mid, in_=sum_lh, func=AF.Copy, scale=0.5), inc=(a_ev, 1))
                kb.wait("vector", a_ev, am)
                kb.op("vector", lambda E: E.tensor_scalar(
                    out=cmp_bf, in0=sfold, scalar1=mid, scalar2=None,
                    op0=ALU.is_gt))
                kb.drain("vector")
                vp = kb.op("vector", lambda E: E.reduce_sum(
                    out=part, in_=cmp_bf, axis=AX), inc=(v_ev, 1))
                kb.wait("tensor", v_ev, vp)
                tb = kb.op("tensor", lambda E: E.matmul(
                    wk0[:, 0:1], e128[:], part, start=True, stop=True),
                    inc=(t_mm, 1))
                kb.wait("vector", t_mm, tb)
                vc = kb.op("vector", lambda E: E.tensor_scalar(
                    out=cond, in0=wk0[:, 0:1], scalar1=4096.0, scalar2=None,
                    op0=ALU.is_ge), inc=(v_ev, 1))
                kb.wait("scalar", v_ev, vc)
                ac = kb.op("scalar", lambda E: E.activation(
                    out=cond_c, in_=cond, func=AF.Copy), inc=(a_ev, 1))
                kb.wait("vector", a_ev, ac)
                kb.drain("vector")
                kb.op("vector", lambda E: E.tensor_sub(out=dtmp, in0=mid, in1=lo))
                kb.drain("vector")
                kb.op("vector", lambda E: E.scalar_tensor_tensor(
                    out=lo, in0=dtmp, scalar=cond_c, in1=lo,
                    op0=ALU.mult, op1=ALU.add))
                kb.drain("vector")
                kb.op("vector", lambda E: E.tensor_sub(out=dtmp, in0=hi, in1=mid))
                kb.drain("vector")
                vs = kb.op("vector", lambda E: E.scalar_tensor_tensor(
                    out=hi, in0=dtmp, scalar=cond_c, in1=mid,
                    op0=ALU.mult, op1=ALU.add), inc=(v_ev, 1))
                kb.wait("tensor", v_ev, vs)
            kb.drain("vector")
            vb = kb.op("vector", lambda E: E.tensor_scalar_mul(
                out=part, in0=lo, scalar1=1.0 / 16.0), inc=(v_ev, 1))
            kb.wait("tensor", v_ev, vb)
            t1 = kb.op("tensor", lambda E: E.matmul(
                wk0[0:1, 0:8], part, e128[:, 0::16], start=True, stop=True),
                inc=(t_mm, 1))
            kb.wait("vector", t_mm, t1)
            vthr = kb.op("vector", lambda E: E.tensor_copy(
                out=scr[0:1, C_THR + 8:C_THR + 16], in_=wk0[0:1, 0:8]),
                inc=(v_ev, 1))
            kb.wait("tensor", v_ev, vthr)
            t2 = kb.op("tensor", lambda E: E.matmul(
                wk1[:, 0:8], ones32[0:1, :],
                scr[0:1, C_THR + 8:C_THR + 16], start=True, stop=True),
                inc=(t_mm, 1))
            kb.wait("vector", t_mm, t2)
            gthr = kb.op("vector", lambda E: E.tensor_copy(
                out=scr[:, C_THR:C_THR + 8], in_=wk1[:, 0:8]), inc=(v_ev, 1))
            kb.drain("vector")

            vmk = None
            for tc in range(16):
                sb = scr[:, C_SBG + 8 * tc:C_SBG + 8 * (tc + 1)]
                wbg = scr[:, C_WBG + 8 * tc:C_WBG + 8 * (tc + 1)]
                kb.op("vector", lambda E, sb=sb, wbg=wbg: E.tensor_tensor(
                    out=wbg, in0=sb, in1=scr[:, C_THR:C_THR + 8], op=ALU.is_gt))
                kb.drain("vector")
                vmk = kb.op("vector", lambda E, sb=sb, wbg=wbg: E.tensor_tensor(
                    out=wbg, in0=sb, in1=wbg, op=ALU.mult),
                    inc=(v_ev, 1) if tc == 15 else None)
            kb.wait("scalar", v_ev, vmk)
            aws = kb.op("scalar", lambda E: E.activation(
                out=scr[:, C_WSC:C_WSC + 128], in_=scr[:, C_WBG:C_WBG + 128],
                func=AF.Copy), inc=(a_ev, 1))

            kb.wait("tensor", a_ev, aws)
            kb.wait("vector", st, stv)
            wem_ev = None
            for tc in range(16):
                j, wb = WB.acquire()
                tv = kb.op("tensor", lambda E, wb=wb, tc=tc: E.transpose(
                    wb[0:8, 0:128], scr[:, C_WSC + 8 * tc:C_WSC + 8 * (tc + 1)],
                    ident32[:]), inc=(t_mm, 1))
                kb.wait("vector", t_mm, tv)
                wem_ev = kb.op("vector", lambda E, wb=wb, tc=tc: E.tensor_copy(
                    out=arena[0:8, A4_XC + 128 * tc:A4_XC + 128 * (tc + 1)],
                    in_=wb[0:8, 0:128]), inc=(v_ev, 1))
                WB.release(j, v_ev, wem_ev)
            kb.wait("tensor", v_ev, wem_ev)
            for tc in range(16):
                for half in range(2):
                    j, wb = WB.acquire()
                    tv = kb.op("tensor", lambda E, wb=wb, tc=tc, half=half:
                               E.matmul(wb[:, 0:512],
                                        arena[0:8, A4_XC + 128 * tc:
                                              A4_XC + 128 * (tc + 1)],
                                        arena[0:8, A4_B2 + 512 * half:
                                              A4_B2 + 512 * (half + 1)],
                                        start=True, stop=True), inc=(t_mm, 1))
                    kb.wait("vector", t_mm, tv)
                    dstx = x_tm[:, D * tc + 512 * half:D * tc + 512 * (half + 1)]
                    vv = kb.op("vector", lambda E, wb=wb, dstx=dstx: E.tensor_add(
                        out=dstx, in0=wb[:, 0:512], in1=dstx), inc=(v_ev, 1))
                    WB.release(j, v_ev, vv)

            if DBG:
                kb.wait("sync", v_ev, v_ev.n)
                kb.wait("sync", a_ev, a_ev.n)
                kb.op("sync", lambda E: E.dma_start(
                    out=dbg_scr, in_=scr[:, 0:1024]), inc=(st, 16))

            w1slot = lambda s: A4_W1 + 1024 * s
            w2slot = lambda s: A4_W2 + 8192 * s
            w1_i = [0]
            w2_state = {"free": [None, None]}
            p4_g_last = [g_ld.n]

            def load_w1(e, g, hc, free_at):
                sl = w1_i[0] % 2
                w1_i[0] += 1
                kb.wait("gpsimd", g_ld, p4_g_last[0])
                if free_at[sl] is not None:
                    kb.wait("gpsimd", *free_at[sl])
                col = 128 * (8 * g + hc)
                src3 = w1_c[e, :, col:col + 128].rearrange(
                    "(dc p) c -> p dc c", p=128)
                dst3 = arena[:, w1slot(sl):w1slot(sl) + 1024].rearrange(
                    "p (dc c) -> p dc c", dc=8)
                v = kb.op("gpsimd", lambda E, a=dst3, b=src3: E.dma_start(
                    out=a, in_=b), inc=(g_ld, 16))
                p4_g_last[0] = v
                return sl, v

            def load_w2(e, g):
                sl = (e * 4 + g) % 2
                kb.wait("gpsimd", g_ld, p4_g_last[0])
                if w2_state["free"][sl] is not None:
                    kb.wait("gpsimd", *w2_state["free"][sl])
                v = None
                for hc in range(8):
                    row = 128 * (8 * g + hc)
                    ssrc = w2_c[e, row:row + 128, :]
                    dst = arena[:, w2slot(sl) + 1024 * hc:
                                w2slot(sl) + 1024 * (hc + 1)]
                    v = kb.op("gpsimd", lambda E, a=dst, b=ssrc: E.dma_start(
                        out=a, in_=b), inc=(g_ld, 16))
                p4_g_last[0] = v
                return sl, v

            def load_b1(e):
                kb.wait("gpsimd", g_ld, p4_g_last[0])
                v = kb.op("gpsimd", lambda E, e=e: E.dma_start(
                    out=scr[:, C_B1:C_B1 + 32], in_=b1_c[e]), inc=(g_ld, 16))
                p4_g_last[0] = v
                return v

            w1_free = [None, None]
            xps = [zz, wk23]
            xfree = [None, None]
            kb.wait("tensor", v_ev, v_xfm2)
            for e in range(8):
                b1v = load_b1(e)
                for g in range(4):
                    w2sl, w2v = load_w2(e, g)
                    h1_ev = None
                    for hc in range(8):
                        w1sl, w1v = load_w1(e, g, hc, w1_free)
                        kb.wait("tensor", g_ld, w1v)
                        kb.wait("tensor", g_ld, b1v)
                        hv = None
                        for tq in range(4):
                            j, wb = WB.acquire()
                            for dc in range(8):
                                kb.op("tensor", lambda E, wb=wb, dc=dc, tq=tq,
                                      w1sl=w1sl: E.matmul(
                                          wb[:, 0:512],
                                          arena[:, w1slot(w1sl) + 128 * dc:
                                                w1slot(w1sl) + 128 * (dc + 1)],
                                          xfm2[:, dc, 512 * tq:512 * (tq + 1)],
                                          start=(dc == 0), stop=(dc == 7)),
                                      inc=(t_mm, 1) if dc == 7 else None)
                            tv = t_mm.n
                            kb.wait("scalar", t_mm, tv)
                            dsth = arena[:, A4_H1 + 2048 * hc + 512 * tq:
                                         A4_H1 + 2048 * hc + 512 * (tq + 1)]
                            hv = kb.op("scalar", lambda E, wb=wb, dsth=dsth:
                                       E.activation(out=dsth, in_=wb[:, 0:512],
                                                    func=AF.Gelu,
                                                    bias=scr[:, C_B1 + 8 * g +
                                                             hc:C_B1 + 8 * g +
                                                             hc + 1]),
                                       inc=(a_ev, 1))
                            WB.release(j, a_ev, hv)
                        w1_free[w1sl] = (a_ev, hv)
                        h1_ev = hv
                    kb.wait("tensor", a_ev, h1_ev)
                    kb.wait("tensor", g_ld, w2v)
                    xb_ev = None
                    for tc in range(16):
                        ps = xps[tc % 2]
                        if xfree[tc % 2] is not None:
                            kb.wait("tensor", v_ev, xfree[tc % 2])
                        for hc in range(8):
                            lhs = arena[:, A4_H1 + 2048 * hc + 128 * tc:
                                        A4_H1 + 2048 * hc + 128 * (tc + 1)]
                            kb.op("tensor", lambda E, ps=ps, lhs=lhs, hc=hc,
                                  w2sl=w2sl: E.matmul(
                                      ps[:, 0:512], lhs,
                                      arena[:, w2slot(w2sl) + 1024 * hc:
                                            w2slot(w2sl) + 1024 * hc + 512],
                                      start=(hc == 0), stop=False))
                            kb.op("tensor", lambda E, ps=ps, lhs=lhs, hc=hc,
                                  w2sl=w2sl: E.matmul(
                                      ps[:, 512:1024], lhs,
                                      arena[:, w2slot(w2sl) + 1024 * hc + 512:
                                            w2slot(w2sl) + 1024 * (hc + 1)],
                                      start=(hc == 0), stop=(hc == 7)),
                                  inc=(t_mm, 1) if hc == 7 else None)
                        tv = t_mm.n
                        kb.wait("vector", t_mm, tv)
                        wcol = scr[:, C_WSC + 8 * tc + e:C_WSC + 8 * tc + e + 1]
                        d0 = x_tm[:, D * tc:D * tc + 512]
                        d1 = x_tm[:, D * tc + 512:D * (tc + 1)]
                        kb.op("vector", lambda E, ps=ps, wcol=wcol, d0=d0:
                              E.scalar_tensor_tensor(
                                  out=d0, in0=ps[:, 0:512], scalar=wcol,
                                  in1=d0, op0=ALU.mult, op1=ALU.add))
                        xb_ev = kb.op("vector", lambda E, ps=ps, wcol=wcol, d1=d1:
                                      E.scalar_tensor_tensor(
                                          out=d1, in0=ps[:, 512:1024],
                                          scalar=wcol, in1=d1,
                                          op0=ALU.mult, op1=ALU.add),
                                      inc=(v_ev, 1))
                        xfree[tc % 2] = xb_ev
                    w2_state["free"][w2sl] = (v_ev, xb_ev)
                    WB.free_at[2] = (v_ev, xb_ev)
                    WB.free_at[3] = (v_ev, xb_ev)

            if DBG:
                kb.wait("sync", v_ev, v_ev.n)
                vpre2 = kb.op("sync", lambda E: E.dma_start(
                    out=dbg_xtm2, in_=x_tm[:]), inc=(st, 16))
                kb.wait("vector", st, vpre2)

            layernorm_inplace()

        if stage >= 3:
            NRM = 0.3535533905932738
            RATIO_EPS = 1e-6 / 16.0
            A5_XFM3 = 0
            A5_KF = 16384
            A5_V1 = 32768
            A5_KP = 49408
            A5_KV = 57600
            A5_PR = 59680
            A5_WS = 61984
            A5_BO = 59936
            A5_BV = 60960
            C5_BIAS = 2104
            xfm3 = arena[:, A5_XFM3:A5_XFM3 + 16384].rearrange(
                "p (dc f) -> p dc f", dc=8)
            v5 = arena[:, A5_V1:A5_V1 + 16640].rearrange(
                "p (tc s) -> p tc s", tc=16)
            v_ln2 = v_ev.n
            a_ln2 = a_ev.n

            kb.op("sync", lambda E: E.dma_start(
                out=scr[:, C5_BIAS:C5_BIAS + 8], in_=l1bq), inc=(s_ld, 16))
            kb.op("sync", lambda E: E.dma_start(
                out=scr[:, C5_BIAS + 8:C5_BIAS + 16], in_=l1bk), inc=(s_ld, 16))
            p5c = kb.op("sync", lambda E: E.dma_start(
                out=scr[:, 2304:2560], in_=l1proj), inc=(s_ld, 16))
            kb.wait("gpsimd", g_ld, g_ld.n)
            kb.op("gpsimd", lambda E: E.dma_start(
                out=arena[:, A5_BO:A5_BO + 1024], in_=l1bo_bc), inc=(g_ld, 16))
            p5cg = kb.op("gpsimd", lambda E: E.dma_start(
                out=arena[:, A5_BV:A5_BV + 1024], in_=l1bv_bc), inc=(g_ld, 16))
            kb.wait("scalar", s_ld, p5c)
            kb.wait("scalar", v_ev, v_ln2)
            aprj = kb.op("scalar", lambda E: E.activation(
                out=arena[:, A5_PR:A5_PR + 256], in_=scr[:, 2304:2560],
                func=AF.Copy, scale=NRM), inc=(a_ev, 1))

            kb.wait("vector", v_ev, v_ln2)
            kb.wait("vector", a_ev, a_ln2)
            xc_ev5 = []
            for tc in range(16):
                slot = A4_XC + 1024 * (tc % 2)
                if tc >= 2:
                    kb.wait("vector", t_mm, xc_ev5[tc - 2][0])
                vv = kb.op("vector", lambda E, tc=tc, slot=slot: E.tensor_copy(
                    out=arena[:, slot:slot + 1024],
                    in_=x_tm[:, D * tc:D * (tc + 1)]), inc=(v_ev, 1))
                kb.wait("tensor", v_ev, vv)
                pt = [tpA, tpB][tc % 2]
                if tc >= 2:
                    kb.wait("tensor", v_ev, xc_ev5[tc - 2][1])
                tmm = None
                for dc in range(8):
                    sap = arena[:, slot + 128 * dc:slot + 128 * (dc + 1)]
                    tmm = kb.op("tensor",
                                lambda E, pt=pt, dc=dc, sap=sap: E.transpose(
                                    pt[:, 128 * dc:128 * (dc + 1)], sap,
                                    ident[:]),
                                inc=(t_mm, 1) if dc == 7 else None)
                kb.wait("vector", t_mm, tmm)
                ptv = pt[:].rearrange("p (dc f) -> p dc f", dc=8)
                ev = kb.op("vector", lambda E, tc=tc, ptv=ptv: E.tensor_copy(
                    out=xfm3[:, :, 128 * tc:128 * (tc + 1)], in_=ptv),
                    inc=(v_ev, 1))
                xc_ev5.append((tmm, ev))
            v_xfm3 = xc_ev5[-1][1]

            p5_g_last = [g_ld.n]
            ws_free = [None, None]
            ws_i = [0]

            def load_wslice(wt, fc):
                sl = ws_i[0] % 2
                ws_i[0] += 1
                kb.wait("gpsimd", g_ld, p5_g_last[0])
                if ws_free[sl] is not None:
                    kb.wait("gpsimd", *ws_free[sl])
                src3 = wt[:, 128 * fc:128 * (fc + 1)].rearrange(
                    "(dc p) c -> p dc c", p=128)
                dst3 = arena[:, A5_WS + 1024 * sl:A5_WS + 1024 * (sl + 1)].rearrange(
                    "p (dc c) -> p dc c", dc=8)
                v = kb.op("gpsimd", lambda E, a=dst3, b=src3: E.dma_start(
                    out=a, in_=b), inc=(g_ld, 16))
                p5_g_last[0] = v
                return sl, v

            kb.wait("tensor", g_ld, p5cg)
            for fc in range(8):
                sl, wv = load_wslice(l1wk, fc)
                kb.wait("tensor", g_ld, wv)
                kb.wait("tensor", v_ev, v_xfm3)
                for ch in range(4):
                    j, wb = WB.acquire()
                    for dc in range(8):
                        kb.op("tensor", lambda E, wb=wb, dc=dc, ch=ch, sl=sl:
                              E.matmul(wb[:, 0:512],
                                       arena[:, A5_WS + 1024 * sl + 128 * dc:
                                             A5_WS + 1024 * sl + 128 * (dc + 1)],
                                       xfm3[:, dc, 512 * ch:512 * (ch + 1)],
                                       start=(dc == 0), stop=(dc == 7)),
                              inc=(t_mm, 1) if dc == 7 else None)
                    tv = t_mm.n
                    kb.wait("vector", t_mm, tv)
                    dst = arena[:, A5_KF + 2048 * fc + 512 * ch:
                                A5_KF + 2048 * fc + 512 * (ch + 1)]
                    bcol = scr[:, C5_BIAS + 8 + fc:C5_BIAS + 9 + fc]
                    vv = kb.op("vector", lambda E, wb=wb, dst=dst, bcol=bcol:
                               E.tensor_scalar(out=dst, in0=wb[:, 0:512],
                                               scalar1=bcol, scalar2=None,
                                               op0=ALU.add), inc=(v_ev, 1))
                    WB.release(j, v_ev, vv)
                ws_free[sl] = (v_ev, v_ev.n)

            vones5 = kb.op("vector", lambda E: E.memset(
                arena[:, A5_V1:A5_V1 + 16640], 1.0), inc=(v_ev, 1))
            for fc in range(8):
                sl, wv = load_wslice(l1wv, fc)
                kb.wait("tensor", g_ld, wv)
                kb.wait("tensor", v_ev, vones5)
                for tc in range(16):
                    j, wb = WB.acquire()
                    for dc in range(8):
                        kb.op("tensor", lambda E, wb=wb, dc=dc, tc=tc, sl=sl:
                              E.matmul(wb[:, 0:128],
                                       xfm3[:, dc, 128 * tc:128 * (tc + 1)],
                                       arena[:, A5_WS + 1024 * sl + 128 * dc:
                                             A5_WS + 1024 * sl + 128 * (dc + 1)],
                                       start=(dc == 0), stop=(dc == 7)),
                              inc=(t_mm, 1) if dc == 7 else None)
                    tv = t_mm.n
                    kb.wait("vector", t_mm, tv)
                    hA, hB = 2 * fc, 2 * fc + 1
                    bvA = arena[:, A5_BV + 128 * fc:A5_BV + 128 * fc + 64]
                    bvB = arena[:, A5_BV + 128 * fc + 64:A5_BV + 128 * (fc + 1)]
                    kb.op("vector", lambda E, wb=wb, tc=tc, hA=hA, bvA=bvA:
                          E.tensor_add(out=v5[:, tc, 65 * hA:65 * hA + 64],
                                       in0=wb[:, 0:64], in1=bvA))
                    vv = kb.op("vector", lambda E, wb=wb, tc=tc, hB=hB, bvB=bvB:
                               E.tensor_add(out=v5[:, tc, 65 * hB:65 * hB + 64],
                                            in0=wb[:, 64:128], in1=bvB),
                               inc=(v_ev, 1))
                    WB.release(j, v_ev, vv)
                ws_free[sl] = (v_ev, v_ev.n)

            kb.wait("scalar", a_ev, aprj)
            kq_i = [0]
            kp_free = [None, None]
            tp5_guard = [None, None]

            def kprime(h, src_base):
                sl = kq_i[0] % 2
                kq_i[0] += 1
                if kp_free[sl] is not None:
                    kb.wait("tensor", *kp_free[sl])
                fc, half = h // 2, h % 2
                kbase = src_base + 2048 * fc
                pr = 64 * half
                last = None
                for tc in range(16):
                    j, wb = WB.acquire()
                    lhs = arena[pr:pr + 64, kbase + 128 * tc:kbase + 128 * (tc + 1)]
                    kb.op("tensor", lambda E, wb=wb, lhs=lhs, pr=pr: E.matmul(
                        wb[:, 0:256], lhs, arena[pr:pr + 64, A5_PR:A5_PR + 256],
                        start=True, stop=True))
                    pt = [tpA, tpB][tc % 2]
                    if tp5_guard[tc % 2] is not None:
                        kb.wait("tensor", v_ev, tp5_guard[tc % 2])
                    idsl = ident[pr:pr + 64, pr:pr + 64]
                    tv = kb.op("tensor", lambda E, pt=pt, lhs=lhs, idsl=idsl:
                               E.transpose(pt[:, 0:64], lhs, idsl),
                               inc=(t_mm, 1))
                    kb.wait("scalar", t_mm, tv)
                    sq = scr[:, 2140:2204]
                    dg = scr[:, 2204:2205]
                    asq = kb.op("scalar", lambda E, pt=pt, sq=sq: E.activation(
                        out=sq, in_=pt[:, 0:64], func=AF.Square), inc=(a_ev, 1))
                    kb.wait("vector", a_ev, asq)
                    vd = kb.op("vector", lambda E, sq=sq, dg=dg: E.reduce_sum(
                        out=dg, in_=sq, axis=AX), inc=(v_ev, 1))
                    tp5_guard[tc % 2] = vd
                    kb.wait("scalar", v_ev, vd)
                    et = scr[:, 2205:2206]
                    kb.op("scalar", lambda E, dg=dg, et=et: E.activation(
                        out=et, in_=dg, func=AF.Exp, scale=-(NRM * NRM) / 2.0,
                        bias=fconst[:, 1:2]))
                    dst = arena[:, A5_KP + 4096 * sl + 256 * tc:
                                A5_KP + 4096 * sl + 256 * (tc + 1)]
                    ae = kb.op("scalar", lambda E, wb=wb, dst=dst: E.activation(
                        out=dst, in_=wb[:, 0:256], func=AF.Exp), inc=(a_ev, 1))
                    kb.wait("vector", a_ev, ae)
                    WB.release(j, a_ev, ae)
                    last = kb.op("vector", lambda E, dst=dst, et=et:
                                 E.tensor_scalar(out=dst, in0=dst, scalar1=et,
                                                 scalar2=RATIO_EPS, op0=ALU.mult,
                                                 op1=ALU.add), inc=(v_ev, 1))
                return sl, last

            kv_st = []
            for h in range(16):
                sl, kl = kprime(h, A5_KF)
                kb.wait("tensor", v_ev, kl)
                for fc2 in range(2):
                    j, wb = WB.acquire()
                    for tc in range(16):
                        lhs = arena[:, A5_KP + 4096 * sl + 256 * tc + 128 * fc2:
                                    A5_KP + 4096 * sl + 256 * tc + 128 * (fc2 + 1)]
                        rhs = v5[:, tc, 65 * h:65 * (h + 1)]
                        kb.op("tensor", lambda E, wb=wb, lhs=lhs, rhs=rhs, tc=tc:
                              E.matmul(wb[:, 0:65], lhs, rhs,
                                       start=(tc == 0), stop=(tc == 15)),
                              inc=(t_mm, 1) if tc == 15 else None)
                    tv = t_mm.n
                    kb.wait("vector", t_mm, tv)
                    ksl = (2 * h + fc2) % 2
                    kvs = scr[:, 2208 + 65 * ksl:2273 + 65 * ksl]
                    if len(kv_st) >= 2:
                        kb.wait("vector", st, kv_st[-2])
                    vv = kb.op("vector", lambda E, wb=wb, kvs=kvs: E.tensor_copy(
                        out=kvs, in_=wb[:, 0:65]), inc=(v_ev, 1))
                    WB.release(j, v_ev, vv)
                    kb.wait("sync", v_ev, vv)
                    row0 = 256 * h + 128 * fc2
                    sv = kb.op("sync", lambda E, row0=row0, kvs=kvs: E.dma_start(
                        out=kv_ccin[row0:row0 + 128, :], in_=kvs),
                        inc=(st, 16))
                    if DBG:
                        sv = kb.op("sync", lambda E, row0=row0, kvs=kvs:
                                   E.dma_start(out=dbg_kvp[row0:row0 + 128, :],
                                               in_=kvs), inc=(st, 16))
                    kv_st.append(sv)
                kp_free[sl] = (t_mm, t_mm.n)
            if DBG:
                kb.wait("sync", st, kv_st[-1])
                kb.op("sync", lambda E: E.dma_start(
                    out=dbg_kp, in_=arena[:, A5_KP:A5_KP + 8192]), inc=(st, 16))
                kb.op("sync", lambda E: E.dma_start(
                    out=dbg_v5, in_=arena[:, A5_V1:A5_V1 + 16640]), inc=(st, 16))
            q_wv = []
            for fc in range(2):
                q_wv.append(load_wslice(l1wq, fc))
            kb.wait("gpsimd", st, kv_st[-1])
            kvar = kb.op("gpsimd", lambda E: E.collective_compute(
                "AllReduce", ALU.add, ins=[kv_ccin], outs=[kv_ccout],
                replica_groups=[[0, 1], [2, 3], [4, 5], [6, 7]]), inc=(g_ms, 1))
            kvo4 = kv_ccout.rearrange("(h fc p) c -> p h fc c", h=16, fc=2)
            kva4 = arena[:, A5_KV:A5_KV + 2080].rearrange(
                "p (h fc c) -> p h fc c", h=16, fc=2)
            kb.wait("gpsimd", g_ms, kvar)
            kvld = kb.op("gpsimd", lambda E: E.dma_start(out=kva4, in_=kvo4),
                         inc=(g_ld, 16))
            p5_g_last[0] = kvld

            kb.wait("tensor", t_mm, t_mm.n)
            for fc in range(8):
                if fc < 2:
                    sl, wv = q_wv[fc]
                else:
                    sl, wv = load_wslice(l1wq, fc)
                kb.wait("tensor", g_ld, wv)
                for ch in range(4):
                    j, wb = WB.acquire()
                    for dc in range(8):
                        kb.op("tensor", lambda E, wb=wb, dc=dc, ch=ch, sl=sl:
                              E.matmul(wb[:, 0:512],
                                       arena[:, A5_WS + 1024 * sl + 128 * dc:
                                             A5_WS + 1024 * sl + 128 * (dc + 1)],
                                       xfm3[:, dc, 512 * ch:512 * (ch + 1)],
                                       start=(dc == 0), stop=(dc == 7)),
                              inc=(t_mm, 1) if dc == 7 else None)
                    tv = t_mm.n
                    kb.wait("vector", t_mm, tv)
                    dst = arena[:, A5_KF + 2048 * fc + 512 * ch:
                                A5_KF + 2048 * fc + 512 * (ch + 1)]
                    bcol = scr[:, C5_BIAS + fc:C5_BIAS + 1 + fc]
                    vv = kb.op("vector", lambda E, wb=wb, dst=dst, bcol=bcol:
                               E.tensor_scalar(out=dst, in0=wb[:, 0:512],
                                               scalar1=bcol, scalar2=None,
                                               op0=ALU.add), inc=(v_ev, 1))
                    WB.release(j, v_ev, vv)
                ws_free[sl] = (v_ev, v_ev.n)


            z1 = arena[:, A5_V1:A5_V1 + 16384]
            kb.wait("tensor", g_ld, kvld)
            qp_ev = None
            for h in range(16):
                sl, ql = kprime(h, A5_KF)
                kb.wait("tensor", v_ev, ql)
                fsl = (sl + 1) % 2
                if kp_free[fsl] is not None:
                    kb.wait("tensor", *kp_free[fsl])
                tp_ev5 = []
                for tc in range(16):
                    pt = [tpA, tpB][tc % 2]
                    if tc >= 2:
                        kb.wait("tensor", v_ev, tp_ev5[tc - 2])
                    elif tp5_guard[tc % 2] is not None:
                        kb.wait("tensor", v_ev, tp5_guard[tc % 2])
                    tv = None
                    for f2 in range(2):
                        sap = arena[:, A5_KP + 4096 * sl + 256 * tc + 128 * f2:
                                    A5_KP + 4096 * sl + 256 * tc + 128 * (f2 + 1)]
                        tv = kb.op("tensor", lambda E, pt=pt, f2=f2, sap=sap:
                                   E.transpose(pt[:, 128 * f2:128 * (f2 + 1)],
                                               sap, ident[:]),
                                   inc=(t_mm, 1) if f2 == 1 else None)
                    kb.wait("vector", t_mm, tv)
                    vv = None
                    for f2 in range(2):
                        dst = arena[:, A5_KP + 4096 * fsl + 2048 * f2 + 128 * tc:
                                    A5_KP + 4096 * fsl + 2048 * f2 + 128 * (tc + 1)]
                        vv = kb.op("vector", lambda E, pt=pt, f2=f2, dst=dst:
                                   E.tensor_copy(out=dst,
                                                 in_=pt[:, 128 * f2:128 * (f2 + 1)]),
                                   inc=(v_ev, 1) if f2 == 1 else None)
                    tp_ev5.append(vv)
                    tp5_guard[tc % 2] = vv
                kb.wait("tensor", v_ev, tp_ev5[-1])
                for tc in range(16):
                    j, wb = WB.acquire()
                    for f2 in range(2):
                        lhs = arena[:, A5_KP + 4096 * fsl + 2048 * f2 + 128 * tc:
                                    A5_KP + 4096 * fsl + 2048 * f2 + 128 * (tc + 1)]
                        rhs = kva4[:, h, f2, :]
                        kb.op("tensor", lambda E, wb=wb, lhs=lhs, rhs=rhs, f2=f2:
                              E.matmul(wb[:, 0:65], lhs, rhs,
                                       start=(f2 == 0), stop=(f2 == 1)),
                              inc=(t_mm, 1) if f2 == 1 else None)
                    tv = t_mm.n
                    kb.wait("vector", t_mm, tv)
                    nrm = scr[:, 2340:2341]
                    rec = scr[:, 2341:2342]
                    rec_c = scr[:, 2342:2343]
                    kb.op("vector", lambda E, wb=wb, nrm=nrm: E.tensor_scalar(
                        out=nrm, in0=wb[:, 64:65], scalar1=1e-6, scalar2=None,
                        op0=ALU.add))
                    kb.drain("vector")
                    vr = kb.op("vector", lambda E, nrm=nrm, rec=rec: E.reciprocal(
                        out=rec, in_=nrm), inc=(v_ev, 1))
                    kb.wait("scalar", v_ev, vr)
                    arc = kb.op("scalar", lambda E, rec=rec, rec_c=rec_c:
                                E.activation(out=rec_c, in_=rec, func=AF.Copy),
                                inc=(a_ev, 1))
                    kb.wait("vector", a_ev, arc)
                    dst = z1[:, 1024 * tc + 64 * h:1024 * tc + 64 * (h + 1)]
                    qp_ev = kb.op("vector", lambda E, wb=wb, dst=dst, rec_c=rec_c:
                                  E.tensor_scalar(out=dst, in0=wb[:, 0:64],
                                                  scalar1=rec_c, scalar2=None,
                                                  op0=ALU.mult), inc=(v_ev, 1))
                    WB.release(j, v_ev, qp_ev)
                kp_free[sl] = (t_mm, t_mm.n)

            kb.wait("gpsimd", t_mm, t_mm.n)
            wo_ld = []
            for ec in range(8):
                kb.wait("gpsimd", g_ld, p5_g_last[0])
                dst = arena[:, A5_KP + 1024 * ec:A5_KP + 1024 * (ec + 1)]
                ssrc = l1wo[128 * ec:128 * (ec + 1), :]
                v = kb.op("gpsimd", lambda E, a=dst, b=ssrc: E.dma_start(
                    out=a, in_=b), inc=(g_ld, 16))
                p5_g_last[0] = v
                wo_ld.append(v)

            kb.wait("tensor", v_ev, qp_ev)
            z1fm = arena[:, A5_XFM3:A5_XFM3 + 16384].rearrange(
                "p (dc f) -> p dc f", dc=8)
            tp_ev6 = []
            for tc in range(16):
                pt = [tpA, tpB][tc % 2]
                if tc >= 2:
                    kb.wait("tensor", v_ev, tp_ev6[tc - 2])
                elif tp5_guard[tc % 2] is not None:
                    kb.wait("tensor", v_ev, tp5_guard[tc % 2])
                tmm = None
                for dc in range(8):
                    sap = z1[:, 1024 * tc + 128 * dc:1024 * tc + 128 * (dc + 1)]
                    tmm = kb.op("tensor", lambda E, pt=pt, dc=dc, sap=sap:
                                E.transpose(pt[:, 128 * dc:128 * (dc + 1)], sap,
                                            ident[:]),
                                inc=(t_mm, 1) if dc == 7 else None)
                kb.wait("vector", t_mm, tmm)
                ptv = pt[:].rearrange("p (dc f) -> p dc f", dc=8)
                ev = kb.op("vector", lambda E, tc=tc, ptv=ptv: E.tensor_copy(
                    out=z1fm[:, :, 128 * tc:128 * (tc + 1)], in_=ptv),
                    inc=(v_ev, 1))
                tp_ev6.append(ev)
            v_z1fm = tp_ev6[-1]

            kb.wait("vector", g_ld, p5cg)
            for tc in range(16):
                kb.op("vector", lambda E, tc=tc: E.tensor_add(
                    out=x_tm[:, D * tc:D * (tc + 1)],
                    in0=arena[:, A5_BO:A5_BO + 1024],
                    in1=x_tm[:, D * tc:D * (tc + 1)]),
                    inc=(v_ev, 1) if tc == 15 else None)
            kb.wait("tensor", v_ev, v_z1fm)
            kb.wait("tensor", g_ld, wo_ld[-1])
            xps5 = [zz, wk23]
            xfree5 = [None, None]
            o_ev = None
            for tc in range(16):
                ps = xps5[tc % 2]
                if xfree5[tc % 2] is not None:
                    kb.wait("tensor", v_ev, xfree5[tc % 2])
                for half in range(2):
                    for ec in range(8):
                        lhs = z1fm[:, ec, 128 * tc:128 * (tc + 1)]
                        rhs = arena[:, A5_KP + 1024 * ec + 512 * half:
                                    A5_KP + 1024 * ec + 512 * (half + 1)]
                        kb.op("tensor", lambda E, ps=ps, lhs=lhs, rhs=rhs,
                              ec=ec, half=half: E.matmul(
                                  ps[:, 512 * half:512 * (half + 1)], lhs, rhs,
                                  start=(ec == 0), stop=(ec == 7)),
                              inc=(t_mm, 1) if (half == 1 and ec == 7) else None)
                tv = t_mm.n
                kb.wait("vector", t_mm, tv)
                d0 = x_tm[:, D * tc:D * tc + 512]
                d1 = x_tm[:, D * tc + 512:D * (tc + 1)]
                kb.op("vector", lambda E, ps=ps, d0=d0: E.tensor_add(
                    out=d0, in0=ps[:, 0:512], in1=d0))
                o_ev = kb.op("vector", lambda E, ps=ps, d1=d1: E.tensor_add(
                    out=d1, in0=ps[:, 512:1024], in1=d1), inc=(v_ev, 1))
                xfree5[tc % 2] = o_ev

            if DBG:
                kb.wait("sync", v_ev, v_ev.n)
                kb.op("sync", lambda E: E.dma_start(
                    out=dbg_kv, in_=arena[:, A5_KV:A5_KV + 2080]), inc=(st, 16))
                kb.op("sync", lambda E: E.dma_start(
                    out=dbg_z1, in_=z1), inc=(st, 16))
                vpre3 = kb.op("sync", lambda E: E.dma_start(
                    out=dbg_xtm3, in_=x_tm[:]), inc=(st, 16))
                kb.wait("vector", st, vpre3)

            layernorm_inplace()

        if stage >= 4:
            A6_XFM4 = 0
            A6_A = 16384
            A6_GU = 38912
            A6_DW = 43008
            A6_WR = 57984
            A6_W1T = 54336
            A6_W2A = 54848
            A6_ST = 54856
            A6_AH = 55880
            C6_K0 = 2580
            C6_K0C = 2600
            C6_LG = 2620
            xfm4 = arena[:, A6_XFM4:A6_XFM4 + 16384].rearrange(
                "p (dc f) -> p dc f", dc=8)
            v_ln3 = v_ev.n
            a_ln3 = a_ev.n

            kb.op("sync", lambda E: E.dma_start(
                out=scr[:, 2560:2576], in_=jrow_c), inc=(s_ld, 16))
            p6c = kb.op("sync", lambda E: E.dma_start(
                out=scr[:, 2576:2577], in_=ab1p), inc=(s_ld, 16))
            kb.wait("gpsimd", g_ld, g_ld.n)
            wr23 = wr2_c.rearrange("(dc p) e -> p dc e", p=128)
            kb.op("gpsimd", lambda E: E.dma_start(
                out=arena[:, A6_WR:A6_WR + 64].rearrange(
                    "p (dc e) -> p dc e", dc=8), in_=wr23), inc=(g_ld, 16))
            aw13 = aw1T.rearrange("(dc p) a -> p dc a", p=128)
            kb.op("gpsimd", lambda E: E.dma_start(
                out=arena[:, A6_W1T:A6_W1T + 400].rearrange(
                    "p (dc a) -> p dc a", dc=8), in_=aw13), inc=(g_ld, 16))
            p6cg = kb.op("gpsimd", lambda E: E.dma_start(
                out=arena[:, A6_W2A:A6_W2A + 6], in_=aw2A), inc=(g_ld, 16))

            kb.wait("vector", v_ev, v_ln3)
            kb.wait("vector", a_ev, a_ln3)
            xc_ev6 = []
            for tc in range(16):
                slot = A4_XC + 1024 * (tc % 2)
                if tc >= 2:
                    kb.wait("vector", t_mm, xc_ev6[tc - 2][0])
                vv = kb.op("vector", lambda E, tc=tc, slot=slot: E.tensor_copy(
                    out=arena[:, slot:slot + 1024],
                    in_=x_tm[:, D * tc:D * (tc + 1)]), inc=(v_ev, 1))
                kb.wait("tensor", v_ev, vv)
                pt = [tpA, tpB][tc % 2]
                if tc >= 2:
                    kb.wait("tensor", v_ev, xc_ev6[tc - 2][1])
                elif tp5_guard[tc % 2] is not None:
                    kb.wait("tensor", v_ev, tp5_guard[tc % 2])
                tmm = None
                for dc in range(8):
                    sap = arena[:, slot + 128 * dc:slot + 128 * (dc + 1)]
                    tmm = kb.op("tensor",
                                lambda E, pt=pt, dc=dc, sap=sap: E.transpose(
                                    pt[:, 128 * dc:128 * (dc + 1)], sap,
                                    ident[:]),
                                inc=(t_mm, 1) if dc == 7 else None)
                kb.wait("vector", t_mm, tmm)
                ptv = pt[:].rearrange("p (dc f) -> p dc f", dc=8)
                ev = kb.op("vector", lambda E, tc=tc, ptv=ptv: E.tensor_copy(
                    out=xfm4[:, :, 128 * tc:128 * (tc + 1)], in_=ptv),
                    inc=(v_ev, 1))
                xc_ev6.append((tmm, ev))
            v_xfm4 = xc_ev6[-1][1]

            kb.wait("tensor", g_ld, p6cg)
            kb.wait("tensor", s_ld, p6c)
            kb.wait("tensor", v_ev, v_xfm4)
            sden6 = scr[:, 2100:2101]
            rden6 = scr[:, 2101:2102]
            for tc in range(16):
                j, wb = WB.acquire()
                for dc in range(8):
                    kb.op("tensor", lambda E, wb=wb, dc=dc, tc=tc: E.matmul(
                        wb[:, 0:8],
                        xfm4[:, dc, 128 * tc:128 * (tc + 1)],
                        arena[:, A6_WR + 8 * dc:A6_WR + 8 * (dc + 1)],
                        start=(dc == 0), stop=(dc == 7)),
                        inc=(t_mm, 1) if dc == 7 else None)
                tv = t_mm.n
                kb.wait("vector", t_mm, tv)
                negmax = scr[:, 2102:2103]
                vm = kb.op("vector", lambda E, wb=wb, negmax=negmax: E.reduce_max(
                    out=negmax, in_=wb[:, 0:8], axis=AX, negate=True),
                    inc=(v_ev, 1))
                kb.wait("scalar", v_ev, vm)
                sb = scr[:, C_SBG + 8 * tc:C_SBG + 8 * (tc + 1)]
                ae = kb.op("scalar", lambda E, wb=wb, sb=sb, negmax=negmax:
                           E.activation(out=sb, in_=wb[:, 0:8], func=AF.Exp,
                                        bias=negmax, accum_out=sden6),
                           inc=(a_ev, 1))
                WB.release(j, a_ev, ae)
                kb.wait("vector", a_ev, ae)
                vr = kb.op("vector", lambda E: E.reciprocal(
                    out=rden6, in_=sden6), inc=(v_ev, 1))
                kb.wait("scalar", v_ev, vr)
                kb.op("scalar", lambda E, sb=sb: E.activation(
                    out=sb, in_=sb, func=AF.Copy, scale=rden6), inc=(a_ev, 1))
            a_probs6 = a_ev.n
            if DBG:
                kb.wait("vector", a_ev, a_probs6)
                kb.op("vector", lambda E: E.tensor_copy(
                    out=scr[:, 2816:2944], in_=scr[:, C_SBG:C_SBG + 128]),
                    inc=(v_ev, 1))

            vah = kb.op("vector", lambda E: E.memset(
                arena[32:64, A6_AH:A6_AH + 2048], 1.0), inc=(v_ev, 1))
            kb.wait("scalar", v_ev, vah)
            for ch in range(4):
                j, wb = WB.acquire()
                for dc in range(8):
                    kb.op("tensor", lambda E, wb=wb, dc=dc, ch=ch: E.matmul(
                        wb[0:50, 0:512],
                        arena[:, A6_W1T + 50 * dc:A6_W1T + 50 * (dc + 1)],
                        xfm4[:, dc, 512 * ch:512 * (ch + 1)],
                        start=(dc == 0), stop=(dc == 7)),
                        inc=(t_mm, 1) if dc == 7 else None)
                tv = t_mm.n
                kb.wait("scalar", t_mm, tv)
                dst = arena[0:50, A6_AH + 512 * ch:A6_AH + 512 * (ch + 1)]
                ah = kb.op("scalar", lambda E, wb=wb, dst=dst: E.activation(
                    out=dst, in_=wb[0:50, 0:512], func=AF.Gelu_apprx_tanh,
                    bias=scr[0:50, 2576:2577]), inc=(a_ev, 1))
                WB.release(j, a_ev, ah)
            a_ah = a_ev.n

            kb.wait("tensor", a_ev, a_ah)
            lg = scr[:, C6_LG:C6_LG + 6]
            m6 = scr[:, C6_LG + 6:C6_LG + 7]
            m6c = scr[:, C6_LG + 7:C6_LG + 8]
            tmp6 = scr[:, C6_LG + 8:C6_LG + 14]
            for tc in range(16):
                j, wb = WB.acquire()
                tv = kb.op("tensor", lambda E, wb=wb, tc=tc: E.matmul(
                    wb[:, 0:6],
                    arena[0:51, A6_AH + 128 * tc:A6_AH + 128 * (tc + 1)],
                    arena[0:51, A6_W2A:A6_W2A + 6],
                    start=True, stop=True), inc=(t_mm, 1))
                kb.wait("vector", t_mm, tv)
                vc = kb.op("vector", lambda E, wb=wb: E.tensor_scalar(
                    out=lg, in0=wb[:, 0:6], scalar1=30.0, scalar2=-30.0,
                    op0=ALU.min, op1=ALU.max), inc=(v_ev, 1))
                WB.release(j, v_ev, vc)
                kb.drain("vector")
                kb.op("vector", lambda E: E.reduce_max(out=m6, in_=lg, axis=AX),
                      inc=(v_ev, 1))
                kb.wait("scalar", v_ev, v_ev.n)
                kb.op("scalar", lambda E: E.activation(
                    out=m6c, in_=m6, func=AF.Copy), inc=(a_ev, 1))
                kb.wait("vector", a_ev, a_ev.n)
                kb.op("vector", lambda E: E.tensor_scalar(
                    out=tmp6, in0=lg, scalar1=m6c, scalar2=None, op0=ALU.is_ge))
                kb.drain("vector")
                kb.op("vector", lambda E: E.scalar_tensor_tensor(
                    out=tmp6, in0=tmp6, scalar=-1000.0, in1=scr[:, 2560:2566],
                    op0=ALU.mult, op1=ALU.add))
                kb.drain("vector")
                kb.op("vector", lambda E, tc=tc: E.tensor_reduce(
                    out=scr[:, C6_K0 + tc:C6_K0 + tc + 1], in_=tmp6,
                    axis=AX, op=ALU.min), inc=(v_ev, 1))
            kb.wait("scalar", v_ev, v_ev.n)
            ak0 = kb.op("scalar", lambda E: E.activation(
                out=scr[:, C6_K0C:C6_K0C + 16], in_=scr[:, C6_K0:C6_K0 + 16],
                func=AF.Copy), inc=(a_ev, 1))

            kb.wait("vector", a_ev, a_probs6)
            V6 = scr[:, C_WBG:C_WBG + 128]
            kb.drain("vector")
            kb.op("vector", lambda E: E.tensor_copy(
                out=V6, in_=scr[:, C_SBG:C_SBG + 128]))
            CE = [(0, 1), (2, 3), (4, 5), (6, 7), (0, 2), (1, 3), (4, 6),
                  (5, 7), (1, 2), (5, 6), (0, 4), (3, 7), (1, 5), (2, 6),
                  (3, 5), (2, 4), (1, 2), (3, 4), (5, 6)]
            t8 = scr[:, 2644:2660]
            for (i, jj) in CE:
                Vi = scr[:, C_WBG + i:C_WBG + i + 121:8]
                Vj = scr[:, C_WBG + jj:C_WBG + jj + 121:8]
                kb.drain("vector")
                kb.op("vector", lambda E, Vi=Vi, Vj=Vj: E.tensor_tensor(
                    out=t8, in0=Vi, in1=Vj, op=ALU.min))
                kb.drain("vector")
                kb.op("vector", lambda E, Vi=Vi, Vj=Vj: E.tensor_tensor(
                    out=Vi, in0=Vi, in1=Vj, op=ALU.max))
                kb.drain("vector")
                kb.op("vector", lambda E, Vj=Vj: E.tensor_copy(
                    out=Vj, in_=t8))
            vso = kb.op("vector", lambda E: E.engine_nop(), inc=(v_ev, 1))

            kb.wait("vector", a_ev, ak0)
            tau = scr[:, 2660:2661]
            tauc = scr[:, 2661:2662]
            tmp8b = scr[:, 2662:2670]
            mk8 = scr[:, 2670:2678]
            for tc in range(16):
                kb.drain("vector")
                kb.op("vector", lambda E, tc=tc: E.tensor_scalar(
                    out=tmp8b, in0=scr[:, 2568:2576],
                    scalar1=scr[:, C6_K0C + tc:C6_K0C + tc + 1], scalar2=None,
                    op0=ALU.is_equal))
                kb.drain("vector")
                kb.op("vector", lambda E, tc=tc: E.tensor_mul(
                    out=tmp8b, in0=tmp8b,
                    in1=scr[:, C_WBG + 8 * tc:C_WBG + 8 * (tc + 1)]))
                kb.drain("vector")
                vt = kb.op("vector", lambda E: E.reduce_sum(
                    out=tau, in_=tmp8b, axis=AX), inc=(v_ev, 1))
                kb.wait("scalar", v_ev, vt)
                at = kb.op("scalar", lambda E: E.activation(
                    out=tauc, in_=tau, func=AF.Copy), inc=(a_ev, 1))
                kb.wait("vector", a_ev, at)
                sb = scr[:, C_SBG + 8 * tc:C_SBG + 8 * (tc + 1)]
                kb.op("vector", lambda E, sb=sb: E.tensor_scalar(
                    out=mk8, in0=sb, scalar1=tauc, scalar2=None, op0=ALU.is_ge))
                kb.drain("vector")
                kb.op("vector", lambda E, sb=sb, tc=tc: E.tensor_mul(
                    out=scr[:, C_WSC + 8 * tc:C_WSC + 8 * (tc + 1)],
                    in0=sb, in1=mk8), inc=(v_ev, 1) if tc == 15 else None)
            vsc = v_ev.n
            kb.wait("scalar", v_ev, vsc)
            asc6 = kb.op("scalar", lambda E: E.activation(
                out=scr[:, C_SBG:C_SBG + 128], in_=scr[:, C_WSC:C_WSC + 128],
                func=AF.Copy), inc=(a_ev, 1))

            ICG = [list(range(0, 7)), list(range(7, 14)), list(range(14, 21))]
            gu_free = [None] * 4
            gu_i = [0]
            p6_g_last = [g_ld.n]

            def load_gu(wt, e, icg):
                sl = gu_i[0] % 4
                gu_i[0] += 1
                kb.wait("gpsimd", g_ld, p6_g_last[0])
                if gu_free[sl] is not None:
                    kb.wait("gpsimd", *gu_free[sl])
                src3 = wt[e, :, 128 * icg:128 * (icg + 1)].rearrange(
                    "(dc p) c -> p dc c", p=128)
                dst3 = arena[:, A6_GU + 1024 * sl:A6_GU + 1024 * (sl + 1)].rearrange(
                    "p (dc c) -> p dc c", dc=8)
                v = kb.op("gpsimd", lambda E, a=dst3, b=src3: E.dma_start(
                    out=a, in_=b), inc=(g_ld, 16))
                p6_g_last[0] = v
                return sl, v

            DWS = [43008, 30720]

            def load_down(e, g, gate_after, slot):
                base = DWS[slot]
                kb.wait("gpsimd", g_ld, p6_g_last[0])
                if gate_after is not None:
                    kb.wait("gpsimd", *gate_after)
                v = None
                for i_loc, icg in enumerate(ICG[g]):
                    ssrc = dwT[e, 128 * icg:128 * (icg + 1), :]
                    dst = arena[:, base + 1024 * i_loc:base + 1024 * (i_loc + 1)]
                    v = kb.op("gpsimd", lambda E, a=dst, b=ssrc: E.dma_start(
                        out=a, in_=b), inc=(g_ld, 16))
                p6_g_last[0] = v
                return v

            kb.wait("tensor", a_ev, asc6)
            xps6 = [zz, wk23]
            xfree6 = [None, None]
            dwn_guard = [None, None]
            dv_pend = {}
            gidx = 0
            dv_pend[0] = load_down(0, 0, None, 0)
            for e in range(8):
                for g in range(3):
                    slot = gidx % 2
                    dv = dv_pend[gidx]
                    a_last = None
                    st_hist = []
                    for i_loc, icg in enumerate(ICG[g]):
                        slg, gv = load_gu(gwT, e, icg)
                        slu, uv = load_gu(uwT, e, icg)
                        kb.wait("tensor", g_ld, uv)
                        tvg = tvu = None
                        for tq in range(4):
                            jg, wbg_ = WB.acquire()
                            for dc in range(8):
                                kb.op("tensor", lambda E, wbg_=wbg_, dc=dc,
                                      tq=tq, slg=slg: E.matmul(
                                          wbg_[:, 0:512],
                                          arena[:, A6_GU + 1024 * slg + 128 * dc:
                                                A6_GU + 1024 * slg + 128 * (dc + 1)],
                                          xfm4[:, dc, 512 * tq:512 * (tq + 1)],
                                          start=(dc == 0), stop=(dc == 7)),
                                      inc=(t_mm, 1) if dc == 7 else None)
                            tvg = t_mm.n
                            ju, wbu = WB.acquire()
                            for dc in range(8):
                                kb.op("tensor", lambda E, wbu=wbu, dc=dc,
                                      tq=tq, slu=slu: E.matmul(
                                          wbu[:, 0:512],
                                          arena[:, A6_GU + 1024 * slu + 128 * dc:
                                                A6_GU + 1024 * slu + 128 * (dc + 1)],
                                          xfm4[:, dc, 512 * tq:512 * (tq + 1)],
                                          start=(dc == 0), stop=(dc == 7)),
                                      inc=(t_mm, 1) if dc == 7 else None)
                            tvu = t_mm.n
                            kb.wait("scalar", t_mm, tvg)
                            ssl = A6_ST + 512 * (tq % 2)
                            if len(st_hist) >= 2:
                                kb.wait("scalar", v_ev, st_hist[-2])
                            ag = kb.op("scalar", lambda E, wbg_=wbg_, ssl=ssl:
                                       E.activation(out=arena[:, ssl:ssl + 512],
                                                    in_=wbg_[:, 0:512],
                                                    func=AF.Silu), inc=(a_ev, 1))
                            WB.release(jg, a_ev, ag)
                            kb.wait("vector", a_ev, ag)
                            kb.wait("vector", t_mm, tvu)
                            dsta = arena[:, A6_A + 2048 * i_loc + 512 * tq:
                                         A6_A + 2048 * i_loc + 512 * (tq + 1)]
                            a_last = kb.op("vector", lambda E, wbu=wbu, ssl=ssl,
                                           dsta=dsta: E.tensor_mul(
                                               out=dsta, in0=arena[:, ssl:ssl + 512],
                                               in1=wbu[:, 0:512]), inc=(v_ev, 1))
                            WB.release(ju, v_ev, a_last)
                            st_hist.append(a_last)
                        gu_free[slg] = (t_mm, tvg)
                        gu_free[slu] = (t_mm, tvu)
                    ng = gidx + 1
                    if ng < 24:
                        ne, nge = divmod(ng, 3)
                        dv_pend[ng] = load_down(ne, nge, dwn_guard[ng % 2],
                                                ng % 2)
                    kb.wait("tensor", v_ev, a_last)
                    kb.wait("tensor", g_ld, dv)
                    for tc in range(16):
                        ps = xps6[tc % 2]
                        if xfree6[tc % 2] is not None:
                            kb.wait("tensor", v_ev, xfree6[tc % 2])
                        for i_loc in range(7):
                            lhs = arena[:, A6_A + 2048 * i_loc + 128 * tc:
                                        A6_A + 2048 * i_loc + 128 * (tc + 1)]
                            for half in range(2):
                                kb.op("tensor", lambda E, ps=ps, lhs=lhs,
                                      i_loc=i_loc, half=half, slot=slot: E.matmul(
                                          ps[:, 512 * half:512 * (half + 1)], lhs,
                                          arena[:, DWS[slot] + 1024 * i_loc + 512 * half:
                                                DWS[slot] + 1024 * i_loc + 512 * (half + 1)],
                                          start=(i_loc == 0), stop=(i_loc == 6)),
                                      inc=(t_mm, 1) if (i_loc == 6 and half == 1)
                                      else None)
                        tv = t_mm.n
                        kb.wait("vector", t_mm, tv)
                        wcol = scr[:, C_SBG + 8 * tc + e:C_SBG + 8 * tc + e + 1]
                        d0 = x_tm[:, D * tc:D * tc + 512]
                        d1 = x_tm[:, D * tc + 512:D * (tc + 1)]
                        kb.op("vector", lambda E, ps=ps, wcol=wcol, d0=d0:
                              E.scalar_tensor_tensor(
                                  out=d0, in0=ps[:, 0:512], scalar=wcol,
                                  in1=d0, op0=ALU.mult, op1=ALU.add))
                        xv = kb.op("vector", lambda E, ps=ps, wcol=wcol, d1=d1:
                                   E.scalar_tensor_tensor(
                                       out=d1, in0=ps[:, 512:1024], scalar=wcol,
                                       in1=d1, op0=ALU.mult, op1=ALU.add),
                                   inc=(v_ev, 1))
                        xfree6[tc % 2] = xv
                    WB.free_at[2] = (v_ev, xv)
                    WB.free_at[3] = (v_ev, xv)
                    dwn_guard[slot] = (t_mm, t_mm.n)
                    gidx += 1

            if DBG:
                kb.wait("sync", v_ev, v_ev.n)
                kb.wait("sync", a_ev, a_ev.n)
                kb.op("sync", lambda E: E.dma_start(
                    out=dbg_scr6[:, 0:512], in_=scr[:, 0:512]), inc=(st, 16))
                kb.op("sync", lambda E: E.dma_start(
                    out=dbg_scr6[:, 512:1024], in_=scr[:, 2304:2816]), inc=(st, 16))
                kb.op("sync", lambda E: E.dma_start(
                    out=dbg_xtm[:, 0:128], in_=scr[:, 2816:2944]), inc=(st, 16))
                kb.op("sync", lambda E: E.dma_start(
                    out=dbg_kp[:, 0:64], in_=arena[:, A6_WR:A6_WR + 64]),
                    inc=(st, 16))

            layernorm_inplace()

        if DBG:
            kb.wait("sync", v_ev, v_ev.n)
            kb.op("sync", lambda E: E.dma_start(out=dbg_xfm, in_=arena[:, A_XFM:A_XFM + 32768]), inc=(st, 16))
            kb.op("sync", lambda E: E.dma_start(out=dbg_q2, in_=arena[:, A_Q2:A_Q2 + 2048]), inc=(st, 16))
            kb.op("sync", lambda E: E.dma_start(out=dbg_k2, in_=arena[:, A_K2:A_K2 + 4096]), inc=(st, 16))
            kb.op("sync", lambda E: E.dma_start(out=dbg_v1, in_=arena[:, A_V1:A_V1 + 4160]), inc=(st, 16))
            kb.op("sync", lambda E: E.dma_start(out=dbg_z2, in_=arena[:, A_Z2:A_Z2 + 2048]), inc=(st, 16))

        kb.wait("sync", v_ev, v_ev.n)
        out_ap = out.rearrange("(c p) d -> p c d", p=128)
        kb.op("sync", lambda E: E.dma_start(out=out_ap, in_=x_tm_ap), inc=(st, 16))
        kb.wait("sync", st, st.n)

        kb.finalize(block)
    return nc



_BF = ml_dtypes.bfloat16


def kernel(**inputs):
    x = np.asarray(inputs["L0_attn_x"])
    mask_inds = np.asarray(inputs["L0_attn_mask_inds"]).astype(np.int32)
    wq = np.asarray(inputs["L0_attn_Wq"])
    wk = np.asarray(inputs["L0_attn_Wk"])
    wv = np.asarray(inputs["L0_attn_Wv"])
    wo = np.asarray(inputs["L0_attn_Wo"])

    wqT = np.ascontiguousarray(wq.T)
    wkT = np.ascontiguousarray(wk.T)
    wvT = np.ascontiguousarray(wv.T)
    woT = np.ascontiguousarray(wo.T)

    ident = np.eye(128, dtype=np.float32).astype(_BF)
    ident32 = np.eye(128, dtype=np.float32)
    e128 = (np.arange(128)[:, None] // 16 == np.arange(128)[None, :] // 16
            ).astype(np.float32)
    ones32 = np.ones((1, 128), np.float32)
    ones_row = np.ones((1, 128), np.float32).astype(_BF)
    fconst = np.zeros((128, 8), np.float32)
    fconst[:, 0] = 1e-5
    fconst[:, 1] = float(np.log(1.0 / 16.0))

    per = 100000 // 8
    padded = 128 * PAIR_COLS

    in_maps = []
    for c in range(8):
        b, h = c // 2, c % 2
        own = np.ascontiguousarray(x[b, 2048 * h:2048 * (h + 1)])
        other = np.ascontiguousarray(x[b, 2048 * (1 - h):2048 * (2 - h)])
        xb = np.concatenate([own, other], 0)
        pq = mask_inds[0, per * c:per * (c + 1)]
        pk = mask_inds[1, per * c:per * (c + 1)]
        pq = np.concatenate([pq, np.full(padded - per, pq[0], np.int32)])
        pk = np.concatenate([pk, np.full(padded - per, pk[0], np.int32)])
        mthr = np.full((128, 1), 0.0 if h == 0 else 1e9, np.float32)
        in_maps.append({
            "x_own": own,
            "x_batch": np.ascontiguousarray(xb),
            "pairs_q": np.ascontiguousarray(pq.reshape(128, PAIR_COLS)),
            "pairs_k": np.ascontiguousarray(pk.reshape(128, PAIR_COLS)),
            "mask_thr": mthr,
            "ident_c": ident,
            "ones_row_c": ones_row,
            "fconst_c": fconst,
            "wqT": wqT, "wkT": wkT, "wvT": wvT, "woT": woT,
            "ident32_c": ident32, "e128_c": e128, "ones32_c": ones32,
            "wr_c": np.ascontiguousarray(inputs["L0_ffn_Wr"]),
            "w1_c": np.ascontiguousarray(inputs["L0_ffn_W1"]),
            "b1_c": np.ascontiguousarray(
                np.asarray(inputs["L0_ffn_b1"]).reshape(8, 32, 128)
                .transpose(0, 2, 1)),
            "w2_c": np.ascontiguousarray(inputs["L0_ffn_W2"]),
            "b2_c": np.ascontiguousarray(inputs["L0_ffn_b2"]),
            "l1wq": np.ascontiguousarray(inputs["L1_attn_Wq"]),
            "l1wk": np.ascontiguousarray(inputs["L1_attn_Wk"]),
            "l1wv": np.ascontiguousarray(inputs["L1_attn_Wv"]),
            "l1wo": np.ascontiguousarray(inputs["L1_attn_Wo"]),
            "l1bq": np.ascontiguousarray(
                np.asarray(inputs["L1_attn_bq"]).reshape(8, 128).T),
            "l1bk": np.ascontiguousarray(
                np.asarray(inputs["L1_attn_bk"]).reshape(8, 128).T),
            "l1bv_bc": np.ascontiguousarray(np.broadcast_to(
                np.asarray(inputs["L1_attn_bv"]), (128, 1024))),
            "l1bo_bc": np.ascontiguousarray(np.broadcast_to(
                np.asarray(inputs["L1_attn_bo"]), (128, 1024))),
            "wr2_c": np.ascontiguousarray(np.asarray(inputs["L1_ffn_router_w"]).T),
            "aw1T": np.ascontiguousarray(np.asarray(inputs["L1_ffn_actor_w1"]).T),
            "aw2A": np.ascontiguousarray(np.vstack(
                [np.asarray(inputs["L1_ffn_actor_w2"]).T,
                 np.asarray(inputs["L1_ffn_actor_b2"]).reshape(1, 6),
                 np.zeros((77, 6), np.float32)])),
            "ab1p": np.ascontiguousarray(np.pad(
                np.asarray(inputs["L1_ffn_actor_b1"]),
                (0, 78)).reshape(128, 1)),
            "jrow_c": np.ascontiguousarray(np.broadcast_to(np.concatenate(
                [np.arange(6) + 1000.0, np.zeros(2),
                 np.arange(8), np.zeros(0)]).astype(np.float32), (128, 16))),
            "gwT": np.ascontiguousarray(
                np.asarray(inputs["L1_ffn_gate_w"]).transpose(0, 2, 1)),
            "uwT": np.ascontiguousarray(
                np.asarray(inputs["L1_ffn_up_w"]).transpose(0, 2, 1)),
            "dwT": np.ascontiguousarray(
                np.asarray(inputs["L1_ffn_down_w"]).transpose(0, 2, 1)),
            "l1proj": np.ascontiguousarray(np.vstack(
                [np.asarray(inputs["L1_attn_proj"]),
                 np.asarray(inputs["L1_attn_proj"])])),
        })

    nc = build_program()
    res = run_bass_kernel_spmd(nc, in_maps, core_ids=list(range(8)))
    outs = [r["out"] for r in res.results]
    y = np.zeros((4, 4096, 1024), np.float32)
    for c in range(8):
        b, h = c // 2, c % 2
        y[b, 2048 * h:2048 * (h + 1)] = outs[c]
    return y
